# version 07
# baseline (speedup 1.0000x reference)
import numpy as np
import ml_dtypes
from contextlib import ExitStack

import concourse.bass as bass
import concourse.mybir as mybir
from concourse.bass_utils import run_bass_kernel_spmd
from concourse.expressions import RuntimeValue

F32 = mybir.dt.float32
BF16 = mybir.dt.bfloat16
I32 = mybir.dt.int32
AF = mybir.ActivationFunctionType
ALU = mybir.AluOpType
AX = mybir.AxisListType
ds = bass.ds

B, S, D, H, DH = 2, 8192, 1024, 16, 64
TOK = 2048
NT = TOK // 128
EPS = 1e-5
NFFN = 2816
IM = 4096
VCOLS = 2048
GCOLS = 2048 + 16 * 65
MAGIC = 1.5 * (2 ** 22)
NB_LIST = [(0, 512), (512, 512), (1024, 512), (1536, 512), (2048, 512), (2560, 256)]


class Ctr:
    def __init__(self, h):
        self.h = h
        self.n = 0

    def inc(self, inst, k=1):
        inst.then_inc(self.h, k)
        self.n += k
        return self.n

    def rec(self, inst, k=1):
        self.inc(inst, k)
        return (self.h, self.n)

    def fin(self):
        return (self.h, self.n)


def W(eng, rec):
    if rec and rec[1] > 0:
        eng.wait_ge(rec[0], rec[1])


def build_nc():
    nc = bass.Bass("TRN2", target_bir_lowering=False)
    dt = nc.dram_tensor

    x_own = dt("x_own", [TOK, D], F32, kind="ExternalInput").ap()
    qkvU = dt("qkvU", [D, 3 * 1024], F32, kind="ExternalInput").ap()
    qkvV = dt("qkvV", [128, 3 * 1024], F32, kind="ExternalInput").ap()
    ln1g = dt("ln1g", [128, 8], F32, kind="ExternalInput").ap()
    ln1b = dt("ln1b", [128, 8], F32, kind="ExternalInput").ap()
    ln2g = dt("ln2g", [128, 8], F32, kind="ExternalInput").ap()
    ln2b = dt("ln2b", [128, 8], F32, kind="ExternalInput").ap()
    outU = dt("outU", [D, 768], F32, kind="ExternalInput").ap()
    outV = dt("outV", [768, D], F32, kind="ExternalInput").ap()
    outb = dt("outb", [128, 8], F32, kind="ExternalInput").ap()
    fc1U = dt("fc1U", [D, 512], F32, kind="ExternalInput").ap()
    fc1V = dt("fc1V", [512, IM], F32, kind="ExternalInput").ap()
    fc1b = dt("fc1b", [128, 32], F32, kind="ExternalInput").ap()
    fc2U = dt("fc2U", [IM, 512], F32, kind="ExternalInput").ap()
    fc2V = dt("fc2V", [512, D], F32, kind="ExternalInput").ap()
    fc2b = dt("fc2b", [128, 8], F32, kind="ExternalInput").ap()
    wgwu = dt("wgwu", [2 * NFFN, D], F32, kind="ExternalInput").ap()
    wdn = dt("wdn", [D, NFFN], F32, kind="ExternalInput").ap()
    cmask = dt("cmask", [128, 4 * 512], BF16, kind="ExternalInput").ap()
    idenb = dt("idenb", [128, 128], BF16, kind="ExternalInput").ap()
    idenf = dt("idenf", [128, 128], F32, kind="ExternalInput").ap()
    qkvb = dt("qkvb", [128, 3088], F32, kind="ExternalInput").ap()
    coffs = dt("coffs", [128, 4], I32, kind="ExternalInput").ap()

    out_t = dt("out", [TOK, D], F32, kind="ExternalOutput").ap()

    qkv_loc = dt("qkv_loc", [TOK, GCOLS], BF16, kind="Internal").ap()
    qkv_g = dt("qkv_g", [S, GCOLS], BF16, kind="Internal").ap()
    ctx_loc = dt("ctx_loc", [256, S], BF16, kind="Internal").ap()
    ctx_g = dt("ctx_g", [1024, S], BF16, kind="Internal").ap()
    xn_sp = dt("xn_sp", [128, 8 * TOK], BF16, kind="Internal").ap()
    ffn_sp = dt("ffn_sp", [128, 8 * TOK], BF16, kind="Internal").ap()

    rg = [[0, 1, 2, 3], [4, 5, 6, 7]]

    top = ExitStack()
    sbT = lambda n_, s_, d_: top.enter_context(nc.sbuf_tensor(n_, s_, d_))
    C = {k: Ctr(top.enter_context(nc.semaphore("s_" + k)))
         for k in ["ld", "st", "mm", "dv", "ac", "cc", "dK", "dCO", "dC", "dW3", "dXN", "dF",
                   "sX", "sF", "sC",
                   "dX0", "dX1", "dW0", "dW1", "dQ0", "dQ1", "dV0", "dV1", "dT0", "dT1",
                   "sA0", "sA1", "sO0", "sO1"]}
    ld, st, mm, dv, ac, cc = (C[k] for k in ["ld", "st", "mm", "dv", "ac", "cc"])
    dK, dCO, dC, dW3, dXN, dF = (C[k] for k in ["dK", "dCO", "dC", "dW3", "dXN", "dF"])
    sX, sF, sC = (C[k] for k in ["sX", "sF", "sC"])
    dX = [C["dX0"], C["dX1"]]
    dW = [C["dW0"], C["dW1"]]
    dQ = [C["dQ0"], C["dQ1"]]
    dV = [C["dV0"], C["dV1"]]
    dT = [C["dT0"], C["dT1"]]
    sA = [C["sA0"], C["sA1"]]
    sO = [C["sO0"], C["sO1"]]

    cm = sbT("cm", [128, 2048], BF16)
    idb = sbT("idb", [128, 128], BF16)
    idf = sbT("idf", [128, 128], F32)
    l1g = sbT("l1g", [128, 8], F32)
    l1b = sbT("l1b", [128, 8], F32)
    l2g = sbT("l2g", [128, 8], F32)
    l2b = sbT("l2b", [128, 8], F32)
    ob_pp = sbT("ob_pp", [128, 8], F32)
    f1b_pp = sbT("f1b_pp", [128, 32], F32)
    f2b_pp = sbT("f2b_pp", [128, 8], F32)
    qb_bf = sbT("qb_bf", [128, 3088], BF16)
    ones1 = sbT("ones1", [128, 1], BF16)
    of32 = sbT("of32", [1, 128], F32)
    co_sb = sbT("co_sb", [128, 4], I32)
    qs_box = {}

    class _QS:
        def __getitem__(self, k):
            return qs_box[k]
    

    def quant1(v, src, dst, n):
        qs_a = qs_box["a"]; qs_r = qs_box["r"]; qs_d = qs_box["d"]
        qs_y = qs_box["y"]; qs_m = qs_box["m"]; qs_t = qs_box["t"]
        nb = n // 16
        dv.inc(v.tensor_reduce(out=qs_a[:, 0:nb],
                               in_=src.rearrange("p (nb k) -> p nb k", k=16),
                               axis=AX.X, op=ALU.max, apply_absolute_value=True))
        v.wait_ge(dv.h, dv.n)
        v.tensor_scalar(out=qs_a[:, 0:nb], in0=qs_a[:, 0:nb], scalar1=6e-12, scalar2=None, op0=ALU.max)
        dv.inc(v.reciprocal(out=qs_r[:, 0:nb], in_=qs_a[:, 0:nb]))
        v.wait_ge(dv.h, dv.n)
        v.tensor_scalar(out=qs_r[:, 0:nb], in0=qs_r[:, 0:nb], scalar1=12.0, scalar2=None, op0=ALU.mult)
        v.tensor_scalar(out=qs_d[:, 0:nb], in0=qs_a[:, 0:nb], scalar1=1.0 / 12.0, scalar2=None, op0=ALU.mult)
        r12b = qs_r[:, 0:nb].unsqueeze(2).broadcast_to([128, nb, 16])
        dqb = qs_d[:, 0:nb].unsqueeze(2).broadcast_to([128, nb, 16])
        v.tensor_tensor(out=qs_y[:, 0:n].rearrange("p (nb k) -> p nb k", k=16),
                        in0=src.rearrange("p (nb k) -> p nb k", k=16), in1=r12b, op=ALU.mult)
        v.tensor_scalar(out=qs_m[:, 0:n].bitcast(I32), in0=qs_y[:, 0:n].bitcast(I32),
                        scalar1=0x7F800000, scalar2=None, op0=ALU.bitwise_and)
        v.tensor_scalar(out=qs_m[:, 0:n], in0=qs_m[:, 0:n], scalar1=2.0, scalar2=MAGIC,
                        op0=ALU.max, op1=ALU.mult)
        v.tensor_tensor(out=qs_t[:, 0:n], in0=qs_y[:, 0:n], in1=qs_m[:, 0:n], op=ALU.add)
        v.tensor_tensor(out=qs_t[:, 0:n], in0=qs_t[:, 0:n], in1=qs_m[:, 0:n], op=ALU.subtract)
        return dv.inc(v.tensor_tensor(out=dst.rearrange("p (nb k) -> p nb k", k=16),
                                      in0=qs_t[:, 0:n].rearrange("p (nb k) -> p nb k", k=16),
                                      in1=dqb, op=ALU.mult))

    def quant(v, src, dst, n):
        last = 0
        for o in range(0, n, 1024):
            w = min(1024, n - o)
            last = quant1(v, src[:, o:o + w], dst[:, o:o + w], w)
        return last

    def rsqrt_row(sc, dst, src):
        sc.activation(out=dst, in_=src, func=AF.Ln)
        return ac.inc(sc.activation(out=dst, in_=dst, func=AF.Exp, scale=-0.5))

    with nc.Block() as block:
        COFF = [None, None, None]

        def p0_sync(sync):
            dCO.inc(sync.dma_start(out=co_sb[:], in_=coffs), 16)
            sync.wait_ge(dCO.h, 16)
            r0 = sync.alloc_register(name="coff0")
            sync.load(r0, co_sb[0:1, 0:1])
            COFF[0] = RuntimeValue(r0, min_val=0, max_val=768)
            for d_, s_ in [(cm, cmask), (idb, idenb), (idf, idenf), (l1g, ln1g),
                           (l1b, ln1b), (l2g, ln2g), (l2b, ln2b), (ob_pp, outb),
                           (f1b_pp, fc1b), (f2b_pp, fc2b)]:
                dK.inc(sync.dma_start(out=d_[:], in_=s_), 16)
        block.sync(p0_sync)
        P0_LD = dK.fin()

        def p0_vec(v):
            v.memset(ones1[:], 1.0)
            dv.inc(v.memset(of32[:], 1.0))
        block.vector(p0_vec)


        p1x = ExitStack()
        xqfm = p1x.enter_context(nc.sbuf_tensor("xqfm", [128, 8 * TOK], BF16))
        for _k, _sh in [("y", [128, 1024]), ("m", [128, 1024]), ("t", [128, 1024]),
                        ("a", [128, 64]), ("r", [128, 64]), ("d", [128, 64])]:
            qs_box[_k] = p1x.enter_context(nc.sbuf_tensor("qs_" + _k, _sh, F32))
        qs_y = qs_box["y"]

        qbcast = {}
        for qbi in range(4):
            w = 772
            def qb_sync(sync, qbi=qbi, w=w):
                if qbi >= 1:
                    sync.wait_ge(dv.h, qbcast[qbi - 1])
                qbcast[(qbi, "ld")] = dC.rec(sync.dma_start(out=qs_y[:, 0:w].bitcast(F32),
                                      in_=qkvb[:, qbi * w:(qbi + 1) * w]), 16)
            block.sync(qb_sync)

            def qb_vec(v, qbi=qbi, w=w):
                W(v, qbcast[(qbi, "ld")])
                qbcast[qbi] = dv.inc(v.tensor_copy(out=qb_bf[:, qbi * w:(qbi + 1) * w],
                                                   in_=qs_y[:, 0:w].bitcast(F32)))
            block.vector(qb_vec)
        QB_DV = dv.n


        p1 = ExitStack()
        sb1 = lambda n_, s_, d_: p1.enter_context(nc.sbuf_tensor(n_, s_, d_))
        ps1 = lambda n_, s_, d_: p1.enter_context(nc.psum_tensor(n_, s_, d_))
        wU = sb1("wU", [128, 8 * 3072], BF16)
        wVa = sb1("wVa", [128, 3072], BF16)
        wld = sb1("wld", [128, 3072], F32)
        xt = [sb1(f"xt{i}", [128, D], F32) for i in range(2)]
        xnb = [sb1(f"xnb{i}", [128, D], F32) for i in range(2)]
        xnbf = [sb1(f"xnbf{i}", [128, D], BF16) for i in range(2)]
        l1c = [sb1(f"l1c{i}", [128, D], BF16) for i in range(2)]
        xqt = [sb1(f"xqt{i}", [128, D], BF16) for i in range(2)]
        ring = [sb1(f"ring{i}", [128, 8 * 512], BF16) for i in range(2)]
        xst = [sb1(f"xst{i}", [128, D], BF16) for i in range(2)]
        psb = [sb1(f"psb{i}", [128, 8 * 512], BF16) for i in range(3)]
        asm = [sb1(f"asm{i}", [128, 1040], BF16) for i in range(2)]
        rw = {k: [sb1(f"rw_{k}{i}", [128, 1], F32) for i in range(2)]
              for k in ["s1", "s2", "mu", "var", "u1", "u2", "r1", "a2", "nm1", "nm2"]}
        ptb = [ps1(f"ptb{i}", [128, 1024], BF16) for i in range(2)]
        pst1 = [ps1(f"pst1{i}", [128, 512], F32) for i in range(2)]
        pst2 = [ps1(f"pst2{i}", [128, 512], F32) for i in range(2)]

        wu_ld = [0] * 9
        wu_dv = [0] * 9
        for i in range(8):
            def wl_sync(sync, i=i):
                if i >= 1:
                    sync.wait_ge(dv.h, wu_dv[i - 1])
                else:
                    sync.wait_ge(dv.h, QB_DV)
                wu_ld[i] = dC.rec(sync.dma_start(out=wld[:], in_=qkvU[i * 128:(i + 1) * 128, :]), 16)
            block.sync(wl_sync)

            def wl_vec(v, i=i):
                W(v, wu_ld[i])
                wu_dv[i] = dv.inc(v.tensor_copy(out=wU[:, i * 3072:(i + 1) * 3072], in_=wld[:]))
            block.vector(wl_vec)

        def wv_sync(sync):
            sync.wait_ge(dv.h, wu_dv[7])
            wu_ld[8] = dC.rec(sync.dma_start(out=wld[:, :], in_=qkvV), 16)
        block.sync(wv_sync)

        def wv_vec(v):
            W(v, wu_ld[8])
            dv.inc(v.tensor_copy(out=wVa[0:48, :], in_=wld[0:48, :]))
            dv.inc(v.tensor_copy(out=wVa[64:112, :], in_=wld[0:48, :]))
        block.vector(wv_vec)
        WV_DV = dv.n

        x_ld = [0] * NT
        stat_dv = [0] * NT
        nm_dv = [0] * NT
        rs_ac = [0] * NT
        xn_ac = [0] * NT
        q_dv = [0] * NT
        tr1_mm = [0] * NT
        tr2_mm = [0] * NT
        ev1_ac = [0] * NT
        ev1_dv = [0] * NT
        ev2_dv = [0] * NT
        s1_done_mm = [0] * 4
        s2_done_mm = [0] * 4
        pst1_ev = {}
        pst2_ev = {}
        asm_st = {}
        xsp_st = {}
        qkv_st = [0] * 4

        for i in range(NT):
            p = i % 2
            s = i // 4

            def t_sync(sync, i=i, p=p):
                if i >= 2:
                    sync.wait_ge(ac.h, xn_ac[i - 2])
                x_ld[i] = dX[p].rec(sync.dma_start(out=xt[p][:], in_=x_own[i * 128:(i + 1) * 128, :]), 16)
            block.sync(t_sync)

            def t_vec1(v, i=i, p=p):
                W(v, x_ld[i])
                if i == 0:
                    W(v, P0_LD)
                dv.inc(v.reduce_sum(out=rw["s1"][p][:], in_=xt[p][:], axis=AX.X))
                dv.inc(v.scalar_tensor_tensor(out=qs_y[:, 0:D], in0=xt[p][:], scalar=1.0,
                                              in1=xt[p][:], op0=ALU.mult, op1=ALU.mult,
                                              accum_out=rw["s2"][p][:]))
                v.wait_ge(dv.h, dv.n)
                v.tensor_scalar(out=rw["mu"][p][:], in0=rw["s1"][p][:], scalar1=1.0 / D, scalar2=None, op0=ALU.mult)
                dv.inc(v.tensor_scalar(out=rw["var"][p][:], in0=rw["s2"][p][:], scalar1=1.0 / D, scalar2=None, op0=ALU.mult))
                v.wait_ge(dv.h, dv.n)
                dv.inc(v.tensor_tensor(out=rw["u1"][p][:], in0=rw["mu"][p][:], in1=rw["mu"][p][:], op=ALU.mult))
                v.wait_ge(dv.h, dv.n)
                dv.inc(v.tensor_tensor(out=rw["var"][p][:], in0=rw["var"][p][:], in1=rw["u1"][p][:], op=ALU.subtract))
                v.wait_ge(dv.h, dv.n)
                v.tensor_scalar(out=rw["u1"][p][:], in0=rw["var"][p][:], scalar1=EPS, scalar2=None, op0=ALU.add)
                stat_dv[i] = dv.inc(v.tensor_scalar(out=rw["u2"][p][:], in0=rw["var"][p][:],
                                                    scalar1=1.0 + EPS, scalar2=EPS * EPS,
                                                    op0=ALU.mult, op1=ALU.add))
            block.vector(t_vec1)

            def t_sc1(sc, i=i, p=p):
                sc.wait_ge(dv.h, stat_dv[i])
                rsqrt_row(sc, rw["r1"][p][:], rw["u1"][p][:])
                rs_ac[i] = rsqrt_row(sc, rw["a2"][p][:], rw["u2"][p][:])
            block.scalar(t_sc1)

            def t_vec2(v, i=i, p=p):
                v.wait_ge(ac.h, rs_ac[i])
                v.scalar_tensor_tensor(out=rw["nm1"][p][:], in0=rw["mu"][p][:], scalar=-1.0,
                                       in1=rw["r1"][p][:], op0=ALU.mult, op1=ALU.mult)
                nm_dv[i] = dv.inc(v.scalar_tensor_tensor(out=rw["nm2"][p][:], in0=rw["mu"][p][:],
                                                         scalar=-1.0, in1=rw["a2"][p][:],
                                                         op0=ALU.mult, op1=ALU.mult))
            block.vector(t_vec2)

            def t_sc2(sc, i=i, p=p):
                sc.wait_ge(dv.h, nm_dv[i])
                if i >= 2:
                    sc.wait_ge(dv.h, q_dv[i - 2])
                    sc.wait_ge(mm.h, tr1_mm[i - 2])
                sc.activation(out=xnb[p][:], in_=xt[p][:], func=AF.Identity,
                              scale=rw["r1"][p][:], bias=rw["nm1"][p][:])
                sc.activation(out=xnbf[p][:], in_=xt[p][:], func=AF.Identity,
                              scale=rw["r1"][p][:], bias=rw["nm1"][p][:])
                xn_ac[i] = ac.inc(sc.activation(out=l1c[p][:], in_=xt[p][:], func=AF.Identity,
                                                scale=rw["a2"][p][:], bias=rw["nm2"][p][:]))
            block.scalar(t_sc2)

            def t_vec3(v, i=i, p=p):
                v.wait_ge(ac.h, xn_ac[i])
                if i >= 2:
                    v.wait_ge(mm.h, tr2_mm[i - 2])
                q_dv[i] = quant(v, xnb[p][:], xqt[p][:], D)
            block.vector(t_vec3)

            def t_mm1(t, i=i, p=p):
                t.wait_ge(ac.h, xn_ac[i])
                if i >= 1:
                    t.wait_ge(dv.h, ev2_dv[i - 1])
                    t.wait_ge(dv.h, ev1_dv[i - 1])
                for c in range(8):
                    t.transpose(ptb[0][:, c * 128:(c + 1) * 128],
                                in_=l1c[p][:, c * 128:(c + 1) * 128], identity=idb[:])
                for c in range(8):
                    ins = t.transpose(ptb[1][:, c * 128:(c + 1) * 128],
                                      in_=xnbf[p][:, c * 128:(c + 1) * 128], identity=idb[:])
                tr1_mm[i] = mm.inc(ins)
            block.tensor(t_mm1)

            def t_sc3(sc, i=i, s=s):
                sc.wait_ge(mm.h, tr1_mm[i])
                if s >= 2 and i % 4 == 0:
                    sc.wait_ge(mm.h, s1_done_mm[s - 2])
                slot = i % 4
                for c in range(8):
                    ins = sc.activation(
                        out=ring[s % 2][:, c * 512 + slot * 128: c * 512 + (slot + 1) * 128],
                        in_=ptb[0][:, c * 128:(c + 1) * 128], func=AF.Identity,
                        scale=l1g[:, c:c + 1], bias=l1b[:, c:c + 1])
                ev1_ac[i] = ac.inc(ins)
            block.scalar(t_sc3)

            def t_vec4(v, i=i, p=p):
                v.wait_ge(mm.h, tr1_mm[i])
                if i >= 2:
                    W(v, xsp_st[i - 2])
                ev1_dv[i] = dv.inc(v.tensor_copy(out=xst[p][:], in_=ptb[1][:]))
            block.vector(t_vec4)

            def t_mm2(t, i=i, p=p):
                t.wait_ge(dv.h, q_dv[i])
                t.wait_ge(ac.h, ev1_ac[i])
                for c in range(8):
                    ins = t.transpose(ptb[0][:, c * 128:(c + 1) * 128],
                                      in_=xqt[p][:, c * 128:(c + 1) * 128], identity=idb[:])
                tr2_mm[i] = mm.inc(ins)
            block.tensor(t_mm2)

            def t_vec5(v, i=i):
                v.wait_ge(mm.h, tr2_mm[i])
                ev2_dv[i] = dv.inc(v.tensor_copy(
                    out=xqfm[:].rearrange("p (c t) -> p c t", t=TOK)[:, :, i * 128:(i + 1) * 128],
                    in_=ptb[0][:].rearrange("p (c t) -> p c t", t=128)))
            block.vector(t_vec5)

            def t_sync2(sync, i=i, p=p):
                sync.wait_ge(dv.h, ev1_dv[i])
                xsp_st[i] = sX.rec(sync.dma_start(
                    out=xn_sp.rearrange("p (c t) -> p c t", t=TOK)[:, :, i * 128:(i + 1) * 128],
                    in_=xst[p][:].rearrange("p (c t) -> p c t", t=128)), 16)
            block.sync(t_sync2)

            if i % 4 == 3:
                s1mm = {}

                def sl_pre(t, s=s, i=i):
                    t.wait_ge(ac.h, ev1_ac[i])
                    if s == 0:
                        t.wait_ge(dv.h, WV_DV)
                        t.wait_ge(dv.h, QB_DV)
                    if s >= 1:
                        t.wait_ge(mm.h, s2_done_mm[s - 1])
                block.tensor(sl_pre)

                for prj in range(3):
                    for ccx in range(8):
                        def sl_mm(t, s=s, prj=prj, ccx=ccx):
                            gk = s * 24 + prj * 8 + ccx
                            if gk >= 2:
                                t.wait_ge(dv.h, pst1_ev.get(gk - 2, 0))
                            for dch in range(8):
                                base = dch * 3072 + prj * 1024 + ccx * 128
                                ins = t.matmul(pst1[gk % 2][:],
                                               lhsT=wU[:, base:base + 128],
                                               rhs=ring[s % 2][:, dch * 512:(dch + 1) * 512],
                                               start=(dch == 0), stop=(dch == 7))
                            s1mm[(prj, ccx)] = mm.inc(ins)
                            if prj == 2 and ccx == 7:
                                s1_done_mm[s] = mm.n
                        block.tensor(sl_mm)

                        def sl_ev(v, s=s, prj=prj, ccx=ccx):
                            gk = s * 24 + prj * 8 + ccx
                            v.wait_ge(mm.h, s1mm[(prj, ccx)])
                            v.tensor_copy(out=psb[prj][0:48, ccx * 512:(ccx + 1) * 512],
                                          in_=pst1[gk % 2][0:48, :])
                            pst1_ev[gk] = dv.inc(
                                v.tensor_copy(out=psb[prj][64:112, ccx * 512:(ccx + 1) * 512],
                                              in_=pst1[gk % 2][64:112, :]))
                        block.vector(sl_ev)

                S1EV = dv.n
                s2mm = {}

                def s2_pre(t, s=s):
                    t.wait_ge(dv.h, S1EV)
                block.tensor(s2_pre)

                for tt in range(4):
                    a = tt % 2
                    for prj in range(3):
                        def s2_mm(t, s=s, tt=tt, prj=prj):
                            gk = s * 12 + tt * 3 + prj
                            if gk >= 1:
                                t.wait_ge(dv.h, pst2_ev.get(gk - 1, 0))
                            for h in [0, 2, 4, 6, 8, 10, 12, 14, -1, 1, 3, 5, 7, 9, 11, 13, 15]:
                                if h < 0:
                                    t.drain()
                                    continue
                                base = (h // 2) * 512 + tt * 128
                                ins = t.matmul(
                                    pst2[h // 8][:, (h % 8) * 64:(h % 8 + 1) * 64],
                                    lhsT=psb[prj][(h % 2) * 64:(h % 2) * 64 + 48, base:base + 128],
                                    rhs=wVa[(h % 2) * 64:(h % 2) * 64 + 48,
                                            prj * 1024 + h * 64:prj * 1024 + (h + 1) * 64],
                                    start=True, stop=True)
                            t.drain()
                            s2mm[(tt, prj)] = mm.inc(ins)
                            if tt == 3 and prj == 2:
                                s2_done_mm[s] = mm.n
                        block.tensor(s2_mm)

                        def s2_ev(v, s=s, tt=tt, prj=prj, a=a):
                            gk = s * 12 + tt * 3 + prj
                            v.wait_ge(mm.h, s2mm[(tt, prj)])
                            gabs = s * 4 + tt
                            prev_user = (gabs, prj - 1) if prj >= 1 else (gabs - 2, 2)
                            if prev_user in asm_st:
                                W(v, asm_st[prev_user])
                            if prj < 2:
                                v.tensor_tensor(out=asm[a][:, 0:512], in0=pst2[0][:],
                                                in1=qb_bf[:, prj * 1024:prj * 1024 + 512], op=ALU.add)
                                ins = v.tensor_tensor(out=asm[a][:, 512:1024], in0=pst2[1][:],
                                                      in1=qb_bf[:, prj * 1024 + 512:prj * 1024 + 1024], op=ALU.add)
                            else:
                                v.tensor_copy(out=asm[a][:, 0:1040], in_=qb_bf[:, 2048:3088])
                                av = asm[a][:].rearrange("p (h c) -> p h c", c=65)
                                v.tensor_tensor(out=av[:, 0:8, 0:64], in0=pst2[0][:].rearrange("p (h d) -> p h d", d=64),
                                                in1=av[:, 0:8, 0:64], op=ALU.add)
                                ins = v.tensor_tensor(out=av[:, 8:16, 0:64], in0=pst2[1][:].rearrange("p (h d) -> p h d", d=64),
                                                      in1=av[:, 8:16, 0:64], op=ALU.add)
                            pst2_ev[gk] = dv.inc(ins)
                        block.vector(s2_ev)

                        def s2_st1(sync, s=s, tt=tt, prj=prj, a=a):
                            gk = s * 12 + tt * 3 + prj
                            sync.wait_ge(dv.h, pst2_ev[gk])
                            row0 = s * 512 + tt * 128
                            width = 1024 if prj < 2 else 1040
                            asm_st[(s * 4 + tt, prj)] = sA[a].rec(sync.dma_start(
                                out=qkv_loc[row0:row0 + 128, prj * 1024:prj * 1024 + width],
                                in_=asm[a][:, 0:width]), 16)
                        block.sync(s2_st1)

        COFFG = [None, None, None]

        def coll1(g):
            W(g, sA[0].fin())
            W(g, sA[1].fin())
            g.wait_ge(dCO.h, 16)
            r1 = g.alloc_register(name="coffg1")
            g.load(r1, co_sb[0:1, 1:2])
            COFFG[1] = RuntimeValue(r1, min_val=0, max_val=780)
            r2 = g.alloc_register(name="coffg2")
            g.load(r2, co_sb[0:1, 2:3])
            COFFG[2] = RuntimeValue(r2, min_val=0, max_val=6144)
            cc.inc(g.collective_compute("AllGather", ALU.bypass, replica_groups=rg,
                                        ins=[qkv_loc.opt()], outs=[qkv_g.opt()]))
        block.gpsimd(coll1)
        CC1 = cc.n

        p1.close()

        p15 = ExitStack()
        sb5 = lambda n_, s_, d_: p15.enter_context(nc.sbuf_tensor(n_, s_, d_))
        ps5 = lambda n_, s_, d_: p15.enter_context(nc.psum_tensor(n_, s_, d_))
        wfl = [sb5(f"wfl{i}", [128, NFFN], F32) for i in range(2)]
        wqs = sb5("wqs", [128, D], BF16)
        wds = sb5("wds", [128, 512], BF16)
        wqT = [sb5(f"wqT{i}", [128, 8 * 512], BF16) for i in range(2)]
        wdT = sb5("wdT", [128, 4 * 1024], BF16)
        hqn = sb5("hqn", [128, 4 * TOK], BF16)
        sil = sb5("sil", [128, 512], F32)
        hmt = sb5("hmt", [128, 512], F32)
        hqt = sb5("hqt", [128, 512], BF16)
        facc = sb5("facc", [128, 8 * TOK], F32)
        ptq = ps5("ptq", [128, 1024], BF16)
        pg = [ps5(f"pg{i}", [128, 512], F32) for i in range(2)]
        pu = [ps5(f"pu{i}", [128, 512], F32) for i in range(2)]
        pdn = [ps5(f"pdn{i}", [128, 512], F32) for i in range(2)]

        wjobs = []
        for nbi, (noff, nsz) in enumerate(NB_LIST):
            nch = nsz // 128
            for r_ in range(nch):
                wjobs.append(("g", nbi, r_, noff, nsz))
            for r_ in range(nch):
                wjobs.append(("u", nbi, r_, noff, nsz))
            for r_ in range(8):
                wjobs.append(("d", nbi, r_, noff, nsz))
        wj_ld = {}
        wj_q = {}
        wj_tr = {}
        wj_ev = {}
        ptq_free = [0]
        gu_done_mm = {}
        dn_done_mm = {}
        gu_mm = {}
        sil_ac = {}
        mul_dv = {}
        hq_tr = {}
        hq_ev = {}
        dn_mm = {}
        dn_ev = {}

        def emit_wjob(k):
            kind, nbi, r_, noff, nsz = wjobs[k]
            nch = nsz // 128

            def wj_sync(sync, k=k, kind=kind, nbi=nbi, r_=r_, noff=noff, nsz=nsz):
                if k < 2:
                    sync.wait_ge(mm.h, s2_done_mm[3])
                    W(sync, sA[0].fin())
                    W(sync, sA[1].fin())
                    W(sync, sX.fin())
                if k >= 2:
                    sync.wait_ge(dv.h, wj_q[k - 2])
                if kind == "g":
                    src = wgwu[noff + r_ * 128:noff + (r_ + 1) * 128, :]
                    w = D
                elif kind == "u":
                    src = wgwu[NFFN + noff + r_ * 128:NFFN + noff + (r_ + 1) * 128, :]
                    w = D
                else:
                    src = wdn[r_ * 128:(r_ + 1) * 128, noff:noff + nsz]
                    w = nsz
                wj_ld[k] = dW[k % 2].rec(sync.dma_start(out=wfl[k % 2][:, 0:w], in_=src), 16)
            block.sync(wj_sync)

            def wj_v(v, k=k, kind=kind, nsz=nsz):
                W(v, wj_ld[k])
                if k >= 1:
                    v.wait_ge(mm.h, wj_tr[k - 1])
                if kind == "d":
                    wj_q[k] = quant(v, wfl[k % 2][:, 0:nsz], wds[:, 0:nsz], nsz)
                else:
                    wj_q[k] = quant(v, wfl[k % 2][:, 0:D], wqs[:], D)
            block.vector(wj_v)

            def wj_t(t, k=k, kind=kind, nsz=nsz):
                t.wait_ge(dv.h, wj_q[k])
                t.wait_ge(dv.h, ptq_free[0])
                nblk = (nsz // 128) if kind == "d" else 8
                srcb = wds if kind == "d" else wqs
                for b_ in range(nblk):
                    ins = t.transpose(ptq[:, b_ * 128:(b_ + 1) * 128],
                                      in_=srcb[:, b_ * 128:(b_ + 1) * 128], identity=idb[:])
                wj_tr[k] = mm.inc(ins)
            block.tensor(wj_t)

            def wj_e(v, k=k, kind=kind, nbi=nbi, r_=r_, nsz=nsz):
                v.wait_ge(mm.h, wj_tr[k])
                if nbi >= 1:
                    if kind in ("g", "u"):
                        v.wait_ge(mm.h, gu_done_mm[nbi - 1])
                    else:
                        v.wait_ge(mm.h, dn_done_mm[nbi - 1])
                nblk = (nsz // 128) if kind == "d" else 8
                for b_ in range(nblk):
                    if kind == "d":
                        dst = wdT[:, b_ * 1024 + r_ * 128:b_ * 1024 + (r_ + 1) * 128]
                    else:
                        wi = 0 if kind == "g" else 1
                        dst = wqT[wi][:, b_ * nsz + r_ * 128:b_ * nsz + (r_ + 1) * 128]
                    ins = v.tensor_copy(out=dst, in_=ptq[:, b_ * 128:(b_ + 1) * 128])
                wj_ev[k] = dv.inc(ins)
                ptq_free[0] = wj_ev[k]
            block.vector(wj_e)

        kbase = 0
        for nbi, (noff, nsz) in enumerate(NB_LIST):
            nch = nsz // 128
            njobs = 2 * nch + 8
            for k in range(kbase, kbase + njobs):
                emit_wjob(k)
            WJ_EV_NB = dv.n

            for tt in range(NT):
                def g_mm(t, nbi=nbi, nsz=nsz, tt=tt):
                    if tt == 0:
                        t.wait_ge(dv.h, WJ_EV_NB)
                        if nbi == 0:
                            t.wait_ge(dv.h, ev2_dv[NT - 1])
                    pp = tt % 2
                    if tt >= 2:
                        t.wait_ge(ac.h, sil_ac[(nbi, tt - 2)])
                        t.wait_ge(dv.h, mul_dv[(nbi, tt - 2)])
                    for kc in range(8):
                        t.matmul(pg[pp][:, 0:nsz],
                                 lhsT=xqfm[:].rearrange("p (c t) -> p c t", t=TOK)[:, kc, tt * 128:(tt + 1) * 128],
                                 rhs=wqT[0][:, kc * nsz:(kc + 1) * nsz],
                                 start=(kc == 0), stop=(kc == 7))
                    for kc in range(8):
                        ins = t.matmul(pu[pp][:, 0:nsz],
                                       lhsT=xqfm[:].rearrange("p (c t) -> p c t", t=TOK)[:, kc, tt * 128:(tt + 1) * 128],
                                       rhs=wqT[1][:, kc * nsz:(kc + 1) * nsz],
                                       start=(kc == 0), stop=(kc == 7))
                    gu_mm[(nbi, tt)] = mm.inc(ins)
                    if tt == NT - 1:
                        gu_done_mm[nbi] = mm.n
                block.tensor(g_mm)

                def g_sc(sc, nbi=nbi, tt=tt, nsz=nsz):
                    sc.wait_ge(mm.h, gu_mm[(nbi, tt)])
                    if tt >= 1:
                        sc.wait_ge(dv.h, mul_dv[(nbi, tt - 1)])
                    sil_ac[(nbi, tt)] = ac.inc(
                        sc.activation(out=sil[:, 0:nsz], in_=pg[tt % 2][:, 0:nsz], func=AF.Silu))
                block.scalar(g_sc)

                def g_v(v, nbi=nbi, tt=tt, nsz=nsz):
                    v.wait_ge(ac.h, sil_ac[(nbi, tt)])
                    if tt >= 1:
                        v.wait_ge(mm.h, hq_tr[(nbi, tt - 1)])
                    dv.inc(v.tensor_tensor(out=hmt[:, 0:nsz], in0=sil[:, 0:nsz],
                                           in1=pu[tt % 2][:, 0:nsz], op=ALU.mult))
                    v.wait_ge(dv.h, dv.n)
                    mul_dv[(nbi, tt)] = quant(v, hmt[:, 0:nsz], hqt[:, 0:nsz], nsz)
                block.vector(g_v)

                def h_t(t, nbi=nbi, tt=tt, nsz=nsz):
                    t.wait_ge(dv.h, mul_dv[(nbi, tt)])
                    t.wait_ge(dv.h, ptq_free[0])
                    for c in range(nsz // 128):
                        ins = t.transpose(ptq[:, c * 128:(c + 1) * 128],
                                          in_=hqt[:, c * 128:(c + 1) * 128], identity=idb[:])
                    hq_tr[(nbi, tt)] = mm.inc(ins)
                block.tensor(h_t)

                def h_e(v, nbi=nbi, tt=tt, nsz=nsz):
                    v.wait_ge(mm.h, hq_tr[(nbi, tt)])
                    if nbi >= 1 and tt == 0:
                        v.wait_ge(mm.h, dn_done_mm[nbi - 1])
                    ins = v.tensor_copy(
                        out=hqn[:, 0:(nsz // 128) * TOK].rearrange("p (c t) -> p c t", t=TOK)[:, :, tt * 128:(tt + 1) * 128],
                        in_=ptq[:, 0:nsz].rearrange("p (c t) -> p c t", t=128))
                    hq_ev[(nbi, tt)] = dv.inc(ins)
                    ptq_free[0] = hq_ev[(nbi, tt)]
                block.vector(h_e)

            def d_pre(t, nbi=nbi):
                t.wait_ge(dv.h, hq_ev[(nbi, NT - 1)])
            block.tensor(d_pre)

            for tc in range(4):
                for dc in range(8):
                    def d_mm(t, nbi=nbi, nsz=nsz, nch=nch, tc=tc, dc=dc):
                        gk = nbi * 32 + tc * 8 + dc
                        if gk >= 2:
                            t.wait_ge(dv.h, dn_ev.get(gk - 2, 0))
                        for c in range(nch):
                            ins = t.matmul(
                                pdn[gk % 2][:],
                                lhsT=wdT[:, c * 1024 + dc * 128:c * 1024 + (dc + 1) * 128],
                                rhs=hqn[:, 0:nch * TOK].rearrange("p (cc t) -> p cc t", t=TOK)[:, c, tc * 512:(tc + 1) * 512],
                                start=(c == 0), stop=(c == nch - 1))
                        dn_mm[gk] = mm.inc(ins)
                        if tc == 3 and dc == 7:
                            dn_done_mm[nbi] = mm.n
                    block.tensor(d_mm)

                    def d_ev(v, nbi=nbi, tc=tc, dc=dc):
                        gk = nbi * 32 + tc * 8 + dc
                        v.wait_ge(mm.h, dn_mm[gk])
                        dst = facc[:].rearrange("p (c t) -> p c t", t=TOK)[:, dc, tc * 512:(tc + 1) * 512]
                        if nbi == 0:
                            ins = v.tensor_copy(out=dst, in_=pdn[gk % 2][:])
                        else:
                            ins = v.tensor_tensor(out=dst, in0=dst, in1=pdn[gk % 2][:], op=ALU.add)
                        dn_ev[gk] = dv.inc(ins)
                    block.vector(d_ev)
            kbase += njobs

        FFN_CAST = [0]

        def f_cast2(v):
            v.wait_ge(mm.h, dn_done_mm[len(NB_LIST) - 1])
            for c in range(4):
                ins = v.tensor_copy(out=hqn[:, c * TOK:(c + 1) * TOK],
                                    in_=facc[:, c * TOK:(c + 1) * TOK])
            FFN_CAST[0] = dv.inc(ins)
        block.vector(f_cast2)

        def f_st1(sync):
            sync.wait_ge(dv.h, FFN_CAST[0])
            sF.inc(sync.dma_start(out=ffn_sp[:, 0:4 * TOK], in_=hqn[:, 0:4 * TOK]), 16)
        block.sync(f_st1)
        FFN_ST1 = sF.fin()

        def f_cast3(v):
            W(v, FFN_ST1)
            for c in range(4):
                ins = v.tensor_copy(out=hqn[:, c * TOK:(c + 1) * TOK],
                                    in_=facc[:, (4 + c) * TOK:(5 + c) * TOK])
            FFN_CAST[0] = dv.inc(ins)
        block.vector(f_cast3)

        def f_st2(sync):
            sync.wait_ge(dv.h, FFN_CAST[0])
            sF.inc(sync.dma_start(out=ffn_sp[:, 4 * TOK:8 * TOK], in_=hqn[:, 0:4 * TOK]), 16)
        block.sync(f_st2)
        FFN_ST = sF.fin()

        P15_DV_END = dv.n
        P15_MM_END = mm.n

        p15.close()
        p1x.close()

        p2 = ExitStack()
        sb2 = lambda n_, s_, d_: p2.enter_context(nc.sbuf_tensor(n_, s_, d_))
        ps2 = lambda n_, s_, d_: p2.enter_context(nc.psum_tensor(n_, s_, d_))
        Qfm = [[sb2(f"Qfm{u}_{l}", [64, S], BF16) for l in range(2)] for u in range(2)]
        Kfm = [[sb2(f"Kfm{u}_{l}", [64, S], BF16) for l in range(2)] for u in range(2)]
        VT = [sb2(f"VT{u}", [128, 64 * 130], BF16) for u in range(2)]
        Et = [sb2(f"Et{i}", [128, 1024], BF16) for i in range(2)]
        rrow = sb2("rrow", [1, 1024], F32)
        rb = sb2("rb", [64, 1024], F32)
        cta = sb2("cta", [128, 512], BF16)

        unit_ld = [0, 0]

        p2p = ExitStack()
        qkst = [p2p.enter_context(nc.sbuf_tensor("qkst0", [128, 4096], BF16))]
        ptk = [p2p.enter_context(nc.psum_tensor(f"ptk{i}", [128, 1024], BF16)) for i in range(2)]
        qk_ld = {}
        qk_tr = {}
        qk_ev = {}
        gjob = 0
        for u in range(2):
            for ten in range(2):
                for hf in range(2):
                    def qk_sync(sync, u=u, ten=ten, hf=hf, gjob=gjob):
                        cbase = ten * 1024 + u * 128
                        if gjob >= 1:
                            sync.wait_ge(dv.h, qk_ev.get((gjob - 1, 3), 0))
                        if gjob == 0:
                            sync.wait_ge(cc.h, CC1)
                            sync.wait_ge(dv.h, P15_DV_END)
                            sync.wait_ge(mm.h, P15_MM_END)
                            W(sync, sF.fin())
                        qk_ld[gjob] = dQ[gjob % 2].rec(sync.dma_start(
                            out=qkst[0][:].rearrange("p (a c) -> p a c", c=128),
                            in_=qkv_g.rearrange("(a p) c -> p a c", p=128)[:, hf * 32:(hf + 1) * 32][:, :, cbase:cbase + 896 + 128][:, :, ds(COFF[0], 128)]), 16)
                    block.sync(qk_sync)

                    for rd in range(4):
                        def qk_t(t, gjob=gjob, rd=rd):
                            W(t, qk_ld[gjob])
                            prev = (gjob, rd - 2) if rd >= 2 else ((gjob - 1, rd + 2) if gjob else None)
                            if prev is not None:
                                t.wait_ge(dv.h, qk_ev.get(prev, 0))
                            for c in range(8):
                                ins = t.transpose(ptk[rd % 2][:, c * 128:(c + 1) * 128],
                                                  in_=qkst[0][:, rd * 1024 + c * 128:rd * 1024 + (c + 1) * 128],
                                                  identity=idb[:])
                            qk_tr[(gjob, rd)] = mm.inc(ins)
                        block.tensor(qk_t)

                        def qk_e(v, u=u, ten=ten, hf=hf, gjob=gjob, rd=rd):
                            v.wait_ge(mm.h, qk_tr[(gjob, rd)])
                            dst = (Qfm if ten == 0 else Kfm)[u]
                            a0 = hf * 4 + rd
                            v.tensor_copy(out=dst[0][:, a0 * 1024:(a0 + 1) * 1024],
                                          in_=ptk[rd % 2][0:64, :])
                            qk_ev[(gjob, rd)] = dv.inc(v.tensor_copy(
                                out=dst[1][:, a0 * 1024:(a0 + 1) * 1024], in_=ptk[rd % 2][64:128, :]))
                        block.vector(qk_e)
                    gjob += 1
        QK_DV = dv.n

        def p2_vld(g):
            g.wait_ge(dv.h, P15_DV_END)
            g.wait_ge(mm.h, P15_MM_END)
            for u in range(2):
                vbase = VCOLS + u * 130
                for hf in range(2):
                    dV[u].inc(g.dma_start(
                        out=VT[u][:].rearrange("p (kt c) -> p kt c", c=130)[:, hf * 32:(hf + 1) * 32, :],
                        in_=qkv_g.rearrange("(kt p) c -> p kt c", p=128)[:, hf * 32:(hf + 1) * 32][:, :, vbase:vbase + 780 + 130][:, :, ds(COFFG[1], 130)]), 16)
                unit_ld[u] = dV[u].fin()
        block.gpsimd(p2_vld)
        p2p.close()

        pS = [ps2(f"pS{i}", [128, 1024], F32) for i in range(2)]
        pc0 = ps2("pc0", [65, 512], F32)
        pc1 = ps2("pc1", [65, 512], F32)
        pbc = ps2("pbc", [64, 1024], F32)

        exp_rec = {}
        scm_rec = {}
        avm_rec = {}
        msk_rec = {}
        ctx_free = [dv.n]
        cta_st = [None]
        gi = 0
        for u in range(2):
            for qc in range(16):
                nkt = 4 * qc + 4
                gqc0 = gi
                for kt in range(nkt):
                    def a_mm(t, u=u, qc=qc, kt=kt, nkt=nkt, gi=gi):
                        if gi == 0:
                            W(t, unit_ld[0])
                            t.wait_ge(dv.h, QK_DV)
                        if u == 1 and qc == 0 and kt == 0:
                            W(t, unit_ld[1])
                        if gi >= 2:
                            t.wait_ge(ac.h, exp_rec[gi - 2])
                        t.matmul(pS[gi % 2][:, 0:512],
                                 lhsT=Kfm[u][0][:, kt * 128:(kt + 1) * 128],
                                 rhs=Qfm[u][0][:, qc * 512:(qc + 1) * 512], start=True, stop=True)
                        ins = t.matmul(pS[gi % 2][:, 512:1024],
                                       lhsT=Kfm[u][1][:, kt * 128:(kt + 1) * 128],
                                       rhs=Qfm[u][1][:, qc * 512:(qc + 1) * 512], start=True, stop=True)
                        scm_rec[gi] = mm.inc(ins)
                        if kt >= 1:
                            gip = gi - 1
                            t.wait_ge(ac.h, exp_rec[gip])
                            if kt - 1 == 0:
                                t.wait_ge(dv.h, ctx_free[0])
                            t.matmul(pc0[:], lhsT=VT[u][:, (kt - 1) * 130:(kt - 1) * 130 + 65],
                                     rhs=Et[gip % 2][:, 0:512], start=(kt - 1 == 0), stop=(kt - 1 == nkt - 1))
                            ins = t.matmul(pc1[:], lhsT=VT[u][:, (kt - 1) * 130 + 65:(kt - 1) * 130 + 130],
                                           rhs=Et[gip % 2][:, 512:1024], start=(kt - 1 == 0), stop=(kt - 1 == nkt - 1))
                            avm_rec[gip] = mm.inc(ins)
                    block.tensor(a_mm)

                    if kt >= 4 * qc:
                        def a_msk(v, qc=qc, kt=kt, gi=gi):
                            v.wait_ge(mm.h, scm_rec[gi])
                            mloc = kt - 4 * qc
                            mview = cm[:, mloc * 512:(mloc + 1) * 512].unsqueeze(1).broadcast_to([128, 2, 512])
                            msk_rec[gi] = dv.inc(v.tensor_tensor(
                                out=pS[gi % 2][:].rearrange("p (b q) -> p b q", q=512),
                                in0=pS[gi % 2][:].rearrange("p (b q) -> p b q", q=512),
                                in1=mview, op=ALU.add))
                        block.vector(a_msk)

                    def a_exp(sc, gi=gi, diag=(kt >= 4 * qc)):
                        sc.wait_ge(mm.h, scm_rec[gi])
                        if diag:
                            sc.wait_ge(dv.h, msk_rec[gi])
                        if gi >= 2:
                            sc.wait_ge(mm.h, avm_rec[gi - 2])
                        exp_rec[gi] = ac.inc(sc.activation(out=Et[gi % 2][:], in_=pS[gi % 2][:],
                                                           func=AF.Exp, scale=0.125))
                    block.scalar(a_exp)
                    gi += 1

                def a_avl(t, u=u, qc=qc, nkt=nkt, gi=gi):
                    gip = gi - 1
                    t.wait_ge(ac.h, exp_rec[gip])
                    if nkt == 1:
                        t.wait_ge(dv.h, ctx_free[0])
                    kt = nkt - 1
                    t.matmul(pc0[:], lhsT=VT[u][:, kt * 130:kt * 130 + 65],
                             rhs=Et[gip % 2][:, 0:512], start=(kt == 0), stop=True)
                    ins = t.matmul(pc1[:], lhsT=VT[u][:, kt * 130 + 65:kt * 130 + 130],
                                   rhs=Et[gip % 2][:, 512:1024], start=(kt == 0), stop=True)
                    avm_rec[gip] = mm.inc(ins)
                block.tensor(a_avl)

                rr_dv = [0]

                def a_rr(v, gi=gi):
                    v.wait_ge(mm.h, avm_rec[gi - 1])
                    dv.inc(v.reciprocal(out=rrow[0:1, 0:512], in_=pc0[64:65, :]))
                    dv.inc(v.reciprocal(out=rrow[0:1, 512:1024], in_=pc1[64:65, :]))
                    v.wait_ge(dv.h, dv.n)
                    rr_dv[0] = dv.n
                block.vector(a_rr)

                bc_mm = [0]

                def a_bc(t):
                    t.wait_ge(dv.h, rr_dv[0])
                    t.matmul(pbc[:, 0:512], lhsT=of32[0:1, 0:64], rhs=rrow[0:1, 0:512],
                             start=True, stop=True)
                    ins = t.matmul(pbc[:, 512:1024], lhsT=of32[0:1, 0:64], rhs=rrow[0:1, 512:1024],
                                   start=True, stop=True)
                    bc_mm[0] = mm.inc(ins)
                block.tensor(a_bc)

                def a_nrm(v, u=u, qc=qc):
                    v.wait_ge(mm.h, bc_mm[0])
                    if cta_st[0] is not None:
                        W(v, cta_st[0])
                    dv.inc(v.tensor_copy(out=rb[:], in_=pbc[:]))
                    v.wait_ge(dv.h, dv.n)
                    v.tensor_tensor(out=cta[0:64, :], in0=pc0[0:64, :], in1=rb[:, 0:512], op=ALU.mult)
                    ctx_free[0] = dv.inc(v.tensor_tensor(out=cta[64:128, :], in0=pc1[0:64, :],
                                                         in1=rb[:, 512:1024], op=ALU.mult))
                block.vector(a_nrm)

                def a_st(sync, u=u, qc=qc):
                    sync.wait_ge(dv.h, ctx_free[0])
                    cta_st[0] = sC.rec(sync.dma_start(out=ctx_loc[u * 128:(u + 1) * 128, qc * 512:(qc + 1) * 512],
                                          in_=cta[:]), 16)
                block.sync(a_st)

        def coll2(g):
            W(g, sC.fin())
            cc.inc(g.collective_compute("AllGather", ALU.bypass, replica_groups=rg,
                                        ins=[ctx_loc.opt()], outs=[ctx_g.opt()]))
        block.gpsimd(coll2)
        CC2 = cc.n

        p2.close()

        p3 = ExitStack()
        sb3 = lambda n_, s_, d_: p3.enter_context(nc.sbuf_tensor(n_, s_, d_))
        hbf = sb3("hbf", [128, 8 * TOK], BF16)
        wld3 = sb3("wld3", [128, 1024], F32)
        a3r = sb3("a3r", [1, 2048], F32)
        c3r = sb3("c3r", [1, 2048], F32)
        pA = [p3.enter_context(nc.psum_tensor(f"pA{i}", [128, 512], F32)) for i in range(2)]
        pB = [p3.enter_context(nc.psum_tensor(f"pB{i}", [128, 512], F32)) for i in range(2)]

        p3a = ExitStack()
        sa = lambda n_, s_, d_: p3a.enter_context(nc.sbuf_tensor(n_, s_, d_))
        xnr = sa("xnr", [128, 8 * TOK], BF16)
        ctx_t = sa("ctx_t", [128, 8 * 512], BF16)
        oU = sa("oU", [128, 8 * 768], BF16)
        oV = sa("oV", [128, 6 * 1024], BF16)
        t1s = sa("t1s", [128, 6 * 512], BF16)
        ao = sa("ao", [128, 512], F32)

        XNR_LD = [0]

        def p3a_xn(sync):
            sync.wait_ge(cc.h, CC2)
            W(sync, sX.fin())
            XNR_LD[0] = dXN.rec(sync.dma_start(out=xnr[:], in_=xn_sp[:, :]), 16)
        block.sync(p3a_xn)

        ocast = {}

        def emit_wcast(i, src_ap, w, dst_ap):
            def ow_sync(sync, i=i):
                if i >= 1:
                    sync.wait_ge(dv.h, ocast[i - 1])
                ocast[(i, "ld")] = dW3.rec(sync.dma_start(out=wld3[:, 0:w], in_=src_ap), 16)
            block.sync(ow_sync)

            def ow_vec(v, i=i):
                W(v, ocast[(i, "ld")])
                ocast[i] = dv.inc(v.tensor_copy(out=dst_ap, in_=wld3[:, 0:w]))
            block.vector(ow_vec)

        def p3a_gate(sync):
            sync.wait_ge(cc.h, CC2)
        block.sync(p3a_gate)

        wi = 0
        for i in range(8):
            emit_wcast(wi, outU[i * 128:(i + 1) * 128, :], 768, oU[:, i * 768:(i + 1) * 768])
            wi += 1
        for i in range(6):
            emit_wcast(wi, outV[i * 128:(i + 1) * 128, :], 1024, oV[:, i * 1024:(i + 1) * 1024])
            wi += 1
        OW_DV = dv.n

        ctx_ld = [0] * 4
        t1_mm = {}
        t1_ev = {}
        t2_mm = {}
        h_dv = {}
        HSC = {}

        for tc in range(4):
            def c_sync(g, tc=tc):
                if tc == 0:
                    g.wait_ge(cc.h, CC2)
                if tc >= 2:
                    g.wait_ge(mm.h, t1_mm.get((tc - 2, 5), 0))
                ctx_ld[tc] = dT[tc % 2].rec(g.dma_start(
                    out=ctx_t[:].rearrange("p (dch t) -> p dch t", t=512),
                    in_=ctx_g.rearrange("(dch p) t -> p dch t", p=128)[:, :, tc * 512:tc * 512 + 6144 + 512][:, :, ds(COFFG[2], 512)]), 16)
            block.gpsimd(c_sync)

            def t1_pre(t, tc=tc):
                W(t, ctx_ld[tc])
                if tc == 0:
                    t.wait_ge(dv.h, OW_DV)
            block.tensor(t1_pre)

            for cc6 in range(6):
                def t1_f(t, tc=tc, cc6=cc6):
                    gk = tc * 6 + cc6
                    if gk >= 2:
                        t.wait_ge(dv.h, t1_ev.get(gk - 2, 0))
                    for dch in range(8):
                        ins = t.matmul(pA[gk % 2][:],
                                       lhsT=oU[:, dch * 768 + cc6 * 128:dch * 768 + (cc6 + 1) * 128],
                                       rhs=ctx_t[:, dch * 512:(dch + 1) * 512],
                                       start=(dch == 0), stop=(dch == 7))
                    t1_mm[(tc, cc6)] = mm.inc(ins)
                block.tensor(t1_f)

                def t1_e(v, tc=tc, cc6=cc6):
                    gk = tc * 6 + cc6
                    v.wait_ge(mm.h, t1_mm[(tc, cc6)])
                    t1_ev[gk] = dv.inc(v.tensor_copy(out=t1s[:, cc6 * 512:(cc6 + 1) * 512],
                                                     in_=pA[gk % 2][:]))
                block.vector(t1_e)
            T1E_DV = dv.n

            for dch in range(8):
                def t2_f2(t, tc=tc, dch=dch):
                    if dch == 0:
                        t.wait_ge(dv.h, T1E_DV)
                    gk = tc * 8 + dch
                    if gk >= 2:
                        t.wait_ge(dv.h, h_dv.get(gk - 2, 0))
                    for rc in range(6):
                        ins = t.matmul(pB[gk % 2][:],
                                       lhsT=oV[:, rc * 1024 + dch * 128:rc * 1024 + (dch + 1) * 128],
                                       rhs=t1s[:, rc * 512:(rc + 1) * 512],
                                       start=(rc == 0), stop=(rc == 5))
                    t2_mm[(tc, dch)] = mm.inc(ins)
                block.tensor(t2_f2)
            for dch in range(8):
                def h_dve(v, tc=tc, dch=dch):
                    v.wait_ge(mm.h, t2_mm[(tc, dch)])
                    if tc == 0 and dch == 0:
                        W(v, XNR_LD[0])
                    gk = tc * 8 + dch
                    v.tensor_scalar(out=ao[:], in0=pB[gk % 2][:],
                                    scalar1=ob_pp[:, dch:dch + 1], scalar2=None, op0=ALU.add)
                    h_dv[gk] = dv.inc(v.tensor_tensor(
                        out=hbf[:].rearrange("p (c t) -> p c t", t=TOK)[:, dch, tc * 512:(tc + 1) * 512],
                        in0=ao[:],
                        in1=xnr[:].rearrange("p (c t) -> p c t", t=TOK)[:, dch, tc * 512:(tc + 1) * 512],
                        op=ALU.add))
                block.vector(h_dve)
        H_DONE_DV = dv.n

        p3a.close()

        p3b = ExitStack()
        sbb = lambda n_, s_, d_: p3b.enter_context(nc.sbuf_tensor(n_, s_, d_))
        h2t = sbb("h2t", [128, 512], BF16)
        srow = sbb("srow", [1, 512], F32)
        trow = sbb("trow", [1, 512], F32)
        pR = [p3b.enter_context(nc.psum_tensor(f"pR{i}", [1, 512], F32)) for i in range(2)]

        sq_ac = {}
        st_mm = {}
        row_dv = {}
        row_ac = {}

        for tc in range(4):
            for dch in range(8):
                def b_sc(sc, tc=tc, dch=dch):
                    if tc == 0 and dch == 0:
                        sc.wait_ge(dv.h, H_DONE_DV)
                    if dch == 0 and tc >= 1:
                        sc.wait_ge(dv.h, row_dv.get(tc - 1, 0))
                    if dch >= 1:
                        sc.wait_ge(mm.h, st_mm.get((tc, dch - 1), 0))
                    sq_ac[(tc, dch)] = ac.inc(sc.activation(
                        out=h2t[:],
                        in_=hbf[:].rearrange("p (c t) -> p c t", t=TOK)[:, dch, tc * 512:(tc + 1) * 512],
                        func=AF.Square))
                block.scalar(b_sc)

                def b_mm(t, tc=tc, dch=dch):
                    t.wait_ge(ac.h, sq_ac[(tc, dch)])
                    t.matmul(pR[0][:],
                             lhsT=ones1[:],
                             rhs=hbf[:].rearrange("p (c t) -> p c t", t=TOK)[:, dch, tc * 512:(tc + 1) * 512],
                             start=(dch == 0), stop=(dch == 7))
                    ins = t.matmul(pR[1][:], lhsT=ones1[:], rhs=h2t[:],
                                   start=(dch == 0), stop=(dch == 7))
                    st_mm[(tc, dch)] = mm.inc(ins)
                block.tensor(b_mm)

            def b_row(v, tc=tc):
                v.wait_ge(mm.h, st_mm[(tc, 7)])
                v.tensor_scalar(out=srow[:], in0=pR[0][:], scalar1=1.0 / D, scalar2=None, op0=ALU.mult)
                dv.inc(v.tensor_scalar(out=trow[:], in0=pR[1][:], scalar1=1.0 / D, scalar2=None, op0=ALU.mult))
                v.wait_ge(dv.h, dv.n)
                dv.inc(v.scalar_tensor_tensor(out=c3r[0:1, tc * 512:(tc + 1) * 512], in0=srow[:], scalar=1.0,
                                              in1=srow[:], op0=ALU.mult, op1=ALU.mult))
                v.wait_ge(dv.h, dv.n)
                dv.inc(v.tensor_sub(out=trow[:], in0=trow[:], in1=c3r[0:1, tc * 512:(tc + 1) * 512]))
                v.wait_ge(dv.h, dv.n)
                row_dv[tc] = dv.inc(v.tensor_scalar(out=trow[:], in0=trow[:], scalar1=EPS,
                                                    scalar2=None, op0=ALU.add))
            block.vector(b_row)

            def b_rs(sc, tc=tc):
                sc.wait_ge(dv.h, row_dv[tc])
                sc.activation(out=a3r[0:1, tc * 512:(tc + 1) * 512], in_=trow[:], func=AF.Ln)
                row_ac[tc] = ac.inc(sc.activation(out=a3r[0:1, tc * 512:(tc + 1) * 512],
                                                  in_=a3r[0:1, tc * 512:(tc + 1) * 512],
                                                  func=AF.Exp, scale=-0.5))
            block.scalar(b_rs)

            def b_c3(v, tc=tc):
                v.wait_ge(ac.h, row_ac[tc])
                row_dv[tc] = dv.inc(v.scalar_tensor_tensor(
                    out=c3r[0:1, tc * 512:(tc + 1) * 512], in0=srow[:], scalar=-1.0,
                    in1=a3r[0:1, tc * 512:(tc + 1) * 512], op0=ALU.mult, op1=ALU.mult))
            block.vector(b_c3)
        STATS_DV = dv.n
        STATS_MM = mm.n
        p3b.close()

        p3c = ExitStack()
        scx = lambda n_, s_, d_: p3c.enter_context(nc.sbuf_tensor(n_, s_, d_))
        f1U = scx("f1U", [128, 8 * 512], BF16)
        f1V = scx("f1V", [128, 4 * 4096], BF16)
        f2U = scx("f2U", [128, 32 * 512], BF16)
        f2V = scx("f2V", [128, 4 * 1024], BF16)
        lnt = scx("lnt", [128, 8 * 512], BF16)
        t34 = scx("t34", [128, 4 * 512], BF16)
        hmid = scx("hmid", [128, 32 * 512], BF16)
        u1s = scx("u1s", [128, 512], F32)
        bb3 = scx("bb3", [128, 1024], F32)
        ao3 = scx("ao3", [128, 512], F32)
        pBC = p3c.enter_context(nc.psum_tensor("pBC", [128, 1024], F32))

        fcast = {}

        def emit_fw(i, src_ap, w, dst_ap):
            def fw_sync(sync, i=i):
                if i >= 1:
                    sync.wait_ge(dv.h, fcast[i - 1])
                fcast[(i, "ld")] = dW3.rec(sync.dma_start(out=wld3[:, 0:w], in_=src_ap), 16)
            block.sync(fw_sync)

            def fw_vec(v, i=i):
                W(v, fcast[(i, "ld")])
                fcast[i] = dv.inc(v.tensor_copy(out=dst_ap, in_=wld3[:, 0:w]))
            block.vector(fw_vec)

        def p3c_gate(sync):
            sync.wait_ge(mm.h, STATS_MM)
            sync.wait_ge(dv.h, STATS_DV)
        block.sync(p3c_gate)

        fwi = 0
        for i in range(8):
            emit_fw(fwi, fc1U[i * 128:(i + 1) * 128, :], 512, f1U[:, i * 512:(i + 1) * 512])
            fwi += 1
        for i in range(4):
            for qr in range(4):
                emit_fw(fwi, fc1V[i * 128:(i + 1) * 128, qr * 1024:(qr + 1) * 1024], 1024,
                        f1V[:, i * 4096 + qr * 1024:i * 4096 + (qr + 1) * 1024])
                fwi += 1
        for i in range(32):
            emit_fw(fwi, fc2U[i * 128:(i + 1) * 128, :], 512, f2U[:, i * 512:(i + 1) * 512])
            fwi += 1
        for i in range(4):
            emit_fw(fwi, fc2V[i * 128:(i + 1) * 128, :], 1024, f2V[:, i * 1024:(i + 1) * 1024])
            fwi += 1
        FW_DV = dv.n

        bc3_mm = {}
        ln_dv = {}
        ln_ac = {}
        t3_mm = {}
        t3_ev = {}
        hm_mm = {}
        hm_ac = {}
        t4_mm = {}
        t4_ev = {}
        o5_mm = {}
        o5_ac = {}
        blk_dv = {}

        for tc in range(4):
            def m_bc(t, tc=tc):
                if tc == 0:
                    t.wait_ge(dv.h, STATS_DV)
                if tc >= 1:
                    t.wait_ge(dv.h, ln_dv.get((tc - 1, 7), 0))
                t.matmul(pBC[:, 0:512], lhsT=of32[0:1, 0:128], rhs=a3r[0:1, tc * 512:(tc + 1) * 512],
                         start=True, stop=True)
                ins = t.matmul(pBC[:, 512:1024], lhsT=of32[0:1, 0:128], rhs=c3r[0:1, tc * 512:(tc + 1) * 512],
                               start=True, stop=True)
                bc3_mm[tc] = mm.inc(ins)
            block.tensor(m_bc)

            def m_cp(v, tc=tc):
                v.wait_ge(mm.h, bc3_mm[tc])
                dv.inc(v.tensor_copy(out=bb3[:], in_=pBC[:]))
                v.wait_ge(dv.h, dv.n)
                bc3_mm[(tc, "cp")] = dv.n
            block.vector(m_cp)

            for dch in range(8):
                def m_ln(v, tc=tc, dch=dch):
                    v.wait_ge(dv.h, bc3_mm[(tc, "cp")])
                    prev = (tc, dch - 1) if dch else ((tc - 1, 7) if tc else None)
                    if prev is not None:
                        v.wait_ge(ac.h, ln_ac[prev])
                    v.tensor_mul(out=u1s[:],
                                 in0=hbf[:].rearrange("p (c t) -> p c t", t=TOK)[:, dch, tc * 512:(tc + 1) * 512],
                                 in1=bb3[:, 0:512])
                    ln_dv[(tc, dch)] = dv.inc(v.tensor_add(out=u1s[:], in0=u1s[:], in1=bb3[:, 512:1024]))
                block.vector(m_ln)

                def m_lna(sc, tc=tc, dch=dch):
                    sc.wait_ge(dv.h, ln_dv[(tc, dch)])
                    ln_ac[(tc, dch)] = ac.inc(sc.activation(
                        out=lnt[:, dch * 512:(dch + 1) * 512], in_=u1s[:], func=AF.Identity,
                        scale=l2g[:, dch:dch + 1], bias=l2b[:, dch:dch + 1]))
                block.scalar(m_lna)
            LN_AC = ac.n

            for rc in range(4):
                def m_t3(t, tc=tc, rc=rc):
                    if rc == 0:
                        t.wait_ge(ac.h, LN_AC)
                        if tc == 0:
                            t.wait_ge(dv.h, FW_DV)
                    gk = tc * 4 + rc
                    if gk >= 2:
                        t.wait_ge(dv.h, t3_ev.get(gk - 2, 0))
                    for dch in range(8):
                        ins = t.matmul(pA[gk % 2][:],
                                       lhsT=f1U[:, dch * 512 + rc * 128:dch * 512 + (rc + 1) * 128],
                                       rhs=lnt[:, dch * 512:(dch + 1) * 512],
                                       start=(dch == 0), stop=(dch == 7))
                    t3_mm[gk] = mm.inc(ins)
                block.tensor(m_t3)

                def m_t3e(v, tc=tc, rc=rc):
                    gk = tc * 4 + rc
                    v.wait_ge(mm.h, t3_mm[gk])
                    t3_ev[gk] = dv.inc(v.tensor_copy(out=t34[:, rc * 512:(rc + 1) * 512],
                                                     in_=pA[gk % 2][:]))
                block.vector(m_t3e)
            T3E_DV = dv.n

            for ic in range(32):
                def m_hm(t, tc=tc, ic=ic, tdv=T3E_DV):
                    if ic == 0:
                        t.wait_ge(dv.h, tdv)
                    gk = tc * 32 + ic
                    if gk >= 2:
                        t.wait_ge(ac.h, hm_ac.get(gk - 2, 0))
                    for rc in range(4):
                        ins = t.matmul(pB[gk % 2][:],
                                       lhsT=f1V[:, rc * 4096 + ic * 128:rc * 4096 + (ic + 1) * 128],
                                       rhs=t34[:, rc * 512:(rc + 1) * 512],
                                       start=(rc == 0), stop=(rc == 3))
                    hm_mm[gk] = mm.inc(ins)
                block.tensor(m_hm)

                def m_ge(sc, tc=tc, ic=ic):
                    sc.wait_ge(mm.h, hm_mm[tc * 32 + ic])
                    hm_ac[tc * 32 + ic] = ac.inc(sc.activation(
                        out=hmid[:, ic * 512:(ic + 1) * 512], in_=pB[(tc * 32 + ic) % 2][:],
                        func=AF.Gelu_apprx_tanh, bias=f1b_pp[:, ic:ic + 1]))
                block.scalar(m_ge)
            HM_AC = ac.n

            for cc4 in range(4):
                def m_t4(t, tc=tc, cc4=cc4):
                    if cc4 == 0:
                        t.wait_ge(ac.h, HM_AC)
                    gk = tc * 4 + cc4
                    if gk >= 2:
                        t.wait_ge(dv.h, t4_ev.get(gk - 2, 0))
                    for ic in range(32):
                        ins = t.matmul(pA[gk % 2][:],
                                       lhsT=f2U[:, ic * 512 + cc4 * 128:ic * 512 + (cc4 + 1) * 128],
                                       rhs=hmid[:, ic * 512:(ic + 1) * 512],
                                       start=(ic == 0), stop=(ic == 31))
                    t4_mm[gk] = mm.inc(ins)
                block.tensor(m_t4)

                def m_t4e(v, tc=tc, cc4=cc4):
                    gk = tc * 4 + cc4
                    v.wait_ge(mm.h, t4_mm[gk])
                    t4_ev[gk] = dv.inc(v.tensor_copy(out=t34[:, cc4 * 512:(cc4 + 1) * 512],
                                                     in_=pA[gk % 2][:]))
                block.vector(m_t4e)
            T4E_DV = dv.n

            for dch in range(8):
                def m_o5(t, tc=tc, dch=dch, tdv=T4E_DV):
                    if dch == 0:
                        t.wait_ge(dv.h, tdv)
                    gk = tc * 8 + dch
                    if gk >= 2:
                        t.wait_ge(dv.h, blk_dv.get(gk - 2, 0))
                    for cc4 in range(4):
                        ins = t.matmul(pB[gk % 2][:],
                                       lhsT=f2V[:, cc4 * 1024 + dch * 128:cc4 * 1024 + (dch + 1) * 128],
                                       rhs=t34[:, cc4 * 512:(cc4 + 1) * 512],
                                       start=(cc4 == 0), stop=(cc4 == 3))
                    o5_mm[gk] = mm.inc(ins)
                block.tensor(m_o5)

                def m_oa(sc, tc=tc, dch=dch):
                    sc.wait_ge(mm.h, o5_mm[tc * 8 + dch])
                    if tc * 8 + dch >= 1:
                        sc.wait_ge(dv.h, blk_dv.get(tc * 8 + dch - 1, 0))
                    o5_ac[tc * 8 + dch] = ac.inc(sc.activation(
                        out=ao3[:], in_=pB[(tc * 8 + dch) % 2][:], func=AF.Identity,
                        bias=f2b_pp[:, dch:dch + 1]))
                block.scalar(m_oa)

                def m_ov(v, tc=tc, dch=dch):
                    v.wait_ge(ac.h, o5_ac[tc * 8 + dch])
                    hsl = hbf[:].rearrange("p (c t) -> p c t", t=TOK)[:, dch, tc * 512:(tc + 1) * 512]
                    blk_dv[tc * 8 + dch] = dv.inc(v.tensor_tensor(out=hsl, in0=hsl, in1=ao3[:], op=ALU.add))
                block.vector(m_ov)
        BLK_DV = dv.n
        p3c.close()

        p3d = ExitStack()
        sdd = lambda n_, s_, d_: p3d.enter_context(nc.sbuf_tensor(n_, s_, d_))
        fft = sdd("fft", [128, 8 * 512], BF16)
        sfin = sdd("sfin", [128, 8 * 512], F32)
        xrt = [sdd(f"xrt{i}", [128, D], F32) for i in range(2)]
        ot = [sdd(f"ot{i}", [128, D], F32) for i in range(2)]
        ptf = [p3d.enter_context(nc.psum_tensor(f"ptf{i}", [128, 1024], F32)) for i in range(2)]

        fft_ld = [0] * 4
        sf_dv = [0] * 4
        ftr_mm = {}
        ot_dv = {}
        out_st = {}
        xr_ld = {}

        for tc in range(4):
            def d_ld(sync, tc=tc):
                if tc == 0:
                    W(sync, FFN_ST)
                    sync.wait_ge(dv.h, BLK_DV)
                if tc >= 1:
                    sync.wait_ge(dv.h, sf_dv[tc - 1])
                fft_ld[tc] = dF.rec(sync.dma_start(out=fft[:],
                                      in_=ffn_sp.rearrange("p (c t) -> p c t", t=TOK)[:, :, tc * 512:(tc + 1) * 512]), 16)
            block.sync(d_ld)

            def d_sum(v, tc=tc):
                W(v, fft_ld[tc])
                if tc == 0:
                    v.wait_ge(dv.h, BLK_DV)
                for dch in range(8):
                    ins = v.tensor_tensor(
                        out=sfin[:, dch * 512:(dch + 1) * 512],
                        in0=hbf[:].rearrange("p (c t) -> p c t", t=TOK)[:, dch, tc * 512:(tc + 1) * 512],
                        in1=fft[:, dch * 512:(dch + 1) * 512], op=ALU.add)
                sf_dv[tc] = dv.inc(ins)
            block.vector(d_sum)

            for tb in range(4):
                ti = tc * 4 + tb

                def d_x(sync, tc=tc, tb=tb, ti=ti):
                    if ti >= 2:
                        sync.wait_ge(dv.h, ot_dv.get(ti - 2, 0))
                    xr_ld[ti] = dX[ti % 2].rec(sync.dma_start(out=xrt[ti % 2][:],
                                          in_=x_own[ti * 128:(ti + 1) * 128, :]), 16)
                block.sync(d_x)

                def d_tr(t, tc=tc, tb=tb, ti=ti):
                    t.wait_ge(dv.h, sf_dv[tc])
                    if ti >= 2:
                        t.wait_ge(dv.h, ot_dv.get(ti - 2, 0))
                    for dch in range(8):
                        ins = t.transpose(ptf[ti % 2][:, dch * 128:(dch + 1) * 128],
                                          in_=sfin[:, dch * 512 + tb * 128:dch * 512 + (tb + 1) * 128],
                                          identity=idf[:])
                    ftr_mm[ti] = mm.inc(ins)
                block.tensor(d_tr)

                def d_add(v, ti=ti):
                    v.wait_ge(mm.h, ftr_mm[ti])
                    W(v, xr_ld[ti])
                    if ti >= 2 and out_st.get(ti - 2):
                        W(v, out_st.get(ti - 2))
                    v.tensor_tensor(out=ot[ti % 2][:, 0:512], in0=ptf[ti % 2][:, 0:512],
                                    in1=xrt[ti % 2][:, 0:512], op=ALU.add)
                    ot_dv[ti] = dv.inc(v.tensor_tensor(out=ot[ti % 2][:, 512:1024], in0=ptf[ti % 2][:, 512:1024],
                                                       in1=xrt[ti % 2][:, 512:1024], op=ALU.add))
                block.vector(d_add)

                def d_st(sync, ti=ti):
                    sync.wait_ge(dv.h, ot_dv[ti])
                    out_st[ti] = sO[ti % 2].rec(sync.dma_start(out=out_t[ti * 128:(ti + 1) * 128, :], in_=ot[ti % 2][:]), 16)
                block.sync(d_st)
        def fin_sync(sync):
            W(sync, sO[0].fin())
            W(sync, sO[1].fin())
        block.sync(fin_sync)

    p3d.close()
    p3.close()
    top.close()
    return nc


_CACHE = {}


def kernel(**inputs):
    g = lambda k: np.asarray(inputs[k])
    x = g("L0_attn_hidden_states").astype(np.float32)

    qU, kU, vU = g("L0_attn_q_U"), g("L0_attn_k_U"), g("L0_attn_v_U")
    qV, kV, vV = g("L0_attn_q_V"), g("L0_attn_k_V"), g("L0_attn_v_V")
    qb, kb, vb = g("L0_attn_q_b"), g("L0_attn_k_b"), g("L0_attn_v_b")

    qkvU = np.zeros((D, 3 * 1024), np.float32)
    qkvV = np.zeros((128, 3 * 1024), np.float32)
    qkvb = np.zeros((128, 3088), np.float32)
    for pi, (U, V, bvec) in enumerate([(qU, qV, qb), (kU, kV, kb), (vU, vV, vb)]):
        for h in range(H):
            qkvU[:, pi * 1024 + h * 64:pi * 1024 + h * 64 + 48] = U[:, h, :]
            qkvV[0:48, pi * 1024 + h * 64:pi * 1024 + (h + 1) * 64] = V[h]
            if pi < 2:
                qkvb[:, pi * 1024 + h * 64:pi * 1024 + (h + 1) * 64] = bvec[h][None, :]
            else:
                qkvb[:, 2048 + h * 65:2048 + h * 65 + 64] = bvec[h][None, :]
                qkvb[:, 2048 + h * 65 + 64] = 1.0

    pp = lambda vec, nch: np.ascontiguousarray(vec.astype(np.float32).reshape(nch, 128).T)

    kp = np.arange(128)[:, None]
    qq = np.arange(512)[None, :]
    cmask = np.zeros((128, 4 * 512), np.float32)
    for m in range(4):
        cmask[:, m * 512:(m + 1) * 512] = np.where(qq >= m * 128 + kp, 0.0, -1e9)

    feeds_common = {
        "qkvU": qkvU,
        "qkvV": qkvV,
        "ln1g": pp(g("L0_attn_ln1_g"), 8),
        "ln1b": pp(g("L0_attn_ln1_b"), 8),
        "ln2g": pp(g("L0_attn_ln2_g"), 8),
        "ln2b": pp(g("L0_attn_ln2_b"), 8),
        "outU": g("L0_attn_out_U").astype(np.float32),
        "outV": g("L0_attn_out_V").astype(np.float32),
        "outb": pp(g("L0_attn_out_b"), 8),
        "fc1U": g("L0_attn_fc1_U").astype(np.float32),
        "fc1V": g("L0_attn_fc1_V").astype(np.float32),
        "fc1b": pp(g("L0_attn_fc1_b"), 32),
        "fc2U": g("L0_attn_fc2_U").astype(np.float32),
        "fc2V": g("L0_attn_fc2_V").astype(np.float32),
        "fc2b": pp(g("L0_attn_fc2_b"), 8),
        "wgwu": np.ascontiguousarray(
            np.concatenate([g("L0_ffn_w_gate"), g("L0_ffn_w_up")], axis=0).astype(np.float32)),
        "wdn": g("L0_ffn_w_down").astype(np.float32),
        "cmask": cmask.astype(ml_dtypes.bfloat16),
        "qkvb": qkvb,
        "idenb": np.eye(128).astype(ml_dtypes.bfloat16),
        "idenf": np.eye(128, dtype=np.float32),
    }

    in_maps = []
    for c in range(8):
        b, j = c // 4, c % 4
        m = dict(feeds_common)
        m["x_own"] = np.ascontiguousarray(x[b, j * TOK:(j + 1) * TOK, :])
        m["coffs"] = np.tile(np.array([[j * 256, j * 260, j * TOK, 0]], np.int32), (128, 1))
        in_maps.append(m)

    if "nc" not in _CACHE:
        _CACHE["nc"] = build_nc()
    res = run_bass_kernel_spmd(_CACHE["nc"], in_maps, core_ids=list(range(8)))

    out = np.empty((B, S, D), np.float32)
    for c in range(8):
        b, j = c // 4, c % 4
        out[b, j * TOK:(j + 1) * TOK, :] = res.results[c]["out"]
    return out


# version 10
# speedup vs baseline: 1.0183x; 1.0183x over previous
import numpy as np
import ml_dtypes
from contextlib import ExitStack

import concourse.bass as bass
import concourse.mybir as mybir
from concourse.bass_utils import run_bass_kernel_spmd
from concourse.expressions import RuntimeValue

F32 = mybir.dt.float32
BF16 = mybir.dt.bfloat16
I32 = mybir.dt.int32
AF = mybir.ActivationFunctionType
ALU = mybir.AluOpType
AX = mybir.AxisListType
ds = bass.ds

B, S, D, H, DH = 2, 8192, 1024, 16, 64
TOK = 2048
NT = TOK // 128
EPS = 1e-5
NFFN = 2816
IM = 4096
VCOLS = 2048
GCOLS = 2048 + 16 * 65
MAGIC = 1.5 * (2 ** 22)
NB_LIST = [(0, 512), (512, 512), (1024, 512), (1536, 512), (2048, 512), (2560, 256)]


class Ctr:
    def __init__(self, h):
        self.h = h
        self.n = 0

    def inc(self, inst, k=1):
        inst.then_inc(self.h, k)
        self.n += k
        return self.n

    def rec(self, inst, k=1):
        self.inc(inst, k)
        return (self.h, self.n)

    def fin(self):
        return (self.h, self.n)


def W(eng, rec):
    if rec and rec[1] > 0:
        eng.wait_ge(rec[0], rec[1])


def build_nc():
    nc = bass.Bass("TRN2", target_bir_lowering=False)
    dt = nc.dram_tensor

    x_own = dt("x_own", [TOK, D], F32, kind="ExternalInput").ap()
    qkvU = dt("qkvU", [D, 3 * 1024], F32, kind="ExternalInput").ap()
    qkvV = dt("qkvV", [128, 3 * 1024], F32, kind="ExternalInput").ap()
    ln1g = dt("ln1g", [128, 8], F32, kind="ExternalInput").ap()
    ln1b = dt("ln1b", [128, 8], F32, kind="ExternalInput").ap()
    ln2g = dt("ln2g", [128, 8], F32, kind="ExternalInput").ap()
    ln2b = dt("ln2b", [128, 8], F32, kind="ExternalInput").ap()
    outU = dt("outU", [D, 768], F32, kind="ExternalInput").ap()
    outV = dt("outV", [768, D], F32, kind="ExternalInput").ap()
    outb = dt("outb", [128, 8], F32, kind="ExternalInput").ap()
    fc1U = dt("fc1U", [D, 512], F32, kind="ExternalInput").ap()
    fc1V = dt("fc1V", [512, IM], F32, kind="ExternalInput").ap()
    fc1b = dt("fc1b", [128, 32], F32, kind="ExternalInput").ap()
    fc2U = dt("fc2U", [IM, 512], F32, kind="ExternalInput").ap()
    fc2V = dt("fc2V", [512, D], F32, kind="ExternalInput").ap()
    fc2b = dt("fc2b", [128, 8], F32, kind="ExternalInput").ap()
    wgwu = dt("wgwu", [2 * NFFN, D], F32, kind="ExternalInput").ap()
    wdn = dt("wdn", [D, NFFN], F32, kind="ExternalInput").ap()
    cmask = dt("cmask", [128, 4 * 512], BF16, kind="ExternalInput").ap()
    idenb = dt("idenb", [128, 128], BF16, kind="ExternalInput").ap()
    idenf = dt("idenf", [128, 128], F32, kind="ExternalInput").ap()
    qkvb = dt("qkvb", [128, 3088], F32, kind="ExternalInput").ap()
    coffs = dt("coffs", [128, 4], I32, kind="ExternalInput").ap()

    out_t = dt("out", [TOK, D], F32, kind="ExternalOutput").ap()

    qkv_loc = dt("qkv_loc", [TOK, GCOLS], BF16, kind="Internal").ap()
    qkv_g = dt("qkv_g", [S, GCOLS], BF16, kind="Internal").ap()
    ctx_loc = dt("ctx_loc", [256, S], BF16, kind="Internal").ap()
    ctx_g = dt("ctx_g", [1024, S], BF16, kind="Internal").ap()
    xn_sp = dt("xn_sp", [128, 8 * TOK], BF16, kind="Internal").ap()
    ffn_sp = dt("ffn_sp", [128, 8 * TOK], BF16, kind="Internal").ap()

    rg = [[0, 1, 2, 3], [4, 5, 6, 7]]

    top = ExitStack()
    sbT = lambda n_, s_, d_: top.enter_context(nc.sbuf_tensor(n_, s_, d_))
    C = {k: Ctr(top.enter_context(nc.semaphore("s_" + k)))
         for k in ["ld", "st", "mm", "dv", "ac", "cc", "dK", "dCO", "dC", "dW3", "dXN", "dF",
                   "sX", "sF", "sC",
                   "dX0", "dX1", "dW0", "dW1", "dQ0", "dQ1", "dV0", "dV1", "dT0", "dT1",
                   "sA0", "sA1", "sO0", "sO1"]}
    ld, st, mm, dv, ac, cc = (C[k] for k in ["ld", "st", "mm", "dv", "ac", "cc"])
    dK, dCO, dC, dW3, dXN, dF = (C[k] for k in ["dK", "dCO", "dC", "dW3", "dXN", "dF"])
    sX, sF, sC = (C[k] for k in ["sX", "sF", "sC"])
    dX = [C["dX0"], C["dX1"]]
    dW = [C["dW0"], C["dW1"]]
    dQ = [C["dQ0"], C["dQ1"]]
    dV = [C["dV0"], C["dV1"]]
    dT = [C["dT0"], C["dT1"]]
    sA = [C["sA0"], C["sA1"]]
    sO = [C["sO0"], C["sO1"]]

    cm = sbT("cm", [128, 2048], BF16)
    idb = sbT("idb", [128, 128], BF16)
    idf = sbT("idf", [128, 128], F32)
    l1g = sbT("l1g", [128, 8], F32)
    l1b = sbT("l1b", [128, 8], F32)
    l2g = sbT("l2g", [128, 8], F32)
    l2b = sbT("l2b", [128, 8], F32)
    ob_pp = sbT("ob_pp", [128, 8], F32)
    f1b_pp = sbT("f1b_pp", [128, 32], F32)
    f2b_pp = sbT("f2b_pp", [128, 8], F32)
    qb_bf = sbT("qb_bf", [128, 3088], BF16)
    ones1 = sbT("ones1", [128, 1], BF16)
    of32 = sbT("of32", [1, 128], F32)
    co_sb = sbT("co_sb", [128, 4], I32)
    qs_box = {}

    class _QS:
        def __getitem__(self, k):
            return qs_box[k]
    

    def quant1(v, src, dst, n):
        qs_a = qs_box["a"]; qs_r = qs_box["r"]; qs_d = qs_box["d"]
        qs_y = qs_box["y"]; qs_m = qs_box["m"]; qs_t = qs_box["t"]
        nb = n // 16
        dv.inc(v.tensor_reduce(out=qs_a[:, 0:nb],
                               in_=src.rearrange("p (nb k) -> p nb k", k=16),
                               axis=AX.X, op=ALU.max, apply_absolute_value=True))
        v.wait_ge(dv.h, dv.n)
        v.tensor_scalar(out=qs_a[:, 0:nb], in0=qs_a[:, 0:nb], scalar1=6e-12, scalar2=None, op0=ALU.max)
        dv.inc(v.reciprocal(out=qs_r[:, 0:nb], in_=qs_a[:, 0:nb]))
        v.wait_ge(dv.h, dv.n)
        v.tensor_scalar(out=qs_r[:, 0:nb], in0=qs_r[:, 0:nb], scalar1=12.0, scalar2=None, op0=ALU.mult)
        v.tensor_scalar(out=qs_d[:, 0:nb], in0=qs_a[:, 0:nb], scalar1=1.0 / 12.0, scalar2=None, op0=ALU.mult)
        r12b = qs_r[:, 0:nb].unsqueeze(2).broadcast_to([128, nb, 16])
        dqb = qs_d[:, 0:nb].unsqueeze(2).broadcast_to([128, nb, 16])
        v.tensor_tensor(out=qs_y[:, 0:n].rearrange("p (nb k) -> p nb k", k=16),
                        in0=src.rearrange("p (nb k) -> p nb k", k=16), in1=r12b, op=ALU.mult)
        v.tensor_scalar(out=qs_m[:, 0:n].bitcast(I32), in0=qs_y[:, 0:n].bitcast(I32),
                        scalar1=0x7F800000, scalar2=None, op0=ALU.bitwise_and)
        v.tensor_scalar(out=qs_m[:, 0:n], in0=qs_m[:, 0:n], scalar1=2.0, scalar2=MAGIC,
                        op0=ALU.max, op1=ALU.mult)
        v.tensor_tensor(out=qs_t[:, 0:n], in0=qs_y[:, 0:n], in1=qs_m[:, 0:n], op=ALU.add)
        v.tensor_tensor(out=qs_t[:, 0:n], in0=qs_t[:, 0:n], in1=qs_m[:, 0:n], op=ALU.subtract)
        return dv.inc(v.tensor_tensor(out=dst.rearrange("p (nb k) -> p nb k", k=16),
                                      in0=qs_t[:, 0:n].rearrange("p (nb k) -> p nb k", k=16),
                                      in1=dqb, op=ALU.mult))

    def quant(v, src, dst, n):
        last = 0
        for o in range(0, n, 1024):
            w = min(1024, n - o)
            last = quant1(v, src[:, o:o + w], dst[:, o:o + w], w)
        return last

    def rsqrt_row(sc, dst, src):
        sc.activation(out=dst, in_=src, func=AF.Ln)
        return ac.inc(sc.activation(out=dst, in_=dst, func=AF.Exp, scale=-0.5))

    with nc.Block() as block:
        COFF = [None, None, None]

        def p0_sync(sync):
            dCO.inc(sync.dma_start(out=co_sb[:], in_=coffs), 16)
            sync.wait_ge(dCO.h, 16)
            r0 = sync.alloc_register(name="coff0")
            sync.load(r0, co_sb[0:1, 0:1])
            COFF[0] = RuntimeValue(r0, min_val=0, max_val=768)
            for d_, s_ in [(cm, cmask), (idb, idenb), (idf, idenf), (l1g, ln1g),
                           (l1b, ln1b), (l2g, ln2g), (l2b, ln2b), (ob_pp, outb),
                           (f1b_pp, fc1b), (f2b_pp, fc2b)]:
                dK.inc(sync.dma_start(out=d_[:], in_=s_), 16)
        block.sync(p0_sync)
        P0_LD = dK.fin()

        def p0_vec(v):
            v.memset(ones1[:], 1.0)
            dv.inc(v.memset(of32[:], 1.0))
        block.vector(p0_vec)


        p1x = ExitStack()
        xqfm = p1x.enter_context(nc.sbuf_tensor("xqfm", [128, 8 * TOK], BF16))
        for _k, _sh in [("y", [128, 1024]), ("m", [128, 1024]), ("t", [128, 1024]),
                        ("a", [128, 64]), ("r", [128, 64]), ("d", [128, 64])]:
            qs_box[_k] = p1x.enter_context(nc.sbuf_tensor("qs_" + _k, _sh, F32))
        qs_y = qs_box["y"]

        qbcast = {}
        for qbi in range(4):
            w = 772
            def qb_sync(sync, qbi=qbi, w=w):
                if qbi >= 1:
                    sync.wait_ge(dv.h, qbcast[qbi - 1])
                qbcast[(qbi, "ld")] = dC.rec(sync.dma_start(out=qs_y[:, 0:w].bitcast(F32),
                                      in_=qkvb[:, qbi * w:(qbi + 1) * w]), 16)
            block.sync(qb_sync)

            def qb_vec(v, qbi=qbi, w=w):
                W(v, qbcast[(qbi, "ld")])
                qbcast[qbi] = dv.inc(v.tensor_copy(out=qb_bf[:, qbi * w:(qbi + 1) * w],
                                                   in_=qs_y[:, 0:w].bitcast(F32)))
            block.vector(qb_vec)
        QB_DV = dv.n


        p1 = ExitStack()
        sb1 = lambda n_, s_, d_: p1.enter_context(nc.sbuf_tensor(n_, s_, d_))
        ps1 = lambda n_, s_, d_: p1.enter_context(nc.psum_tensor(n_, s_, d_))
        wU = sb1("wU", [128, 8 * 3072], BF16)
        wVa = sb1("wVa", [128, 3072], BF16)
        wld = [sb1(f"wld{i}", [128, 1536], F32) for i in range(2)]
        xt = [sb1(f"xt{i}", [128, D], F32) for i in range(2)]
        xnb = [sb1(f"xnb{i}", [128, D], F32) for i in range(2)]
        xnbf = [sb1(f"xnbf{i}", [128, D], BF16) for i in range(2)]
        l1c = [sb1(f"l1c{i}", [128, D], BF16) for i in range(2)]
        xqt = [sb1(f"xqt{i}", [128, D], BF16) for i in range(2)]
        ring = [sb1(f"ring{i}", [128, 8 * 512], BF16) for i in range(2)]
        xst = [sb1(f"xst{i}", [128, D], BF16) for i in range(2)]
        psb = [sb1(f"psb{i}", [128, 8 * 512], BF16) for i in range(3)]
        asm = [sb1(f"asm{i}", [128, 1040], BF16) for i in range(2)]
        rw = {k: [sb1(f"rw_{k}{i}", [128, 1], F32) for i in range(2)]
              for k in ["s1", "s2", "mu", "var", "u1", "u2", "r1", "a2", "nm1", "nm2"]}
        ptb = [ps1(f"ptb{i}", [128, 1024], BF16) for i in range(2)]
        pst1 = [ps1(f"pst1{i}", [128, 512], F32) for i in range(2)]
        pst2 = [ps1(f"pst2{i}", [128, 512], F32) for i in range(2)]

        wu_ld = [0] * 18
        wu_dv = [0] * 18
        for j in range(16):
            i, hh = j // 2, j % 2
            def wl_sync(sync, j=j, i=i, hh=hh):
                if j >= 2:
                    sync.wait_ge(dv.h, wu_dv[j - 2])
                else:
                    sync.wait_ge(dv.h, QB_DV)
                wu_ld[j] = dC.rec(sync.dma_start(
                    out=wld[j % 2][:],
                    in_=qkvU[i * 128:(i + 1) * 128, hh * 1536:(hh + 1) * 1536]), 16)
            block.sync(wl_sync)

            def wl_vec(v, j=j, i=i, hh=hh):
                W(v, wu_ld[j])
                wu_dv[j] = dv.inc(v.tensor_copy(
                    out=wU[:, i * 3072 + hh * 1536:i * 3072 + (hh + 1) * 1536],
                    in_=wld[j % 2][:]))
            block.vector(wl_vec)

        for hh in range(2):
            def wv_sync(sync, hh=hh):
                sync.wait_ge(dv.h, wu_dv[14 + hh])
                wu_ld[16 + hh] = dC.rec(sync.dma_start(
                    out=wld[hh][:, :], in_=qkvV[:, hh * 1536:(hh + 1) * 1536]), 16)
            block.sync(wv_sync)

            def wv_vec(v, hh=hh):
                W(v, wu_ld[16 + hh])
                dv.inc(v.tensor_copy(out=wVa[0:48, hh * 1536:(hh + 1) * 1536], in_=wld[hh][0:48, :]))
                dv.inc(v.tensor_copy(out=wVa[64:112, hh * 1536:(hh + 1) * 1536], in_=wld[hh][0:48, :]))
            block.vector(wv_vec)
        WV_DV = dv.n

        x_ld = [0] * NT
        stat_dv = [0] * NT
        nm_dv = [0] * NT
        rs_ac = [0] * NT
        xn_ac = [0] * NT
        q_dv = [0] * NT
        tr1_mm = [0] * NT
        tr2_mm = [0] * NT
        ev1_ac = [0] * NT
        ev1_dv = [0] * NT
        ev2_dv = [0] * NT
        s1_done_mm = [0] * 4
        s2_done_mm = [0] * 4
        pst1_ev = {}
        pst2_ev = {}
        asm_st = {}
        xsp_st = {}
        qkv_st = [0] * 4

        for i in range(NT):
            p = i % 2
            s = i // 4

            def t_sync(sync, i=i, p=p):
                if i >= 2:
                    sync.wait_ge(ac.h, xn_ac[i - 2])
                x_ld[i] = dX[p].rec(sync.dma_start(out=xt[p][:], in_=x_own[i * 128:(i + 1) * 128, :]), 16)
            block.sync(t_sync)

            def t_vec1(v, i=i, p=p):
                W(v, x_ld[i])
                if i == 0:
                    W(v, P0_LD)
                dv.inc(v.reduce_sum(out=rw["s1"][p][:], in_=xt[p][:], axis=AX.X))
                dv.inc(v.scalar_tensor_tensor(out=qs_y[:, 0:D], in0=xt[p][:], scalar=1.0,
                                              in1=xt[p][:], op0=ALU.mult, op1=ALU.mult,
                                              accum_out=rw["s2"][p][:]))
                v.wait_ge(dv.h, dv.n)
                v.tensor_scalar(out=rw["mu"][p][:], in0=rw["s1"][p][:], scalar1=1.0 / D, scalar2=None, op0=ALU.mult)
                dv.inc(v.tensor_scalar(out=rw["var"][p][:], in0=rw["s2"][p][:], scalar1=1.0 / D, scalar2=None, op0=ALU.mult))
                v.wait_ge(dv.h, dv.n)
                dv.inc(v.tensor_tensor(out=rw["u1"][p][:], in0=rw["mu"][p][:], in1=rw["mu"][p][:], op=ALU.mult))
                v.wait_ge(dv.h, dv.n)
                dv.inc(v.tensor_tensor(out=rw["var"][p][:], in0=rw["var"][p][:], in1=rw["u1"][p][:], op=ALU.subtract))
                v.wait_ge(dv.h, dv.n)
                v.tensor_scalar(out=rw["u1"][p][:], in0=rw["var"][p][:], scalar1=EPS, scalar2=None, op0=ALU.add)
                stat_dv[i] = dv.inc(v.tensor_scalar(out=rw["u2"][p][:], in0=rw["var"][p][:],
                                                    scalar1=1.0 + EPS, scalar2=EPS * EPS,
                                                    op0=ALU.mult, op1=ALU.add))
            block.vector(t_vec1)

            def t_sc1(sc, i=i, p=p):
                sc.wait_ge(dv.h, stat_dv[i])
                rsqrt_row(sc, rw["r1"][p][:], rw["u1"][p][:])
                rs_ac[i] = rsqrt_row(sc, rw["a2"][p][:], rw["u2"][p][:])
            block.scalar(t_sc1)

            def t_vec2(v, i=i, p=p):
                v.wait_ge(ac.h, rs_ac[i])
                v.scalar_tensor_tensor(out=rw["nm1"][p][:], in0=rw["mu"][p][:], scalar=-1.0,
                                       in1=rw["r1"][p][:], op0=ALU.mult, op1=ALU.mult)
                nm_dv[i] = dv.inc(v.scalar_tensor_tensor(out=rw["nm2"][p][:], in0=rw["mu"][p][:],
                                                         scalar=-1.0, in1=rw["a2"][p][:],
                                                         op0=ALU.mult, op1=ALU.mult))
            block.vector(t_vec2)

            def t_sc2(sc, i=i, p=p):
                sc.wait_ge(dv.h, nm_dv[i])
                if i >= 2:
                    sc.wait_ge(dv.h, q_dv[i - 2])
                    sc.wait_ge(mm.h, tr1_mm[i - 2])
                sc.activation(out=xnb[p][:], in_=xt[p][:], func=AF.Identity,
                              scale=rw["r1"][p][:], bias=rw["nm1"][p][:])
                sc.activation(out=xnbf[p][:], in_=xt[p][:], func=AF.Identity,
                              scale=rw["r1"][p][:], bias=rw["nm1"][p][:])
                xn_ac[i] = ac.inc(sc.activation(out=l1c[p][:], in_=xt[p][:], func=AF.Identity,
                                                scale=rw["a2"][p][:], bias=rw["nm2"][p][:]))
            block.scalar(t_sc2)

            def t_vec3(v, i=i, p=p):
                v.wait_ge(ac.h, xn_ac[i])
                if i >= 2:
                    v.wait_ge(mm.h, tr2_mm[i - 2])
                q_dv[i] = quant(v, xnb[p][:], xqt[p][:], D)
            block.vector(t_vec3)

            def t_mm1(t, i=i, p=p):
                t.wait_ge(ac.h, xn_ac[i])
                if i >= 1:
                    t.wait_ge(dv.h, ev2_dv[i - 1])
                    t.wait_ge(dv.h, ev1_dv[i - 1])
                for c in range(8):
                    t.transpose(ptb[0][:, c * 128:(c + 1) * 128],
                                in_=l1c[p][:, c * 128:(c + 1) * 128], identity=idb[:])
                for c in range(8):
                    ins = t.transpose(ptb[1][:, c * 128:(c + 1) * 128],
                                      in_=xnbf[p][:, c * 128:(c + 1) * 128], identity=idb[:])
                tr1_mm[i] = mm.inc(ins)
            block.tensor(t_mm1)

            def t_sc3(sc, i=i, s=s):
                sc.wait_ge(mm.h, tr1_mm[i])
                if s >= 2 and i % 4 == 0:
                    sc.wait_ge(mm.h, s1_done_mm[s - 2])
                slot = i % 4
                for c in range(8):
                    ins = sc.activation(
                        out=ring[s % 2][:, c * 512 + slot * 128: c * 512 + (slot + 1) * 128],
                        in_=ptb[0][:, c * 128:(c + 1) * 128], func=AF.Identity,
                        scale=l1g[:, c:c + 1], bias=l1b[:, c:c + 1])
                ev1_ac[i] = ac.inc(ins)
            block.scalar(t_sc3)

            def t_vec4(v, i=i, p=p):
                v.wait_ge(mm.h, tr1_mm[i])
                if i >= 2:
                    W(v, xsp_st[i - 2])
                ev1_dv[i] = dv.inc(v.tensor_copy(out=xst[p][:], in_=ptb[1][:]))
            block.vector(t_vec4)

            def t_mm2(t, i=i, p=p):
                t.wait_ge(dv.h, q_dv[i])
                t.wait_ge(ac.h, ev1_ac[i])
                for c in range(8):
                    ins = t.transpose(ptb[0][:, c * 128:(c + 1) * 128],
                                      in_=xqt[p][:, c * 128:(c + 1) * 128], identity=idb[:])
                tr2_mm[i] = mm.inc(ins)
            block.tensor(t_mm2)

            def t_vec5(v, i=i):
                v.wait_ge(mm.h, tr2_mm[i])
                ev2_dv[i] = dv.inc(v.tensor_copy(
                    out=xqfm[:].rearrange("p (c t) -> p c t", t=TOK)[:, :, i * 128:(i + 1) * 128],
                    in_=ptb[0][:].rearrange("p (c t) -> p c t", t=128)))
            block.vector(t_vec5)

            def t_sync2(sync, i=i, p=p):
                sync.wait_ge(dv.h, ev1_dv[i])
                xsp_st[i] = sX.rec(sync.dma_start(
                    out=xn_sp.rearrange("p (c t) -> p c t", t=TOK)[:, :, i * 128:(i + 1) * 128],
                    in_=xst[p][:].rearrange("p (c t) -> p c t", t=128)), 16)
            block.sync(t_sync2)

            if i % 4 == 3:
                s1mm = {}

                def sl_pre(t, s=s, i=i):
                    t.wait_ge(ac.h, ev1_ac[i])
                    if s == 0:
                        t.wait_ge(dv.h, WV_DV)
                        t.wait_ge(dv.h, QB_DV)
                    if s >= 1:
                        t.wait_ge(mm.h, s2_done_mm[s - 1])
                block.tensor(sl_pre)

                for prj in range(3):
                    for ccx in range(8):
                        def sl_mm(t, s=s, prj=prj, ccx=ccx):
                            gk = s * 24 + prj * 8 + ccx
                            if gk >= 2:
                                t.wait_ge(dv.h, pst1_ev.get(gk - 2, 0))
                            for dch in range(8):
                                base = dch * 3072 + prj * 1024 + ccx * 128
                                ins = t.matmul(pst1[gk % 2][:],
                                               lhsT=wU[:, base:base + 128],
                                               rhs=ring[s % 2][:, dch * 512:(dch + 1) * 512],
                                               start=(dch == 0), stop=(dch == 7))
                            s1mm[(prj, ccx)] = mm.inc(ins)
                            if prj == 2 and ccx == 7:
                                s1_done_mm[s] = mm.n
                        block.tensor(sl_mm)

                        def sl_ev(v, s=s, prj=prj, ccx=ccx):
                            gk = s * 24 + prj * 8 + ccx
                            v.wait_ge(mm.h, s1mm[(prj, ccx)])
                            v.tensor_copy(out=psb[prj][0:48, ccx * 512:(ccx + 1) * 512],
                                          in_=pst1[gk % 2][0:48, :])
                            pst1_ev[gk] = dv.inc(
                                v.tensor_copy(out=psb[prj][64:112, ccx * 512:(ccx + 1) * 512],
                                              in_=pst1[gk % 2][64:112, :]))
                        block.vector(sl_ev)

                S1EV = dv.n
                s2mm = {}

                def s2_pre(t, s=s):
                    t.wait_ge(dv.h, S1EV)
                block.tensor(s2_pre)

                for tt in range(4):
                    a = tt % 2
                    for prj in range(3):
                        def s2_mm(t, s=s, tt=tt, prj=prj):
                            gk = s * 12 + tt * 3 + prj
                            if gk >= 1:
                                t.wait_ge(dv.h, pst2_ev.get(gk - 1, 0))
                            for h in [0, 2, 4, 6, 8, 10, 12, 14, -1, 1, 3, 5, 7, 9, 11, 13, 15]:
                                if h < 0:
                                    t.drain()
                                    continue
                                base = (h // 2) * 512 + tt * 128
                                ins = t.matmul(
                                    pst2[h // 8][:, (h % 8) * 64:(h % 8 + 1) * 64],
                                    lhsT=psb[prj][(h % 2) * 64:(h % 2) * 64 + 48, base:base + 128],
                                    rhs=wVa[(h % 2) * 64:(h % 2) * 64 + 48,
                                            prj * 1024 + h * 64:prj * 1024 + (h + 1) * 64],
                                    start=True, stop=True)
                            t.drain()
                            s2mm[(tt, prj)] = mm.inc(ins)
                            if tt == 3 and prj == 2:
                                s2_done_mm[s] = mm.n
                        block.tensor(s2_mm)

                        def s2_ev(v, s=s, tt=tt, prj=prj, a=a):
                            gk = s * 12 + tt * 3 + prj
                            v.wait_ge(mm.h, s2mm[(tt, prj)])
                            gabs = s * 4 + tt
                            prev_user = (gabs, prj - 1) if prj >= 1 else (gabs - 2, 2)
                            if prev_user in asm_st:
                                W(v, asm_st[prev_user])
                            if prj < 2:
                                v.tensor_tensor(out=asm[a][:, 0:512], in0=pst2[0][:],
                                                in1=qb_bf[:, prj * 1024:prj * 1024 + 512], op=ALU.add)
                                ins = v.tensor_tensor(out=asm[a][:, 512:1024], in0=pst2[1][:],
                                                      in1=qb_bf[:, prj * 1024 + 512:prj * 1024 + 1024], op=ALU.add)
                            else:
                                v.tensor_copy(out=asm[a][:, 0:1040], in_=qb_bf[:, 2048:3088])
                                av = asm[a][:].rearrange("p (h c) -> p h c", c=65)
                                v.tensor_tensor(out=av[:, 0:8, 0:64], in0=pst2[0][:].rearrange("p (h d) -> p h d", d=64),
                                                in1=av[:, 0:8, 0:64], op=ALU.add)
                                ins = v.tensor_tensor(out=av[:, 8:16, 0:64], in0=pst2[1][:].rearrange("p (h d) -> p h d", d=64),
                                                      in1=av[:, 8:16, 0:64], op=ALU.add)
                            pst2_ev[gk] = dv.inc(ins)
                        block.vector(s2_ev)

                        def s2_st1(sync, s=s, tt=tt, prj=prj, a=a):
                            gk = s * 12 + tt * 3 + prj
                            sync.wait_ge(dv.h, pst2_ev[gk])
                            row0 = s * 512 + tt * 128
                            width = 1024 if prj < 2 else 1040
                            asm_st[(s * 4 + tt, prj)] = sA[a].rec(sync.dma_start(
                                out=qkv_loc[row0:row0 + 128, prj * 1024:prj * 1024 + width],
                                in_=asm[a][:, 0:width]), 16)
                        block.sync(s2_st1)

        COFFG = [None, None, None]

        def coll1(g):
            W(g, sA[0].fin())
            W(g, sA[1].fin())
            g.wait_ge(dCO.h, 16)
            r1 = g.alloc_register(name="coffg1")
            g.load(r1, co_sb[0:1, 1:2])
            COFFG[1] = RuntimeValue(r1, min_val=0, max_val=780)
            r2 = g.alloc_register(name="coffg2")
            g.load(r2, co_sb[0:1, 2:3])
            COFFG[2] = RuntimeValue(r2, min_val=0, max_val=6144)
            cc.inc(g.collective_compute("AllGather", ALU.bypass, replica_groups=rg,
                                        ins=[qkv_loc.opt()], outs=[qkv_g.opt()]))
        block.gpsimd(coll1)
        CC1 = cc.n

        p1.close()

        p15 = ExitStack()
        sb5 = lambda n_, s_, d_: p15.enter_context(nc.sbuf_tensor(n_, s_, d_))
        ps5 = lambda n_, s_, d_: p15.enter_context(nc.psum_tensor(n_, s_, d_))
        wfl = [sb5(f"wfl{i}", [128, NFFN], F32) for i in range(2)]
        wqs = sb5("wqs", [128, D], BF16)
        wds = sb5("wds", [128, 512], BF16)
        wqT = [sb5(f"wqT{i}", [128, 8 * 512], BF16) for i in range(2)]
        wdT = sb5("wdT", [128, 4 * 1024], BF16)
        hqn = sb5("hqn", [128, 4 * TOK], BF16)
        sil = sb5("sil", [128, 512], F32)
        hmt = sb5("hmt", [128, 512], F32)
        hqt = sb5("hqt", [128, 512], BF16)
        facc = sb5("facc", [128, 8 * TOK], F32)
        ptq = ps5("ptq", [128, 1024], BF16)
        pg = [ps5(f"pg{i}", [128, 512], F32) for i in range(2)]
        pu = [ps5(f"pu{i}", [128, 512], F32) for i in range(2)]
        pdn = [ps5(f"pdn{i}", [128, 512], F32) for i in range(2)]

        wjobs = []
        for nbi, (noff, nsz) in enumerate(NB_LIST):
            nch = nsz // 128
            for r_ in range(nch):
                wjobs.append(("g", nbi, r_, noff, nsz))
            for r_ in range(nch):
                wjobs.append(("u", nbi, r_, noff, nsz))
            for r_ in range(8):
                wjobs.append(("d", nbi, r_, noff, nsz))
        wj_ld = {}
        wj_q = {}
        wj_tr = {}
        wj_ev = {}
        ptq_free = [0]
        gu_done_mm = {}
        dn_done_mm = {}
        gu_mm = {}
        sil_ac = {}
        mul_dv = {}
        hq_tr = {}
        hq_ev = {}
        dn_mm = {}
        dn_ev = {}

        def emit_wjob(k):
            kind, nbi, r_, noff, nsz = wjobs[k]
            nch = nsz // 128

            def wj_sync(sync, k=k, kind=kind, nbi=nbi, r_=r_, noff=noff, nsz=nsz):
                if k < 2:
                    sync.wait_ge(mm.h, s2_done_mm[3])
                    W(sync, sA[0].fin())
                    W(sync, sA[1].fin())
                    W(sync, sX.fin())
                if k >= 2:
                    sync.wait_ge(dv.h, wj_q[k - 2])
                if kind == "g":
                    src = wgwu[noff + r_ * 128:noff + (r_ + 1) * 128, :]
                    w = D
                elif kind == "u":
                    src = wgwu[NFFN + noff + r_ * 128:NFFN + noff + (r_ + 1) * 128, :]
                    w = D
                else:
                    src = wdn[r_ * 128:(r_ + 1) * 128, noff:noff + nsz]
                    w = nsz
                wj_ld[k] = dW[k % 2].rec(sync.dma_start(out=wfl[k % 2][:, 0:w], in_=src), 16)
            block.sync(wj_sync)

            def wj_v(v, k=k, kind=kind, nsz=nsz):
                W(v, wj_ld[k])
                if k >= 1:
                    v.wait_ge(mm.h, wj_tr[k - 1])
                if kind == "d":
                    wj_q[k] = quant(v, wfl[k % 2][:, 0:nsz], wds[:, 0:nsz], nsz)
                else:
                    wj_q[k] = quant(v, wfl[k % 2][:, 0:D], wqs[:], D)
            block.vector(wj_v)

            def wj_t(t, k=k, kind=kind, nsz=nsz):
                t.wait_ge(dv.h, wj_q[k])
                t.wait_ge(dv.h, ptq_free[0])
                nblk = (nsz // 128) if kind == "d" else 8
                srcb = wds if kind == "d" else wqs
                for b_ in range(nblk):
                    ins = t.transpose(ptq[:, b_ * 128:(b_ + 1) * 128],
                                      in_=srcb[:, b_ * 128:(b_ + 1) * 128], identity=idb[:])
                wj_tr[k] = mm.inc(ins)
            block.tensor(wj_t)

            def wj_e(v, k=k, kind=kind, nbi=nbi, r_=r_, nsz=nsz):
                v.wait_ge(mm.h, wj_tr[k])
                if nbi >= 1:
                    if kind in ("g", "u"):
                        v.wait_ge(mm.h, gu_done_mm[nbi - 1])
                    else:
                        v.wait_ge(mm.h, dn_done_mm[nbi - 1])
                nblk = (nsz // 128) if kind == "d" else 8
                for b_ in range(nblk):
                    if kind == "d":
                        dst = wdT[:, b_ * 1024 + r_ * 128:b_ * 1024 + (r_ + 1) * 128]
                    else:
                        wi = 0 if kind == "g" else 1
                        dst = wqT[wi][:, b_ * nsz + r_ * 128:b_ * nsz + (r_ + 1) * 128]
                    ins = v.tensor_copy(out=dst, in_=ptq[:, b_ * 128:(b_ + 1) * 128])
                wj_ev[k] = dv.inc(ins)
                ptq_free[0] = wj_ev[k]
            block.vector(wj_e)

        kbase = 0
        for nbi, (noff, nsz) in enumerate(NB_LIST):
            nch = nsz // 128
            njobs = 2 * nch + 8
            for k in range(kbase, kbase + njobs):
                emit_wjob(k)
            WJ_EV_NB = dv.n

            for tt in range(NT):
                def g_mm(t, nbi=nbi, nsz=nsz, tt=tt):
                    if tt == 0:
                        t.wait_ge(dv.h, WJ_EV_NB)
                        if nbi == 0:
                            t.wait_ge(dv.h, ev2_dv[NT - 1])
                    pp = tt % 2
                    if tt >= 2:
                        t.wait_ge(ac.h, sil_ac[(nbi, tt - 2)])
                        t.wait_ge(dv.h, mul_dv[(nbi, tt - 2)])
                    for kc in range(8):
                        t.matmul(pg[pp][:, 0:nsz],
                                 lhsT=xqfm[:].rearrange("p (c t) -> p c t", t=TOK)[:, kc, tt * 128:(tt + 1) * 128],
                                 rhs=wqT[0][:, kc * nsz:(kc + 1) * nsz],
                                 start=(kc == 0), stop=(kc == 7))
                    for kc in range(8):
                        ins = t.matmul(pu[pp][:, 0:nsz],
                                       lhsT=xqfm[:].rearrange("p (c t) -> p c t", t=TOK)[:, kc, tt * 128:(tt + 1) * 128],
                                       rhs=wqT[1][:, kc * nsz:(kc + 1) * nsz],
                                       start=(kc == 0), stop=(kc == 7))
                    gu_mm[(nbi, tt)] = mm.inc(ins)
                    if tt == NT - 1:
                        gu_done_mm[nbi] = mm.n
                block.tensor(g_mm)

                def g_sc(sc, nbi=nbi, tt=tt, nsz=nsz):
                    sc.wait_ge(mm.h, gu_mm[(nbi, tt)])
                    if tt >= 1:
                        sc.wait_ge(dv.h, mul_dv[(nbi, tt - 1)])
                    sil_ac[(nbi, tt)] = ac.inc(
                        sc.activation(out=sil[:, 0:nsz], in_=pg[tt % 2][:, 0:nsz], func=AF.Silu))
                block.scalar(g_sc)

                def g_v(v, nbi=nbi, tt=tt, nsz=nsz):
                    v.wait_ge(ac.h, sil_ac[(nbi, tt)])
                    if tt >= 1:
                        v.wait_ge(mm.h, hq_tr[(nbi, tt - 1)])
                    dv.inc(v.tensor_tensor(out=hmt[:, 0:nsz], in0=sil[:, 0:nsz],
                                           in1=pu[tt % 2][:, 0:nsz], op=ALU.mult))
                    v.wait_ge(dv.h, dv.n)
                    mul_dv[(nbi, tt)] = quant(v, hmt[:, 0:nsz], hqt[:, 0:nsz], nsz)
                block.vector(g_v)

                def h_t(t, nbi=nbi, tt=tt, nsz=nsz):
                    t.wait_ge(dv.h, mul_dv[(nbi, tt)])
                    t.wait_ge(dv.h, ptq_free[0])
                    for c in range(nsz // 128):
                        ins = t.transpose(ptq[:, c * 128:(c + 1) * 128],
                                          in_=hqt[:, c * 128:(c + 1) * 128], identity=idb[:])
                    hq_tr[(nbi, tt)] = mm.inc(ins)
                block.tensor(h_t)

                def h_e(v, nbi=nbi, tt=tt, nsz=nsz):
                    v.wait_ge(mm.h, hq_tr[(nbi, tt)])
                    if nbi >= 1 and tt == 0:
                        v.wait_ge(mm.h, dn_done_mm[nbi - 1])
                    ins = v.tensor_copy(
                        out=hqn[:, 0:(nsz // 128) * TOK].rearrange("p (c t) -> p c t", t=TOK)[:, :, tt * 128:(tt + 1) * 128],
                        in_=ptq[:, 0:nsz].rearrange("p (c t) -> p c t", t=128))
                    hq_ev[(nbi, tt)] = dv.inc(ins)
                    ptq_free[0] = hq_ev[(nbi, tt)]
                block.vector(h_e)

            def d_pre(t, nbi=nbi):
                t.wait_ge(dv.h, hq_ev[(nbi, NT - 1)])
            block.tensor(d_pre)

            for tc in range(4):
                for dc in range(8):
                    def d_mm(t, nbi=nbi, nsz=nsz, nch=nch, tc=tc, dc=dc):
                        gk = nbi * 32 + tc * 8 + dc
                        if gk >= 2:
                            t.wait_ge(dv.h, dn_ev.get(gk - 2, 0))
                        for c in range(nch):
                            ins = t.matmul(
                                pdn[gk % 2][:],
                                lhsT=wdT[:, c * 1024 + dc * 128:c * 1024 + (dc + 1) * 128],
                                rhs=hqn[:, 0:nch * TOK].rearrange("p (cc t) -> p cc t", t=TOK)[:, c, tc * 512:(tc + 1) * 512],
                                start=(c == 0), stop=(c == nch - 1))
                        dn_mm[gk] = mm.inc(ins)
                        if tc == 3 and dc == 7:
                            dn_done_mm[nbi] = mm.n
                    block.tensor(d_mm)

                    def d_ev(v, nbi=nbi, tc=tc, dc=dc):
                        gk = nbi * 32 + tc * 8 + dc
                        v.wait_ge(mm.h, dn_mm[gk])
                        dst = facc[:].rearrange("p (c t) -> p c t", t=TOK)[:, dc, tc * 512:(tc + 1) * 512]
                        if nbi == 0:
                            ins = v.tensor_copy(out=dst, in_=pdn[gk % 2][:])
                        else:
                            ins = v.tensor_tensor(out=dst, in0=dst, in1=pdn[gk % 2][:], op=ALU.add)
                        dn_ev[gk] = dv.inc(ins)
                    block.vector(d_ev)
            kbase += njobs

        FFN_CAST = [0]

        def f_cast2(v):
            v.wait_ge(mm.h, dn_done_mm[len(NB_LIST) - 1])
            for c in range(4):
                ins = v.tensor_copy(out=hqn[:, c * TOK:(c + 1) * TOK],
                                    in_=facc[:, c * TOK:(c + 1) * TOK])
            FFN_CAST[0] = dv.inc(ins)
        block.vector(f_cast2)

        def f_st1(sync):
            sync.wait_ge(dv.h, FFN_CAST[0])
            sF.inc(sync.dma_start(out=ffn_sp[:, 0:4 * TOK], in_=hqn[:, 0:4 * TOK]), 16)
        block.sync(f_st1)
        FFN_ST1 = sF.fin()

        def f_cast3(v):
            W(v, FFN_ST1)
            for c in range(4):
                ins = v.tensor_copy(out=hqn[:, c * TOK:(c + 1) * TOK],
                                    in_=facc[:, (4 + c) * TOK:(5 + c) * TOK])
            FFN_CAST[0] = dv.inc(ins)
        block.vector(f_cast3)

        def f_st2(sync):
            sync.wait_ge(dv.h, FFN_CAST[0])
            sF.inc(sync.dma_start(out=ffn_sp[:, 4 * TOK:8 * TOK], in_=hqn[:, 0:4 * TOK]), 16)
        block.sync(f_st2)
        FFN_ST = sF.fin()

        P15_DV_END = dv.n
        P15_MM_END = mm.n

        p15.close()
        p1x.close()

        p2 = ExitStack()
        sb2 = lambda n_, s_, d_: p2.enter_context(nc.sbuf_tensor(n_, s_, d_))
        ps2 = lambda n_, s_, d_: p2.enter_context(nc.psum_tensor(n_, s_, d_))
        Qfm = [[sb2(f"Qfm{u}_{l}", [64, S], BF16) for l in range(2)] for u in range(2)]
        Kfm = [[sb2(f"Kfm{u}_{l}", [64, S], BF16) for l in range(2)] for u in range(2)]
        VT = [sb2(f"VT{u}", [128, 64 * 130], BF16) for u in range(2)]
        Et = [sb2(f"Et{i}", [128, 1024], BF16) for i in range(2)]
        rrow = sb2("rrow", [1, 1024], F32)
        rb = sb2("rb", [64, 1024], F32)
        cta = sb2("cta", [128, 512], BF16)

        unit_ld = [0, 0]

        p2p = ExitStack()
        qkst = [p2p.enter_context(nc.sbuf_tensor("qkst0", [128, 4096], BF16))]
        ptk = [p2p.enter_context(nc.psum_tensor(f"ptk{i}", [128, 1024], BF16)) for i in range(2)]
        qk_ld = {}
        qk_tr = {}
        qk_ev = {}
        gjob = 0
        for u in range(2):
            for ten in range(2):
                for hf in range(2):
                    def qk_sync(sync, u=u, ten=ten, hf=hf, gjob=gjob):
                        cbase = ten * 1024 + u * 128
                        if gjob >= 1:
                            sync.wait_ge(dv.h, qk_ev.get((gjob - 1, 3), 0))
                        if gjob == 0:
                            sync.wait_ge(cc.h, CC1)
                            sync.wait_ge(dv.h, P15_DV_END)
                            sync.wait_ge(mm.h, P15_MM_END)
                            W(sync, sF.fin())
                        qk_ld[gjob] = dQ[gjob % 2].rec(sync.dma_start(
                            out=qkst[0][:].rearrange("p (a c) -> p a c", c=128),
                            in_=qkv_g.rearrange("(a p) c -> p a c", p=128)[:, hf * 32:(hf + 1) * 32][:, :, cbase:cbase + 896 + 128][:, :, ds(COFF[0], 128)]), 16)
                    block.sync(qk_sync)

                    for rd in range(4):
                        def qk_t(t, gjob=gjob, rd=rd):
                            W(t, qk_ld[gjob])
                            prev = (gjob, rd - 2) if rd >= 2 else ((gjob - 1, rd + 2) if gjob else None)
                            if prev is not None:
                                t.wait_ge(dv.h, qk_ev.get(prev, 0))
                            for c in range(8):
                                ins = t.transpose(ptk[rd % 2][:, c * 128:(c + 1) * 128],
                                                  in_=qkst[0][:, rd * 1024 + c * 128:rd * 1024 + (c + 1) * 128],
                                                  identity=idb[:])
                            qk_tr[(gjob, rd)] = mm.inc(ins)
                        block.tensor(qk_t)

                        def qk_e(v, u=u, ten=ten, hf=hf, gjob=gjob, rd=rd):
                            v.wait_ge(mm.h, qk_tr[(gjob, rd)])
                            dst = (Qfm if ten == 0 else Kfm)[u]
                            a0 = hf * 4 + rd
                            v.tensor_copy(out=dst[0][:, a0 * 1024:(a0 + 1) * 1024],
                                          in_=ptk[rd % 2][0:64, :])
                            qk_ev[(gjob, rd)] = dv.inc(v.tensor_copy(
                                out=dst[1][:, a0 * 1024:(a0 + 1) * 1024], in_=ptk[rd % 2][64:128, :]))
                        block.vector(qk_e)
                    gjob += 1
        QK_DV = dv.n

        def p2_vld(g):
            g.wait_ge(dv.h, P15_DV_END)
            g.wait_ge(mm.h, P15_MM_END)
            for u in range(2):
                vbase = VCOLS + u * 130
                for hf in range(2):
                    dV[u].inc(g.dma_start(
                        out=VT[u][:].rearrange("p (kt c) -> p kt c", c=130)[:, hf * 32:(hf + 1) * 32, :],
                        in_=qkv_g.rearrange("(kt p) c -> p kt c", p=128)[:, hf * 32:(hf + 1) * 32][:, :, vbase:vbase + 780 + 130][:, :, ds(COFFG[1], 130)]), 16)
                unit_ld[u] = dV[u].fin()
        block.gpsimd(p2_vld)
        p2p.close()

        pS = [ps2(f"pS{i}", [128, 1024], F32) for i in range(2)]
        pc0 = ps2("pc0", [65, 512], F32)
        pc1 = ps2("pc1", [65, 512], F32)
        pbc = ps2("pbc", [64, 1024], F32)

        exp_rec = {}
        scm_rec = {}
        avm_rec = {}
        msk_rec = {}
        ctx_free = [dv.n]
        cta_st = [None]
        gi = 0
        for u in range(2):
            for qc in range(16):
                nkt = 4 * qc + 4
                gqc0 = gi
                for kt in range(nkt):
                    def a_mm(t, u=u, qc=qc, kt=kt, nkt=nkt, gi=gi):
                        if gi == 0:
                            W(t, unit_ld[0])
                            t.wait_ge(dv.h, QK_DV)
                        if u == 1 and qc == 0 and kt == 0:
                            W(t, unit_ld[1])
                        if gi >= 2:
                            t.wait_ge(ac.h, exp_rec[gi - 2])
                        t.matmul(pS[gi % 2][:, 0:512],
                                 lhsT=Kfm[u][0][:, kt * 128:(kt + 1) * 128],
                                 rhs=Qfm[u][0][:, qc * 512:(qc + 1) * 512], start=True, stop=True)
                        ins = t.matmul(pS[gi % 2][:, 512:1024],
                                       lhsT=Kfm[u][1][:, kt * 128:(kt + 1) * 128],
                                       rhs=Qfm[u][1][:, qc * 512:(qc + 1) * 512], start=True, stop=True)
                        scm_rec[gi] = mm.inc(ins)
                        if kt >= 1:
                            gip = gi - 1
                            t.wait_ge(ac.h, exp_rec[gip])
                            if kt - 1 == 0:
                                t.wait_ge(dv.h, ctx_free[0])
                            t.matmul(pc0[:], lhsT=VT[u][:, (kt - 1) * 130:(kt - 1) * 130 + 65],
                                     rhs=Et[gip % 2][:, 0:512], start=(kt - 1 == 0), stop=(kt - 1 == nkt - 1))
                            ins = t.matmul(pc1[:], lhsT=VT[u][:, (kt - 1) * 130 + 65:(kt - 1) * 130 + 130],
                                           rhs=Et[gip % 2][:, 512:1024], start=(kt - 1 == 0), stop=(kt - 1 == nkt - 1))
                            avm_rec[gip] = mm.inc(ins)
                    block.tensor(a_mm)

                    if kt >= 4 * qc:
                        def a_msk(v, qc=qc, kt=kt, gi=gi):
                            v.wait_ge(mm.h, scm_rec[gi])
                            mloc = kt - 4 * qc
                            mview = cm[:, mloc * 512:(mloc + 1) * 512].unsqueeze(1).broadcast_to([128, 2, 512])
                            msk_rec[gi] = dv.inc(v.tensor_tensor(
                                out=pS[gi % 2][:].rearrange("p (b q) -> p b q", q=512),
                                in0=pS[gi % 2][:].rearrange("p (b q) -> p b q", q=512),
                                in1=mview, op=ALU.add))
                        block.vector(a_msk)

                    def a_exp(sc, gi=gi, diag=(kt >= 4 * qc)):
                        sc.wait_ge(mm.h, scm_rec[gi])
                        if diag:
                            sc.wait_ge(dv.h, msk_rec[gi])
                        if gi >= 2:
                            sc.wait_ge(mm.h, avm_rec[gi - 2])
                        exp_rec[gi] = ac.inc(sc.activation(out=Et[gi % 2][:], in_=pS[gi % 2][:],
                                                           func=AF.Exp, scale=0.125))
                    block.scalar(a_exp)
                    gi += 1

                def a_avl(t, u=u, qc=qc, nkt=nkt, gi=gi):
                    gip = gi - 1
                    t.wait_ge(ac.h, exp_rec[gip])
                    if nkt == 1:
                        t.wait_ge(dv.h, ctx_free[0])
                    kt = nkt - 1
                    t.matmul(pc0[:], lhsT=VT[u][:, kt * 130:kt * 130 + 65],
                             rhs=Et[gip % 2][:, 0:512], start=(kt == 0), stop=True)
                    ins = t.matmul(pc1[:], lhsT=VT[u][:, kt * 130 + 65:kt * 130 + 130],
                                   rhs=Et[gip % 2][:, 512:1024], start=(kt == 0), stop=True)
                    avm_rec[gip] = mm.inc(ins)
                block.tensor(a_avl)

                rr_dv = [0]

                def a_rr(v, gi=gi):
                    v.wait_ge(mm.h, avm_rec[gi - 1])
                    dv.inc(v.reciprocal(out=rrow[0:1, 0:512], in_=pc0[64:65, :]))
                    dv.inc(v.reciprocal(out=rrow[0:1, 512:1024], in_=pc1[64:65, :]))
                    v.wait_ge(dv.h, dv.n)
                    rr_dv[0] = dv.n
                block.vector(a_rr)

                bc_mm = [0]

                def a_bc(t):
                    t.wait_ge(dv.h, rr_dv[0])
                    t.matmul(pbc[:, 0:512], lhsT=of32[0:1, 0:64], rhs=rrow[0:1, 0:512],
                             start=True, stop=True)
                    ins = t.matmul(pbc[:, 512:1024], lhsT=of32[0:1, 0:64], rhs=rrow[0:1, 512:1024],
                                   start=True, stop=True)
                    bc_mm[0] = mm.inc(ins)
                block.tensor(a_bc)

                def a_nrm(v, u=u, qc=qc):
                    v.wait_ge(mm.h, bc_mm[0])
                    if cta_st[0] is not None:
                        W(v, cta_st[0])
                    dv.inc(v.tensor_copy(out=rb[:], in_=pbc[:]))
                    v.wait_ge(dv.h, dv.n)
                    v.tensor_tensor(out=cta[0:64, :], in0=pc0[0:64, :], in1=rb[:, 0:512], op=ALU.mult)
                    ctx_free[0] = dv.inc(v.tensor_tensor(out=cta[64:128, :], in0=pc1[0:64, :],
                                                         in1=rb[:, 512:1024], op=ALU.mult))
                block.vector(a_nrm)

                def a_st(sync, u=u, qc=qc):
                    sync.wait_ge(dv.h, ctx_free[0])
                    cta_st[0] = sC.rec(sync.dma_start(out=ctx_loc[u * 128:(u + 1) * 128, qc * 512:(qc + 1) * 512],
                                          in_=cta[:]), 16)
                block.sync(a_st)

        def coll2(g):
            W(g, sC.fin())
            cc.inc(g.collective_compute("AllGather", ALU.bypass, replica_groups=rg,
                                        ins=[ctx_loc.opt()], outs=[ctx_g.opt()]))
        block.gpsimd(coll2)
        CC2 = cc.n

        p2.close()

        p3 = ExitStack()
        sb3 = lambda n_, s_, d_: p3.enter_context(nc.sbuf_tensor(n_, s_, d_))
        hbf = sb3("hbf", [128, 8 * TOK], BF16)
        wld3 = [sb3(f"wld3_{i}", [128, 1024], F32) for i in range(2)]
        a3r = sb3("a3r", [1, 2048], F32)
        c3r = sb3("c3r", [1, 2048], F32)
        pA = [p3.enter_context(nc.psum_tensor(f"pA{i}", [128, 512], F32)) for i in range(2)]
        pB = [p3.enter_context(nc.psum_tensor(f"pB{i}", [128, 512], F32)) for i in range(2)]

        p3a = ExitStack()
        sa = lambda n_, s_, d_: p3a.enter_context(nc.sbuf_tensor(n_, s_, d_))
        xnr = sa("xnr", [128, 8 * TOK], BF16)
        ctx_t = sa("ctx_t", [128, 8 * 512], BF16)
        oU = sa("oU", [128, 8 * 768], BF16)
        oV = sa("oV", [128, 6 * 1024], BF16)
        t1s = sa("t1s", [128, 6 * 512], BF16)
        ao = sa("ao", [128, 512], F32)

        XNR_LD = [0]

        def p3a_xn(sync):
            sync.wait_ge(cc.h, CC2)
            W(sync, sX.fin())
            XNR_LD[0] = dXN.rec(sync.dma_start(out=xnr[:], in_=xn_sp[:, :]), 16)
        block.sync(p3a_xn)

        ocast = {}

        def emit_wcast(i, src_ap, w, dst_ap):
            def ow_sync(sync, i=i):
                if i >= 2:
                    sync.wait_ge(dv.h, ocast[i - 2])
                ocast[(i, "ld")] = dW3.rec(sync.dma_start(out=wld3[i % 2][:, 0:w], in_=src_ap), 16)
            block.sync(ow_sync)

            def ow_vec(v, i=i):
                W(v, ocast[(i, "ld")])
                ocast[i] = dv.inc(v.tensor_copy(out=dst_ap, in_=wld3[i % 2][:, 0:w]))
            block.vector(ow_vec)

        def p3a_gate(sync):
            sync.wait_ge(cc.h, CC2)
        block.sync(p3a_gate)

        wi = 0
        for i in range(8):
            emit_wcast(wi, outU[i * 128:(i + 1) * 128, :], 768, oU[:, i * 768:(i + 1) * 768])
            wi += 1
        for i in range(6):
            emit_wcast(wi, outV[i * 128:(i + 1) * 128, :], 1024, oV[:, i * 1024:(i + 1) * 1024])
            wi += 1
        OW_DV = dv.n

        ctx_ld = [0] * 4
        t1_mm = {}
        t1_ev = {}
        t2_mm = {}
        h_dv = {}
        HSC = {}

        for tc in range(4):
            def c_sync(g, tc=tc):
                if tc == 0:
                    g.wait_ge(cc.h, CC2)
                if tc >= 2:
                    g.wait_ge(mm.h, t1_mm.get((tc - 2, 5), 0))
                ctx_ld[tc] = dT[tc % 2].rec(g.dma_start(
                    out=ctx_t[:].rearrange("p (dch t) -> p dch t", t=512),
                    in_=ctx_g.rearrange("(dch p) t -> p dch t", p=128)[:, :, tc * 512:tc * 512 + 6144 + 512][:, :, ds(COFFG[2], 512)]), 16)
            block.gpsimd(c_sync)

            def t1_pre(t, tc=tc):
                W(t, ctx_ld[tc])
                if tc == 0:
                    t.wait_ge(dv.h, OW_DV)
            block.tensor(t1_pre)

            for cc6 in range(6):
                def t1_f(t, tc=tc, cc6=cc6):
                    gk = tc * 6 + cc6
                    if gk >= 2:
                        t.wait_ge(dv.h, t1_ev.get(gk - 2, 0))
                    for dch in range(8):
                        ins = t.matmul(pA[gk % 2][:],
                                       lhsT=oU[:, dch * 768 + cc6 * 128:dch * 768 + (cc6 + 1) * 128],
                                       rhs=ctx_t[:, dch * 512:(dch + 1) * 512],
                                       start=(dch == 0), stop=(dch == 7))
                    t1_mm[(tc, cc6)] = mm.inc(ins)
                block.tensor(t1_f)

                def t1_e(v, tc=tc, cc6=cc6):
                    gk = tc * 6 + cc6
                    v.wait_ge(mm.h, t1_mm[(tc, cc6)])
                    t1_ev[gk] = dv.inc(v.tensor_copy(out=t1s[:, cc6 * 512:(cc6 + 1) * 512],
                                                     in_=pA[gk % 2][:]))
                block.vector(t1_e)
            T1E_DV = dv.n

            for dch in range(8):
                def t2_f2(t, tc=tc, dch=dch):
                    if dch == 0:
                        t.wait_ge(dv.h, T1E_DV)
                    gk = tc * 8 + dch
                    if gk >= 2:
                        t.wait_ge(dv.h, h_dv.get(gk - 2, 0))
                    for rc in range(6):
                        ins = t.matmul(pB[gk % 2][:],
                                       lhsT=oV[:, rc * 1024 + dch * 128:rc * 1024 + (dch + 1) * 128],
                                       rhs=t1s[:, rc * 512:(rc + 1) * 512],
                                       start=(rc == 0), stop=(rc == 5))
                    t2_mm[(tc, dch)] = mm.inc(ins)
                block.tensor(t2_f2)
            for dch in range(8):
                def h_dve(v, tc=tc, dch=dch):
                    v.wait_ge(mm.h, t2_mm[(tc, dch)])
                    if tc == 0 and dch == 0:
                        W(v, XNR_LD[0])
                    gk = tc * 8 + dch
                    v.tensor_scalar(out=ao[:], in0=pB[gk % 2][:],
                                    scalar1=ob_pp[:, dch:dch + 1], scalar2=None, op0=ALU.add)
                    h_dv[gk] = dv.inc(v.tensor_tensor(
                        out=hbf[:].rearrange("p (c t) -> p c t", t=TOK)[:, dch, tc * 512:(tc + 1) * 512],
                        in0=ao[:],
                        in1=xnr[:].rearrange("p (c t) -> p c t", t=TOK)[:, dch, tc * 512:(tc + 1) * 512],
                        op=ALU.add))
                block.vector(h_dve)
        H_DONE_DV = dv.n

        p3a.close()

        p3b = ExitStack()
        sbb = lambda n_, s_, d_: p3b.enter_context(nc.sbuf_tensor(n_, s_, d_))
        h2t = sbb("h2t", [128, 512], BF16)
        srow = sbb("srow", [1, 512], F32)
        trow = sbb("trow", [1, 512], F32)
        pR = [p3b.enter_context(nc.psum_tensor(f"pR{i}", [1, 512], F32)) for i in range(2)]

        sq_ac = {}
        st_mm = {}
        row_dv = {}
        row_ac = {}

        for tc in range(4):
            for dch in range(8):
                def b_sc(sc, tc=tc, dch=dch):
                    if tc == 0 and dch == 0:
                        sc.wait_ge(dv.h, H_DONE_DV)
                    if dch == 0 and tc >= 1:
                        sc.wait_ge(dv.h, row_dv.get(tc - 1, 0))
                    if dch >= 1:
                        sc.wait_ge(mm.h, st_mm.get((tc, dch - 1), 0))
                    sq_ac[(tc, dch)] = ac.inc(sc.activation(
                        out=h2t[:],
                        in_=hbf[:].rearrange("p (c t) -> p c t", t=TOK)[:, dch, tc * 512:(tc + 1) * 512],
                        func=AF.Square))
                block.scalar(b_sc)

                def b_mm(t, tc=tc, dch=dch):
                    t.wait_ge(ac.h, sq_ac[(tc, dch)])
                    t.matmul(pR[0][:],
                             lhsT=ones1[:],
                             rhs=hbf[:].rearrange("p (c t) -> p c t", t=TOK)[:, dch, tc * 512:(tc + 1) * 512],
                             start=(dch == 0), stop=(dch == 7))
                    ins = t.matmul(pR[1][:], lhsT=ones1[:], rhs=h2t[:],
                                   start=(dch == 0), stop=(dch == 7))
                    st_mm[(tc, dch)] = mm.inc(ins)
                block.tensor(b_mm)

            def b_row(v, tc=tc):
                v.wait_ge(mm.h, st_mm[(tc, 7)])
                v.tensor_scalar(out=srow[:], in0=pR[0][:], scalar1=1.0 / D, scalar2=None, op0=ALU.mult)
                dv.inc(v.tensor_scalar(out=trow[:], in0=pR[1][:], scalar1=1.0 / D, scalar2=None, op0=ALU.mult))
                v.wait_ge(dv.h, dv.n)
                dv.inc(v.scalar_tensor_tensor(out=c3r[0:1, tc * 512:(tc + 1) * 512], in0=srow[:], scalar=1.0,
                                              in1=srow[:], op0=ALU.mult, op1=ALU.mult))
                v.wait_ge(dv.h, dv.n)
                dv.inc(v.tensor_sub(out=trow[:], in0=trow[:], in1=c3r[0:1, tc * 512:(tc + 1) * 512]))
                v.wait_ge(dv.h, dv.n)
                row_dv[tc] = dv.inc(v.tensor_scalar(out=trow[:], in0=trow[:], scalar1=EPS,
                                                    scalar2=None, op0=ALU.add))
            block.vector(b_row)

            def b_rs(sc, tc=tc):
                sc.wait_ge(dv.h, row_dv[tc])
                sc.activation(out=a3r[0:1, tc * 512:(tc + 1) * 512], in_=trow[:], func=AF.Ln)
                row_ac[tc] = ac.inc(sc.activation(out=a3r[0:1, tc * 512:(tc + 1) * 512],
                                                  in_=a3r[0:1, tc * 512:(tc + 1) * 512],
                                                  func=AF.Exp, scale=-0.5))
            block.scalar(b_rs)

            def b_c3(v, tc=tc):
                v.wait_ge(ac.h, row_ac[tc])
                row_dv[tc] = dv.inc(v.scalar_tensor_tensor(
                    out=c3r[0:1, tc * 512:(tc + 1) * 512], in0=srow[:], scalar=-1.0,
                    in1=a3r[0:1, tc * 512:(tc + 1) * 512], op0=ALU.mult, op1=ALU.mult))
            block.vector(b_c3)
        STATS_DV = dv.n
        STATS_MM = mm.n
        p3b.close()

        p3c = ExitStack()
        scx = lambda n_, s_, d_: p3c.enter_context(nc.sbuf_tensor(n_, s_, d_))
        f1U = scx("f1U", [128, 8 * 512], BF16)
        f1V = scx("f1V", [128, 4 * 4096], BF16)
        f2U = scx("f2U", [128, 32 * 512], BF16)
        f2V = scx("f2V", [128, 4 * 1024], BF16)
        lnt = scx("lnt", [128, 8 * 512], BF16)
        t34 = scx("t34", [128, 4 * 512], BF16)
        hmid = scx("hmid", [128, 32 * 512], BF16)
        u1s = scx("u1s", [128, 512], F32)
        bb3 = scx("bb3", [128, 1024], F32)
        ao3 = scx("ao3", [128, 512], F32)
        pBC = p3c.enter_context(nc.psum_tensor("pBC", [128, 1024], F32))

        fcast = {}

        def emit_fw(i, src_ap, w, dst_ap):
            def fw_sync(sync, i=i):
                if i >= 2:
                    sync.wait_ge(dv.h, fcast[i - 2])
                else:
                    sync.wait_ge(dv.h, OW_DV)
                fcast[(i, "ld")] = dW3.rec(sync.dma_start(out=wld3[i % 2][:, 0:w], in_=src_ap), 16)
            block.sync(fw_sync)

            def fw_vec(v, i=i):
                W(v, fcast[(i, "ld")])
                fcast[i] = dv.inc(v.tensor_copy(out=dst_ap, in_=wld3[i % 2][:, 0:w]))
            block.vector(fw_vec)

        def p3c_gate(sync):
            sync.wait_ge(mm.h, STATS_MM)
            sync.wait_ge(dv.h, STATS_DV)
        block.sync(p3c_gate)

        fwi = 0
        for i in range(8):
            emit_fw(fwi, fc1U[i * 128:(i + 1) * 128, :], 512, f1U[:, i * 512:(i + 1) * 512])
            fwi += 1
        for i in range(4):
            for qr in range(4):
                emit_fw(fwi, fc1V[i * 128:(i + 1) * 128, qr * 1024:(qr + 1) * 1024], 1024,
                        f1V[:, i * 4096 + qr * 1024:i * 4096 + (qr + 1) * 1024])
                fwi += 1
        for i in range(32):
            emit_fw(fwi, fc2U[i * 128:(i + 1) * 128, :], 512, f2U[:, i * 512:(i + 1) * 512])
            fwi += 1
        for i in range(4):
            emit_fw(fwi, fc2V[i * 128:(i + 1) * 128, :], 1024, f2V[:, i * 1024:(i + 1) * 1024])
            fwi += 1
        FW_DV = dv.n

        bc3_mm = {}
        ln_dv = {}
        ln_ac = {}
        t3_mm = {}
        t3_ev = {}
        hm_mm = {}
        hm_ac = {}
        t4_mm = {}
        t4_ev = {}
        o5_mm = {}
        o5_ac = {}
        blk_dv = {}

        for tc in range(4):
            def m_bc(t, tc=tc):
                if tc == 0:
                    t.wait_ge(dv.h, STATS_DV)
                if tc >= 1:
                    t.wait_ge(dv.h, ln_dv.get((tc - 1, 7), 0))
                t.matmul(pBC[:, 0:512], lhsT=of32[0:1, 0:128], rhs=a3r[0:1, tc * 512:(tc + 1) * 512],
                         start=True, stop=True)
                ins = t.matmul(pBC[:, 512:1024], lhsT=of32[0:1, 0:128], rhs=c3r[0:1, tc * 512:(tc + 1) * 512],
                               start=True, stop=True)
                bc3_mm[tc] = mm.inc(ins)
            block.tensor(m_bc)

            def m_cp(v, tc=tc):
                v.wait_ge(mm.h, bc3_mm[tc])
                dv.inc(v.tensor_copy(out=bb3[:], in_=pBC[:]))
                v.wait_ge(dv.h, dv.n)
                bc3_mm[(tc, "cp")] = dv.n
            block.vector(m_cp)

            for dch in range(8):
                def m_ln(v, tc=tc, dch=dch):
                    v.wait_ge(dv.h, bc3_mm[(tc, "cp")])
                    prev = (tc, dch - 1) if dch else ((tc - 1, 7) if tc else None)
                    if prev is not None:
                        v.wait_ge(ac.h, ln_ac[prev])
                    v.tensor_mul(out=u1s[:],
                                 in0=hbf[:].rearrange("p (c t) -> p c t", t=TOK)[:, dch, tc * 512:(tc + 1) * 512],
                                 in1=bb3[:, 0:512])
                    ln_dv[(tc, dch)] = dv.inc(v.tensor_add(out=u1s[:], in0=u1s[:], in1=bb3[:, 512:1024]))
                block.vector(m_ln)

                def m_lna(sc, tc=tc, dch=dch):
                    sc.wait_ge(dv.h, ln_dv[(tc, dch)])
                    ln_ac[(tc, dch)] = ac.inc(sc.activation(
                        out=lnt[:, dch * 512:(dch + 1) * 512], in_=u1s[:], func=AF.Identity,
                        scale=l2g[:, dch:dch + 1], bias=l2b[:, dch:dch + 1]))
                block.scalar(m_lna)
            LN_AC = ac.n

            for rc in range(4):
                def m_t3(t, tc=tc, rc=rc):
                    if rc == 0:
                        t.wait_ge(ac.h, LN_AC)
                        if tc == 0:
                            t.wait_ge(dv.h, FW_DV)
                    gk = tc * 4 + rc
                    if gk >= 2:
                        t.wait_ge(dv.h, t3_ev.get(gk - 2, 0))
                    for dch in range(8):
                        ins = t.matmul(pA[gk % 2][:],
                                       lhsT=f1U[:, dch * 512 + rc * 128:dch * 512 + (rc + 1) * 128],
                                       rhs=lnt[:, dch * 512:(dch + 1) * 512],
                                       start=(dch == 0), stop=(dch == 7))
                    t3_mm[gk] = mm.inc(ins)
                block.tensor(m_t3)

                def m_t3e(v, tc=tc, rc=rc):
                    gk = tc * 4 + rc
                    v.wait_ge(mm.h, t3_mm[gk])
                    t3_ev[gk] = dv.inc(v.tensor_copy(out=t34[:, rc * 512:(rc + 1) * 512],
                                                     in_=pA[gk % 2][:]))
                block.vector(m_t3e)
            T3E_DV = dv.n

            for ic in range(32):
                def m_hm(t, tc=tc, ic=ic, tdv=T3E_DV):
                    if ic == 0:
                        t.wait_ge(dv.h, tdv)
                    gk = tc * 32 + ic
                    if gk >= 2:
                        t.wait_ge(ac.h, hm_ac.get(gk - 2, 0))
                    for rc in range(4):
                        ins = t.matmul(pB[gk % 2][:],
                                       lhsT=f1V[:, rc * 4096 + ic * 128:rc * 4096 + (ic + 1) * 128],
                                       rhs=t34[:, rc * 512:(rc + 1) * 512],
                                       start=(rc == 0), stop=(rc == 3))
                    hm_mm[gk] = mm.inc(ins)
                block.tensor(m_hm)

                def m_ge(sc, tc=tc, ic=ic):
                    sc.wait_ge(mm.h, hm_mm[tc * 32 + ic])
                    hm_ac[tc * 32 + ic] = ac.inc(sc.activation(
                        out=hmid[:, ic * 512:(ic + 1) * 512], in_=pB[(tc * 32 + ic) % 2][:],
                        func=AF.Gelu_apprx_tanh, bias=f1b_pp[:, ic:ic + 1]))
                block.scalar(m_ge)
            HM_AC = ac.n

            for cc4 in range(4):
                def m_t4(t, tc=tc, cc4=cc4):
                    if cc4 == 0:
                        t.wait_ge(ac.h, HM_AC)
                    gk = tc * 4 + cc4
                    if gk >= 2:
                        t.wait_ge(dv.h, t4_ev.get(gk - 2, 0))
                    for ic in range(32):
                        ins = t.matmul(pA[gk % 2][:],
                                       lhsT=f2U[:, ic * 512 + cc4 * 128:ic * 512 + (cc4 + 1) * 128],
                                       rhs=hmid[:, ic * 512:(ic + 1) * 512],
                                       start=(ic == 0), stop=(ic == 31))
                    t4_mm[gk] = mm.inc(ins)
                block.tensor(m_t4)

                def m_t4e(v, tc=tc, cc4=cc4):
                    gk = tc * 4 + cc4
                    v.wait_ge(mm.h, t4_mm[gk])
                    t4_ev[gk] = dv.inc(v.tensor_copy(out=t34[:, cc4 * 512:(cc4 + 1) * 512],
                                                     in_=pA[gk % 2][:]))
                block.vector(m_t4e)
            T4E_DV = dv.n

            for dch in range(8):
                def m_o5(t, tc=tc, dch=dch, tdv=T4E_DV):
                    if dch == 0:
                        t.wait_ge(dv.h, tdv)
                    gk = tc * 8 + dch
                    if gk >= 2:
                        t.wait_ge(dv.h, blk_dv.get(gk - 2, 0))
                    for cc4 in range(4):
                        ins = t.matmul(pB[gk % 2][:],
                                       lhsT=f2V[:, cc4 * 1024 + dch * 128:cc4 * 1024 + (dch + 1) * 128],
                                       rhs=t34[:, cc4 * 512:(cc4 + 1) * 512],
                                       start=(cc4 == 0), stop=(cc4 == 3))
                    o5_mm[gk] = mm.inc(ins)
                block.tensor(m_o5)

                def m_oa(sc, tc=tc, dch=dch):
                    sc.wait_ge(mm.h, o5_mm[tc * 8 + dch])
                    if tc * 8 + dch >= 1:
                        sc.wait_ge(dv.h, blk_dv.get(tc * 8 + dch - 1, 0))
                    o5_ac[tc * 8 + dch] = ac.inc(sc.activation(
                        out=ao3[:], in_=pB[(tc * 8 + dch) % 2][:], func=AF.Identity,
                        bias=f2b_pp[:, dch:dch + 1]))
                block.scalar(m_oa)

                def m_ov(v, tc=tc, dch=dch):
                    v.wait_ge(ac.h, o5_ac[tc * 8 + dch])
                    hsl = hbf[:].rearrange("p (c t) -> p c t", t=TOK)[:, dch, tc * 512:(tc + 1) * 512]
                    blk_dv[tc * 8 + dch] = dv.inc(v.tensor_tensor(out=hsl, in0=hsl, in1=ao3[:], op=ALU.add))
                block.vector(m_ov)
        BLK_DV = dv.n
        p3c.close()

        p3d = ExitStack()
        sdd = lambda n_, s_, d_: p3d.enter_context(nc.sbuf_tensor(n_, s_, d_))
        fft = sdd("fft", [128, 8 * 512], BF16)
        sfin = sdd("sfin", [128, 8 * 512], F32)
        xrt = [sdd(f"xrt{i}", [128, D], F32) for i in range(2)]
        ot = [sdd(f"ot{i}", [128, D], F32) for i in range(2)]
        ptf = [p3d.enter_context(nc.psum_tensor(f"ptf{i}", [128, 1024], F32)) for i in range(2)]

        fft_ld = [0] * 4
        sf_dv = [0] * 4
        ftr_mm = {}
        ot_dv = {}
        out_st = {}
        xr_ld = {}

        for tc in range(4):
            def d_ld(sync, tc=tc):
                if tc == 0:
                    W(sync, FFN_ST)
                    sync.wait_ge(dv.h, BLK_DV)
                if tc >= 1:
                    sync.wait_ge(dv.h, sf_dv[tc - 1])
                fft_ld[tc] = dF.rec(sync.dma_start(out=fft[:],
                                      in_=ffn_sp.rearrange("p (c t) -> p c t", t=TOK)[:, :, tc * 512:(tc + 1) * 512]), 16)
            block.sync(d_ld)

            def d_sum(v, tc=tc):
                W(v, fft_ld[tc])
                if tc == 0:
                    v.wait_ge(dv.h, BLK_DV)
                for dch in range(8):
                    ins = v.tensor_tensor(
                        out=sfin[:, dch * 512:(dch + 1) * 512],
                        in0=hbf[:].rearrange("p (c t) -> p c t", t=TOK)[:, dch, tc * 512:(tc + 1) * 512],
                        in1=fft[:, dch * 512:(dch + 1) * 512], op=ALU.add)
                sf_dv[tc] = dv.inc(ins)
            block.vector(d_sum)

            for tb in range(4):
                ti = tc * 4 + tb

                def d_x(sync, tc=tc, tb=tb, ti=ti):
                    if ti >= 2:
                        sync.wait_ge(dv.h, ot_dv.get(ti - 2, 0))
                    xr_ld[ti] = dX[ti % 2].rec(sync.dma_start(out=xrt[ti % 2][:],
                                          in_=x_own[ti * 128:(ti + 1) * 128, :]), 16)
                block.sync(d_x)

                def d_tr(t, tc=tc, tb=tb, ti=ti):
                    t.wait_ge(dv.h, sf_dv[tc])
                    if ti >= 2:
                        t.wait_ge(dv.h, ot_dv.get(ti - 2, 0))
                    for dch in range(8):
                        ins = t.transpose(ptf[ti % 2][:, dch * 128:(dch + 1) * 128],
                                          in_=sfin[:, dch * 512 + tb * 128:dch * 512 + (tb + 1) * 128],
                                          identity=idf[:])
                    ftr_mm[ti] = mm.inc(ins)
                block.tensor(d_tr)

                def d_add(v, ti=ti):
                    v.wait_ge(mm.h, ftr_mm[ti])
                    W(v, xr_ld[ti])
                    if ti >= 2 and out_st.get(ti - 2):
                        W(v, out_st.get(ti - 2))
                    v.tensor_tensor(out=ot[ti % 2][:, 0:512], in0=ptf[ti % 2][:, 0:512],
                                    in1=xrt[ti % 2][:, 0:512], op=ALU.add)
                    ot_dv[ti] = dv.inc(v.tensor_tensor(out=ot[ti % 2][:, 512:1024], in0=ptf[ti % 2][:, 512:1024],
                                                       in1=xrt[ti % 2][:, 512:1024], op=ALU.add))
                block.vector(d_add)

                def d_st(sync, ti=ti):
                    sync.wait_ge(dv.h, ot_dv[ti])
                    out_st[ti] = sO[ti % 2].rec(sync.dma_start(out=out_t[ti * 128:(ti + 1) * 128, :], in_=ot[ti % 2][:]), 16)
                block.sync(d_st)
        def fin_sync(sync):
            W(sync, sO[0].fin())
            W(sync, sO[1].fin())
        block.sync(fin_sync)

    p3d.close()
    p3.close()
    top.close()
    return nc


_CACHE = {}


def kernel(**inputs):
    g = lambda k: np.asarray(inputs[k])
    x = g("L0_attn_hidden_states").astype(np.float32)

    qU, kU, vU = g("L0_attn_q_U"), g("L0_attn_k_U"), g("L0_attn_v_U")
    qV, kV, vV = g("L0_attn_q_V"), g("L0_attn_k_V"), g("L0_attn_v_V")
    qb, kb, vb = g("L0_attn_q_b"), g("L0_attn_k_b"), g("L0_attn_v_b")

    qkvU = np.zeros((D, 3 * 1024), np.float32)
    qkvV = np.zeros((128, 3 * 1024), np.float32)
    qkvb = np.zeros((128, 3088), np.float32)
    for pi, (U, V, bvec) in enumerate([(qU, qV, qb), (kU, kV, kb), (vU, vV, vb)]):
        for h in range(H):
            qkvU[:, pi * 1024 + h * 64:pi * 1024 + h * 64 + 48] = U[:, h, :]
            qkvV[0:48, pi * 1024 + h * 64:pi * 1024 + (h + 1) * 64] = V[h]
            if pi < 2:
                qkvb[:, pi * 1024 + h * 64:pi * 1024 + (h + 1) * 64] = bvec[h][None, :]
            else:
                qkvb[:, 2048 + h * 65:2048 + h * 65 + 64] = bvec[h][None, :]
                qkvb[:, 2048 + h * 65 + 64] = 1.0

    pp = lambda vec, nch: np.ascontiguousarray(vec.astype(np.float32).reshape(nch, 128).T)

    kp = np.arange(128)[:, None]
    qq = np.arange(512)[None, :]
    cmask = np.zeros((128, 4 * 512), np.float32)
    for m in range(4):
        cmask[:, m * 512:(m + 1) * 512] = np.where(qq >= m * 128 + kp, 0.0, -1e9)

    feeds_common = {
        "qkvU": qkvU,
        "qkvV": qkvV,
        "ln1g": pp(g("L0_attn_ln1_g"), 8),
        "ln1b": pp(g("L0_attn_ln1_b"), 8),
        "ln2g": pp(g("L0_attn_ln2_g"), 8),
        "ln2b": pp(g("L0_attn_ln2_b"), 8),
        "outU": g("L0_attn_out_U").astype(np.float32),
        "outV": g("L0_attn_out_V").astype(np.float32),
        "outb": pp(g("L0_attn_out_b"), 8),
        "fc1U": g("L0_attn_fc1_U").astype(np.float32),
        "fc1V": g("L0_attn_fc1_V").astype(np.float32),
        "fc1b": pp(g("L0_attn_fc1_b"), 32),
        "fc2U": g("L0_attn_fc2_U").astype(np.float32),
        "fc2V": g("L0_attn_fc2_V").astype(np.float32),
        "fc2b": pp(g("L0_attn_fc2_b"), 8),
        "wgwu": np.ascontiguousarray(
            np.concatenate([g("L0_ffn_w_gate"), g("L0_ffn_w_up")], axis=0).astype(np.float32)),
        "wdn": g("L0_ffn_w_down").astype(np.float32),
        "cmask": cmask.astype(ml_dtypes.bfloat16),
        "qkvb": qkvb,
        "idenb": np.eye(128).astype(ml_dtypes.bfloat16),
        "idenf": np.eye(128, dtype=np.float32),
    }

    in_maps = []
    for c in range(8):
        b, j = c // 4, c % 4
        m = dict(feeds_common)
        m["x_own"] = np.ascontiguousarray(x[b, j * TOK:(j + 1) * TOK, :])
        m["coffs"] = np.tile(np.array([[j * 256, j * 260, j * TOK, 0]], np.int32), (128, 1))
        in_maps.append(m)

    if "nc" not in _CACHE:
        _CACHE["nc"] = build_nc()
    res = run_bass_kernel_spmd(_CACHE["nc"], in_maps, core_ids=list(range(8)))

    out = np.empty((B, S, D), np.float32)
    for c in range(8):
        b, j = c // 4, c % 4
        out[b, j * TOK:(j + 1) * TOK, :] = res.results[c]["out"]
    return out


# version 11
# speedup vs baseline: 1.0249x; 1.0065x over previous
import numpy as np
import ml_dtypes
from contextlib import ExitStack

import concourse.bass as bass
import concourse.mybir as mybir
from concourse.bass_utils import run_bass_kernel_spmd
from concourse.expressions import RuntimeValue

F32 = mybir.dt.float32
BF16 = mybir.dt.bfloat16
I32 = mybir.dt.int32
AF = mybir.ActivationFunctionType
ALU = mybir.AluOpType
AX = mybir.AxisListType
ds = bass.ds

B, S, D, H, DH = 2, 8192, 1024, 16, 64
TOK = 2048
NT = TOK // 128
EPS = 1e-5
NFFN = 2816
IM = 4096
VCOLS = 2048
GCOLS = 2048 + 16 * 65
MAGIC = 1.5 * (2 ** 22)
NB_LIST = [(0, 512), (512, 512), (1024, 512), (1536, 512), (2048, 512), (2560, 256)]


class Ctr:
    def __init__(self, h):
        self.h = h
        self.n = 0

    def inc(self, inst, k=1):
        inst.then_inc(self.h, k)
        self.n += k
        return self.n

    def rec(self, inst, k=1):
        self.inc(inst, k)
        return (self.h, self.n)

    def fin(self):
        return (self.h, self.n)


def W(eng, rec):
    if rec and rec[1] > 0:
        eng.wait_ge(rec[0], rec[1])


def build_nc():
    nc = bass.Bass("TRN2", target_bir_lowering=False)
    dt = nc.dram_tensor

    x_own = dt("x_own", [TOK, D], F32, kind="ExternalInput").ap()
    qkvU = dt("qkvU", [D, 3 * 1024], F32, kind="ExternalInput").ap()
    qkvV = dt("qkvV", [128, 3 * 1024], F32, kind="ExternalInput").ap()
    ln1g = dt("ln1g", [128, 8], F32, kind="ExternalInput").ap()
    ln1b = dt("ln1b", [128, 8], F32, kind="ExternalInput").ap()
    ln2g = dt("ln2g", [128, 8], F32, kind="ExternalInput").ap()
    ln2b = dt("ln2b", [128, 8], F32, kind="ExternalInput").ap()
    outU = dt("outU", [D, 768], F32, kind="ExternalInput").ap()
    outV = dt("outV", [768, D], F32, kind="ExternalInput").ap()
    outb = dt("outb", [128, 8], F32, kind="ExternalInput").ap()
    fc1U = dt("fc1U", [D, 512], F32, kind="ExternalInput").ap()
    fc1V = dt("fc1V", [512, IM], F32, kind="ExternalInput").ap()
    fc1b = dt("fc1b", [128, 32], F32, kind="ExternalInput").ap()
    fc2U = dt("fc2U", [IM, 512], F32, kind="ExternalInput").ap()
    fc2V = dt("fc2V", [512, D], F32, kind="ExternalInput").ap()
    fc2b = dt("fc2b", [128, 8], F32, kind="ExternalInput").ap()
    wgwu = dt("wgwu", [2 * NFFN, D], F32, kind="ExternalInput").ap()
    wdn = dt("wdn", [D, NFFN], F32, kind="ExternalInput").ap()
    cmask = dt("cmask", [128, 4 * 512], BF16, kind="ExternalInput").ap()
    idenb = dt("idenb", [128, 128], BF16, kind="ExternalInput").ap()
    idenf = dt("idenf", [128, 128], F32, kind="ExternalInput").ap()
    qkvb = dt("qkvb", [128, 3088], F32, kind="ExternalInput").ap()
    coffs = dt("coffs", [128, 4], I32, kind="ExternalInput").ap()

    out_t = dt("out", [TOK, D], F32, kind="ExternalOutput").ap()

    qkv_loc = dt("qkv_loc", [TOK, GCOLS], BF16, kind="Internal").ap()
    qkv_g = dt("qkv_g", [S, GCOLS], BF16, kind="Internal").ap()
    ctx_loc = dt("ctx_loc", [256, S], BF16, kind="Internal").ap()
    ctx_g = dt("ctx_g", [1024, S], BF16, kind="Internal").ap()
    xn_sp = dt("xn_sp", [128, 8 * TOK], BF16, kind="Internal").ap()
    ffn_sp = dt("ffn_sp", [128, 8 * TOK], BF16, kind="Internal").ap()

    rg = [[0, 1, 2, 3], [4, 5, 6, 7]]

    top = ExitStack()
    sbT = lambda n_, s_, d_: top.enter_context(nc.sbuf_tensor(n_, s_, d_))
    C = {k: Ctr(top.enter_context(nc.semaphore("s_" + k)))
         for k in ["ld", "st", "mm", "dv", "ac", "cc", "dK", "dCO", "dC", "dW3", "dXN", "dF",
                   "sX", "sF", "sC",
                   "dX0", "dX1", "dW0", "dW1", "dQ0", "dQ1", "dV0", "dV1", "dT0", "dT1",
                   "sA0", "sA1", "sO0", "sO1"]}
    ld, st, mm, dv, ac, cc = (C[k] for k in ["ld", "st", "mm", "dv", "ac", "cc"])
    dK, dCO, dC, dW3, dXN, dF = (C[k] for k in ["dK", "dCO", "dC", "dW3", "dXN", "dF"])
    sX, sF, sC = (C[k] for k in ["sX", "sF", "sC"])
    dX = [C["dX0"], C["dX1"]]
    dW = [C["dW0"], C["dW1"]]
    dQ = [C["dQ0"], C["dQ1"]]
    dV = [C["dV0"], C["dV1"]]
    dT = [C["dT0"], C["dT1"]]
    sA = [C["sA0"], C["sA1"]]
    sO = [C["sO0"], C["sO1"]]

    cm = sbT("cm", [128, 2048], BF16)
    idb = sbT("idb", [128, 128], BF16)
    idf = sbT("idf", [128, 128], F32)
    l1g = sbT("l1g", [128, 8], F32)
    l1b = sbT("l1b", [128, 8], F32)
    l2g = sbT("l2g", [128, 8], F32)
    l2b = sbT("l2b", [128, 8], F32)
    ob_pp = sbT("ob_pp", [128, 8], F32)
    f1b_pp = sbT("f1b_pp", [128, 32], F32)
    f2b_pp = sbT("f2b_pp", [128, 8], F32)
    qb_bf = sbT("qb_bf", [128, 3088], BF16)
    ones1 = sbT("ones1", [128, 1], BF16)
    of32 = sbT("of32", [1, 128], F32)
    co_sb = sbT("co_sb", [128, 4], I32)
    qs_box = {}

    class _QS:
        def __getitem__(self, k):
            return qs_box[k]
    

    def quant1(v, src, dst, n):
        qs_a = qs_box["a"]; qs_r = qs_box["r"]; qs_d = qs_box["d"]
        qs_y = qs_box["y"]; qs_m = qs_box["m"]; qs_t = qs_box["t"]
        nb = n // 16
        dv.inc(v.tensor_reduce(out=qs_a[:, 0:nb],
                               in_=src.rearrange("p (nb k) -> p nb k", k=16),
                               axis=AX.X, op=ALU.max, apply_absolute_value=True))
        v.wait_ge(dv.h, dv.n)
        v.tensor_scalar(out=qs_a[:, 0:nb], in0=qs_a[:, 0:nb], scalar1=6e-12, scalar2=None, op0=ALU.max)
        dv.inc(v.reciprocal(out=qs_r[:, 0:nb], in_=qs_a[:, 0:nb]))
        v.wait_ge(dv.h, dv.n)
        v.tensor_scalar(out=qs_r[:, 0:nb], in0=qs_r[:, 0:nb], scalar1=12.0, scalar2=None, op0=ALU.mult)
        v.tensor_scalar(out=qs_d[:, 0:nb], in0=qs_a[:, 0:nb], scalar1=1.0 / 12.0, scalar2=None, op0=ALU.mult)
        r12b = qs_r[:, 0:nb].unsqueeze(2).broadcast_to([128, nb, 16])
        dqb = qs_d[:, 0:nb].unsqueeze(2).broadcast_to([128, nb, 16])
        v.tensor_tensor(out=qs_y[:, 0:n].rearrange("p (nb k) -> p nb k", k=16),
                        in0=src.rearrange("p (nb k) -> p nb k", k=16), in1=r12b, op=ALU.mult)
        v.tensor_scalar(out=qs_m[:, 0:n].bitcast(I32), in0=qs_y[:, 0:n].bitcast(I32),
                        scalar1=0x7F800000, scalar2=None, op0=ALU.bitwise_and)
        v.tensor_scalar(out=qs_m[:, 0:n], in0=qs_m[:, 0:n], scalar1=2.0, scalar2=MAGIC,
                        op0=ALU.max, op1=ALU.mult)
        v.tensor_tensor(out=qs_t[:, 0:n], in0=qs_y[:, 0:n], in1=qs_m[:, 0:n], op=ALU.add)
        v.tensor_tensor(out=qs_t[:, 0:n], in0=qs_t[:, 0:n], in1=qs_m[:, 0:n], op=ALU.subtract)
        return dv.inc(v.tensor_tensor(out=dst.rearrange("p (nb k) -> p nb k", k=16),
                                      in0=qs_t[:, 0:n].rearrange("p (nb k) -> p nb k", k=16),
                                      in1=dqb, op=ALU.mult))

    def quant(v, src, dst, n):
        last = 0
        for o in range(0, n, 1024):
            w = min(1024, n - o)
            last = quant1(v, src[:, o:o + w], dst[:, o:o + w], w)
        return last

    def rsqrt_row(sc, dst, src):
        sc.activation(out=dst, in_=src, func=AF.Ln)
        return ac.inc(sc.activation(out=dst, in_=dst, func=AF.Exp, scale=-0.5))

    with nc.Block() as block:
        COFF = [None, None, None]

        def p0_sync(sync):
            dCO.inc(sync.dma_start(out=co_sb[:], in_=coffs), 16)
            sync.wait_ge(dCO.h, 16)
            r0 = sync.alloc_register(name="coff0")
            sync.load(r0, co_sb[0:1, 0:1])
            COFF[0] = RuntimeValue(r0, min_val=0, max_val=768)
            r1s = sync.alloc_register(name="coffs1")
            sync.load(r1s, co_sb[0:1, 1:2])
            COFF[1] = RuntimeValue(r1s, min_val=0, max_val=780)
            r2s = sync.alloc_register(name="coffs2")
            sync.load(r2s, co_sb[0:1, 2:3])
            COFF[2] = RuntimeValue(r2s, min_val=0, max_val=6144)
            for d_, s_ in [(cm, cmask), (idb, idenb), (idf, idenf), (l1g, ln1g),
                           (l1b, ln1b), (l2g, ln2g), (l2b, ln2b), (ob_pp, outb),
                           (f1b_pp, fc1b), (f2b_pp, fc2b)]:
                dK.inc(sync.dma_start(out=d_[:], in_=s_), 16)
        block.sync(p0_sync)
        P0_LD = dK.fin()

        def p0_vec(v):
            v.memset(ones1[:], 1.0)
            dv.inc(v.memset(of32[:], 1.0))
        block.vector(p0_vec)


        p1x = ExitStack()
        xqfm = p1x.enter_context(nc.sbuf_tensor("xqfm", [128, 8 * TOK], BF16))
        for _k, _sh in [("y", [128, 1024]), ("m", [128, 1024]), ("t", [128, 1024]),
                        ("a", [128, 64]), ("r", [128, 64]), ("d", [128, 64])]:
            qs_box[_k] = p1x.enter_context(nc.sbuf_tensor("qs_" + _k, _sh, F32))
        qs_y = qs_box["y"]

        qbcast = {}
        for qbi in range(4):
            w = 772
            def qb_sync(sync, qbi=qbi, w=w):
                if qbi >= 1:
                    sync.wait_ge(dv.h, qbcast[qbi - 1])
                qbcast[(qbi, "ld")] = dC.rec(sync.dma_start(out=qs_y[:, 0:w].bitcast(F32),
                                      in_=qkvb[:, qbi * w:(qbi + 1) * w]), 16)
            block.sync(qb_sync)

            def qb_vec(v, qbi=qbi, w=w):
                W(v, qbcast[(qbi, "ld")])
                qbcast[qbi] = dv.inc(v.tensor_copy(out=qb_bf[:, qbi * w:(qbi + 1) * w],
                                                   in_=qs_y[:, 0:w].bitcast(F32)))
            block.vector(qb_vec)
        QB_DV = dv.n


        p1 = ExitStack()
        sb1 = lambda n_, s_, d_: p1.enter_context(nc.sbuf_tensor(n_, s_, d_))
        ps1 = lambda n_, s_, d_: p1.enter_context(nc.psum_tensor(n_, s_, d_))
        wU = sb1("wU", [128, 8 * 3072], BF16)
        wVa = sb1("wVa", [128, 3072], BF16)
        wld = [sb1(f"wld{i}", [128, 1536], F32) for i in range(2)]
        xt = [sb1(f"xt{i}", [128, D], F32) for i in range(2)]
        xnb = [sb1(f"xnb{i}", [128, D], F32) for i in range(2)]
        xnbf = [sb1(f"xnbf{i}", [128, D], BF16) for i in range(2)]
        l1c = [sb1(f"l1c{i}", [128, D], BF16) for i in range(2)]
        xqt = [sb1(f"xqt{i}", [128, D], BF16) for i in range(2)]
        ring = [sb1(f"ring{i}", [128, 8 * 512], BF16) for i in range(2)]
        xst = [sb1(f"xst{i}", [128, D], BF16) for i in range(2)]
        psb = [sb1(f"psb{i}", [128, 8 * 512], BF16) for i in range(3)]
        asm = [sb1(f"asm{i}", [128, 1040], BF16) for i in range(2)]
        rw = {k: [sb1(f"rw_{k}{i}", [128, 1], F32) for i in range(2)]
              for k in ["s1", "s2", "mu", "var", "u1", "u2", "r1", "a2", "nm1", "nm2"]}
        ptb = [ps1(f"ptb{i}", [128, 1024], BF16) for i in range(2)]
        pst1 = [ps1(f"pst1{i}", [128, 512], F32) for i in range(2)]
        pst2 = [ps1(f"pst2{i}", [128, 512], F32) for i in range(2)]

        wu_ld = [0] * 18
        wu_dv = [0] * 18
        for j in range(16):
            i, hh = j // 2, j % 2
            def wl_sync(sync, j=j, i=i, hh=hh):
                if j >= 2:
                    sync.wait_ge(dv.h, wu_dv[j - 2])
                else:
                    sync.wait_ge(dv.h, QB_DV)
                wu_ld[j] = dC.rec(sync.dma_start(
                    out=wld[j % 2][:],
                    in_=qkvU[i * 128:(i + 1) * 128, hh * 1536:(hh + 1) * 1536]), 16)
            block.sync(wl_sync)

            def wl_vec(v, j=j, i=i, hh=hh):
                W(v, wu_ld[j])
                wu_dv[j] = dv.inc(v.tensor_copy(
                    out=wU[:, i * 3072 + hh * 1536:i * 3072 + (hh + 1) * 1536],
                    in_=wld[j % 2][:]))
            block.vector(wl_vec)

        for hh in range(2):
            def wv_sync(sync, hh=hh):
                sync.wait_ge(dv.h, wu_dv[14 + hh])
                wu_ld[16 + hh] = dC.rec(sync.dma_start(
                    out=wld[hh][:, :], in_=qkvV[:, hh * 1536:(hh + 1) * 1536]), 16)
            block.sync(wv_sync)

            def wv_vec(v, hh=hh):
                W(v, wu_ld[16 + hh])
                dv.inc(v.tensor_copy(out=wVa[0:48, hh * 1536:(hh + 1) * 1536], in_=wld[hh][0:48, :]))
                dv.inc(v.tensor_copy(out=wVa[64:112, hh * 1536:(hh + 1) * 1536], in_=wld[hh][0:48, :]))
            block.vector(wv_vec)
        WV_DV = dv.n

        x_ld = [0] * NT
        stat_dv = [0] * NT
        nm_dv = [0] * NT
        rs_ac = [0] * NT
        xn_ac = [0] * NT
        q_dv = [0] * NT
        tr1_mm = [0] * NT
        tr2_mm = [0] * NT
        ev1_ac = [0] * NT
        ev1_dv = [0] * NT
        ev2_dv = [0] * NT
        s1_done_mm = [0] * 4
        s2_done_mm = [0] * 4
        pst1_ev = {}
        pst2_ev = {}
        asm_st = {}
        xsp_st = {}
        qkv_st = [0] * 4

        for i in range(NT):
            p = i % 2
            s = i // 4

            def t_sync(sync, i=i, p=p):
                if i >= 2:
                    sync.wait_ge(ac.h, xn_ac[i - 2])
                x_ld[i] = dX[p].rec(sync.dma_start(out=xt[p][:], in_=x_own[i * 128:(i + 1) * 128, :]), 16)
            block.sync(t_sync)

            def t_vec1(v, i=i, p=p):
                W(v, x_ld[i])
                if i == 0:
                    W(v, P0_LD)
                dv.inc(v.reduce_sum(out=rw["s1"][p][:], in_=xt[p][:], axis=AX.X))
                dv.inc(v.scalar_tensor_tensor(out=qs_y[:, 0:D], in0=xt[p][:], scalar=1.0,
                                              in1=xt[p][:], op0=ALU.mult, op1=ALU.mult,
                                              accum_out=rw["s2"][p][:]))
                v.wait_ge(dv.h, dv.n)
                v.tensor_scalar(out=rw["mu"][p][:], in0=rw["s1"][p][:], scalar1=1.0 / D, scalar2=None, op0=ALU.mult)
                dv.inc(v.tensor_scalar(out=rw["var"][p][:], in0=rw["s2"][p][:], scalar1=1.0 / D, scalar2=None, op0=ALU.mult))
                v.wait_ge(dv.h, dv.n)
                dv.inc(v.tensor_tensor(out=rw["u1"][p][:], in0=rw["mu"][p][:], in1=rw["mu"][p][:], op=ALU.mult))
                v.wait_ge(dv.h, dv.n)
                dv.inc(v.tensor_tensor(out=rw["var"][p][:], in0=rw["var"][p][:], in1=rw["u1"][p][:], op=ALU.subtract))
                v.wait_ge(dv.h, dv.n)
                v.tensor_scalar(out=rw["u1"][p][:], in0=rw["var"][p][:], scalar1=EPS, scalar2=None, op0=ALU.add)
                stat_dv[i] = dv.inc(v.tensor_scalar(out=rw["u2"][p][:], in0=rw["var"][p][:],
                                                    scalar1=1.0 + EPS, scalar2=EPS * EPS,
                                                    op0=ALU.mult, op1=ALU.add))
            block.vector(t_vec1)

            def t_sc1(sc, i=i, p=p):
                sc.wait_ge(dv.h, stat_dv[i])
                rsqrt_row(sc, rw["r1"][p][:], rw["u1"][p][:])
                rs_ac[i] = rsqrt_row(sc, rw["a2"][p][:], rw["u2"][p][:])
            block.scalar(t_sc1)

            def t_vec2(v, i=i, p=p):
                v.wait_ge(ac.h, rs_ac[i])
                v.scalar_tensor_tensor(out=rw["nm1"][p][:], in0=rw["mu"][p][:], scalar=-1.0,
                                       in1=rw["r1"][p][:], op0=ALU.mult, op1=ALU.mult)
                nm_dv[i] = dv.inc(v.scalar_tensor_tensor(out=rw["nm2"][p][:], in0=rw["mu"][p][:],
                                                         scalar=-1.0, in1=rw["a2"][p][:],
                                                         op0=ALU.mult, op1=ALU.mult))
            block.vector(t_vec2)

            def t_sc2(sc, i=i, p=p):
                sc.wait_ge(dv.h, nm_dv[i])
                if i >= 2:
                    sc.wait_ge(dv.h, q_dv[i - 2])
                    sc.wait_ge(mm.h, tr1_mm[i - 2])
                sc.activation(out=xnb[p][:], in_=xt[p][:], func=AF.Identity,
                              scale=rw["r1"][p][:], bias=rw["nm1"][p][:])
                sc.activation(out=xnbf[p][:], in_=xt[p][:], func=AF.Identity,
                              scale=rw["r1"][p][:], bias=rw["nm1"][p][:])
                xn_ac[i] = ac.inc(sc.activation(out=l1c[p][:], in_=xt[p][:], func=AF.Identity,
                                                scale=rw["a2"][p][:], bias=rw["nm2"][p][:]))
            block.scalar(t_sc2)

            def t_vec3(v, i=i, p=p):
                v.wait_ge(ac.h, xn_ac[i])
                if i >= 2:
                    v.wait_ge(mm.h, tr2_mm[i - 2])
                q_dv[i] = quant(v, xnb[p][:], xqt[p][:], D)
            block.vector(t_vec3)

            def t_mm1(t, i=i, p=p):
                t.wait_ge(ac.h, xn_ac[i])
                if i >= 1:
                    t.wait_ge(dv.h, ev2_dv[i - 1])
                    t.wait_ge(dv.h, ev1_dv[i - 1])
                for c in range(8):
                    t.transpose(ptb[0][:, c * 128:(c + 1) * 128],
                                in_=l1c[p][:, c * 128:(c + 1) * 128], identity=idb[:])
                for c in range(8):
                    ins = t.transpose(ptb[1][:, c * 128:(c + 1) * 128],
                                      in_=xnbf[p][:, c * 128:(c + 1) * 128], identity=idb[:])
                tr1_mm[i] = mm.inc(ins)
            block.tensor(t_mm1)

            def t_sc3(sc, i=i, s=s):
                sc.wait_ge(mm.h, tr1_mm[i])
                if s >= 2 and i % 4 == 0:
                    sc.wait_ge(mm.h, s1_done_mm[s - 2])
                slot = i % 4
                for c in range(8):
                    ins = sc.activation(
                        out=ring[s % 2][:, c * 512 + slot * 128: c * 512 + (slot + 1) * 128],
                        in_=ptb[0][:, c * 128:(c + 1) * 128], func=AF.Identity,
                        scale=l1g[:, c:c + 1], bias=l1b[:, c:c + 1])
                ev1_ac[i] = ac.inc(ins)
            block.scalar(t_sc3)

            def t_vec4(v, i=i, p=p):
                v.wait_ge(mm.h, tr1_mm[i])
                if i >= 2:
                    W(v, xsp_st[i - 2])
                ev1_dv[i] = dv.inc(v.tensor_copy(out=xst[p][:], in_=ptb[1][:]))
            block.vector(t_vec4)

            def t_mm2(t, i=i, p=p):
                t.wait_ge(dv.h, q_dv[i])
                t.wait_ge(ac.h, ev1_ac[i])
                for c in range(8):
                    ins = t.transpose(ptb[0][:, c * 128:(c + 1) * 128],
                                      in_=xqt[p][:, c * 128:(c + 1) * 128], identity=idb[:])
                tr2_mm[i] = mm.inc(ins)
            block.tensor(t_mm2)

            def t_vec5(v, i=i):
                v.wait_ge(mm.h, tr2_mm[i])
                ev2_dv[i] = dv.inc(v.tensor_copy(
                    out=xqfm[:].rearrange("p (c t) -> p c t", t=TOK)[:, :, i * 128:(i + 1) * 128],
                    in_=ptb[0][:].rearrange("p (c t) -> p c t", t=128)))
            block.vector(t_vec5)

            def t_sync2(sync, i=i, p=p):
                sync.wait_ge(dv.h, ev1_dv[i])
                xsp_st[i] = sX.rec(sync.dma_start(
                    out=xn_sp.rearrange("p (c t) -> p c t", t=TOK)[:, :, i * 128:(i + 1) * 128],
                    in_=xst[p][:].rearrange("p (c t) -> p c t", t=128)), 16)
            block.sync(t_sync2)

            if i % 4 == 3:
                s1mm = {}

                def sl_pre(t, s=s, i=i):
                    t.wait_ge(ac.h, ev1_ac[i])
                    if s == 0:
                        t.wait_ge(dv.h, WV_DV)
                        t.wait_ge(dv.h, QB_DV)
                    if s >= 1:
                        t.wait_ge(mm.h, s2_done_mm[s - 1])
                block.tensor(sl_pre)

                for prj in range(3):
                    for ccx in range(8):
                        def sl_mm(t, s=s, prj=prj, ccx=ccx):
                            gk = s * 24 + prj * 8 + ccx
                            if gk >= 2:
                                t.wait_ge(dv.h, pst1_ev.get(gk - 2, 0))
                            for dch in range(8):
                                base = dch * 3072 + prj * 1024 + ccx * 128
                                ins = t.matmul(pst1[gk % 2][:],
                                               lhsT=wU[:, base:base + 128],
                                               rhs=ring[s % 2][:, dch * 512:(dch + 1) * 512],
                                               start=(dch == 0), stop=(dch == 7))
                            s1mm[(prj, ccx)] = mm.inc(ins)
                            if prj == 2 and ccx == 7:
                                s1_done_mm[s] = mm.n
                        block.tensor(sl_mm)

                        def sl_ev(v, s=s, prj=prj, ccx=ccx):
                            gk = s * 24 + prj * 8 + ccx
                            v.wait_ge(mm.h, s1mm[(prj, ccx)])
                            v.tensor_copy(out=psb[prj][0:48, ccx * 512:(ccx + 1) * 512],
                                          in_=pst1[gk % 2][0:48, :])
                            pst1_ev[gk] = dv.inc(
                                v.tensor_copy(out=psb[prj][64:112, ccx * 512:(ccx + 1) * 512],
                                              in_=pst1[gk % 2][64:112, :]))
                        block.vector(sl_ev)

                S1EV = dv.n
                s2mm = {}

                def s2_pre(t, s=s):
                    t.wait_ge(dv.h, S1EV)
                block.tensor(s2_pre)

                for tt in range(4):
                    a = tt % 2
                    for prj in range(3):
                        def s2_mm(t, s=s, tt=tt, prj=prj):
                            gk = s * 12 + tt * 3 + prj
                            if gk >= 1:
                                t.wait_ge(dv.h, pst2_ev.get(gk - 1, 0))
                            for h in [0, 2, 4, 6, 8, 10, 12, 14, -1, 1, 3, 5, 7, 9, 11, 13, 15]:
                                if h < 0:
                                    t.drain()
                                    continue
                                base = (h // 2) * 512 + tt * 128
                                ins = t.matmul(
                                    pst2[h // 8][:, (h % 8) * 64:(h % 8 + 1) * 64],
                                    lhsT=psb[prj][(h % 2) * 64:(h % 2) * 64 + 48, base:base + 128],
                                    rhs=wVa[(h % 2) * 64:(h % 2) * 64 + 48,
                                            prj * 1024 + h * 64:prj * 1024 + (h + 1) * 64],
                                    start=True, stop=True)
                            t.drain()
                            s2mm[(tt, prj)] = mm.inc(ins)
                            if tt == 3 and prj == 2:
                                s2_done_mm[s] = mm.n
                        block.tensor(s2_mm)

                        def s2_ev(v, s=s, tt=tt, prj=prj, a=a):
                            gk = s * 12 + tt * 3 + prj
                            v.wait_ge(mm.h, s2mm[(tt, prj)])
                            gabs = s * 4 + tt
                            prev_user = (gabs, prj - 1) if prj >= 1 else (gabs - 2, 2)
                            if prev_user in asm_st:
                                W(v, asm_st[prev_user])
                            if prj < 2:
                                v.tensor_tensor(out=asm[a][:, 0:512], in0=pst2[0][:],
                                                in1=qb_bf[:, prj * 1024:prj * 1024 + 512], op=ALU.add)
                                ins = v.tensor_tensor(out=asm[a][:, 512:1024], in0=pst2[1][:],
                                                      in1=qb_bf[:, prj * 1024 + 512:prj * 1024 + 1024], op=ALU.add)
                            else:
                                v.tensor_copy(out=asm[a][:, 0:1040], in_=qb_bf[:, 2048:3088])
                                av = asm[a][:].rearrange("p (h c) -> p h c", c=65)
                                v.tensor_tensor(out=av[:, 0:8, 0:64], in0=pst2[0][:].rearrange("p (h d) -> p h d", d=64),
                                                in1=av[:, 0:8, 0:64], op=ALU.add)
                                ins = v.tensor_tensor(out=av[:, 8:16, 0:64], in0=pst2[1][:].rearrange("p (h d) -> p h d", d=64),
                                                      in1=av[:, 8:16, 0:64], op=ALU.add)
                            pst2_ev[gk] = dv.inc(ins)
                        block.vector(s2_ev)

                        def s2_st1(sync, s=s, tt=tt, prj=prj, a=a):
                            gk = s * 12 + tt * 3 + prj
                            sync.wait_ge(dv.h, pst2_ev[gk])
                            row0 = s * 512 + tt * 128
                            width = 1024 if prj < 2 else 1040
                            asm_st[(s * 4 + tt, prj)] = sA[a].rec(sync.dma_start(
                                out=qkv_loc[row0:row0 + 128, prj * 1024:prj * 1024 + width],
                                in_=asm[a][:, 0:width]), 16)
                        block.sync(s2_st1)

        COFFG = [None, None, None]

        def coll1(g):
            W(g, sA[0].fin())
            W(g, sA[1].fin())
            g.wait_ge(dCO.h, 16)
            r1 = g.alloc_register(name="coffg1")
            g.load(r1, co_sb[0:1, 1:2])
            COFFG[1] = RuntimeValue(r1, min_val=0, max_val=780)
            r2 = g.alloc_register(name="coffg2")
            g.load(r2, co_sb[0:1, 2:3])
            COFFG[2] = RuntimeValue(r2, min_val=0, max_val=6144)
            cc.inc(g.collective_compute("AllGather", ALU.bypass, replica_groups=rg,
                                        ins=[qkv_loc.opt()], outs=[qkv_g.opt()]))
        block.gpsimd(coll1)
        CC1 = cc.n

        p1.close()

        p15 = ExitStack()
        sb5 = lambda n_, s_, d_: p15.enter_context(nc.sbuf_tensor(n_, s_, d_))
        ps5 = lambda n_, s_, d_: p15.enter_context(nc.psum_tensor(n_, s_, d_))
        wfl = [sb5(f"wfl{i}", [128, NFFN], F32) for i in range(2)]
        wqs = sb5("wqs", [128, D], BF16)
        wds = sb5("wds", [128, 512], BF16)
        wqT = [sb5(f"wqT{i}", [128, 8 * 512], BF16) for i in range(2)]
        wdT = sb5("wdT", [128, 4 * 1024], BF16)
        hqn = sb5("hqn", [128, 4 * TOK], BF16)
        sil = sb5("sil", [128, 512], F32)
        hmt = sb5("hmt", [128, 512], F32)
        hqt = sb5("hqt", [128, 512], BF16)
        facc = sb5("facc", [128, 8 * TOK], F32)
        ptq = ps5("ptq", [128, 1024], BF16)
        pg = [ps5(f"pg{i}", [128, 512], F32) for i in range(2)]
        pu = [ps5(f"pu{i}", [128, 512], F32) for i in range(2)]
        pdn = [ps5(f"pdn{i}", [128, 512], F32) for i in range(2)]

        wjobs = []
        for nbi, (noff, nsz) in enumerate(NB_LIST):
            nch = nsz // 128
            for r_ in range(nch):
                wjobs.append(("g", nbi, r_, noff, nsz))
            for r_ in range(nch):
                wjobs.append(("u", nbi, r_, noff, nsz))
            for r_ in range(8):
                wjobs.append(("d", nbi, r_, noff, nsz))
        wj_ld = {}
        wj_q = {}
        wj_tr = {}
        wj_ev = {}
        ptq_free = [0]
        gu_done_mm = {}
        dn_done_mm = {}
        gu_mm = {}
        sil_ac = {}
        mul_dv = {}
        hq_tr = {}
        hq_ev = {}
        dn_mm = {}
        dn_ev = {}

        def emit_wjob(k):
            kind, nbi, r_, noff, nsz = wjobs[k]
            nch = nsz // 128

            def wj_sync(sync, k=k, kind=kind, nbi=nbi, r_=r_, noff=noff, nsz=nsz):
                if k < 2:
                    sync.wait_ge(mm.h, s2_done_mm[3])
                    W(sync, sA[0].fin())
                    W(sync, sA[1].fin())
                    W(sync, sX.fin())
                if k >= 2:
                    sync.wait_ge(dv.h, wj_q[k - 2])
                if kind == "g":
                    src = wgwu[noff + r_ * 128:noff + (r_ + 1) * 128, :]
                    w = D
                elif kind == "u":
                    src = wgwu[NFFN + noff + r_ * 128:NFFN + noff + (r_ + 1) * 128, :]
                    w = D
                else:
                    src = wdn[r_ * 128:(r_ + 1) * 128, noff:noff + nsz]
                    w = nsz
                wj_ld[k] = dW[k % 2].rec(sync.dma_start(out=wfl[k % 2][:, 0:w], in_=src), 16)
            block.sync(wj_sync)

            def wj_v(v, k=k, kind=kind, nsz=nsz):
                W(v, wj_ld[k])
                if k >= 1:
                    v.wait_ge(mm.h, wj_tr[k - 1])
                if kind == "d":
                    wj_q[k] = quant(v, wfl[k % 2][:, 0:nsz], wds[:, 0:nsz], nsz)
                else:
                    wj_q[k] = quant(v, wfl[k % 2][:, 0:D], wqs[:], D)
            block.vector(wj_v)

            def wj_t(t, k=k, kind=kind, nsz=nsz):
                t.wait_ge(dv.h, wj_q[k])
                t.wait_ge(dv.h, ptq_free[0])
                nblk = (nsz // 128) if kind == "d" else 8
                srcb = wds if kind == "d" else wqs
                for b_ in range(nblk):
                    ins = t.transpose(ptq[:, b_ * 128:(b_ + 1) * 128],
                                      in_=srcb[:, b_ * 128:(b_ + 1) * 128], identity=idb[:])
                wj_tr[k] = mm.inc(ins)
            block.tensor(wj_t)

            def wj_e(v, k=k, kind=kind, nbi=nbi, r_=r_, nsz=nsz):
                v.wait_ge(mm.h, wj_tr[k])
                if nbi >= 1:
                    if kind in ("g", "u"):
                        v.wait_ge(mm.h, gu_done_mm[nbi - 1])
                    else:
                        v.wait_ge(mm.h, dn_done_mm[nbi - 1])
                nblk = (nsz // 128) if kind == "d" else 8
                for b_ in range(nblk):
                    if kind == "d":
                        dst = wdT[:, b_ * 1024 + r_ * 128:b_ * 1024 + (r_ + 1) * 128]
                    else:
                        wi = 0 if kind == "g" else 1
                        dst = wqT[wi][:, b_ * nsz + r_ * 128:b_ * nsz + (r_ + 1) * 128]
                    ins = v.tensor_copy(out=dst, in_=ptq[:, b_ * 128:(b_ + 1) * 128])
                wj_ev[k] = dv.inc(ins)
                ptq_free[0] = wj_ev[k]
            block.vector(wj_e)

        kbase = 0
        for nbi, (noff, nsz) in enumerate(NB_LIST):
            nch = nsz // 128
            njobs = 2 * nch + 8
            for k in range(kbase, kbase + njobs):
                emit_wjob(k)
            WJ_EV_NB = dv.n

            for tt in range(NT):
                def g_mm(t, nbi=nbi, nsz=nsz, tt=tt):
                    if tt == 0:
                        t.wait_ge(dv.h, WJ_EV_NB)
                        if nbi == 0:
                            t.wait_ge(dv.h, ev2_dv[NT - 1])
                    pp = tt % 2
                    if tt >= 2:
                        t.wait_ge(ac.h, sil_ac[(nbi, tt - 2)])
                        t.wait_ge(dv.h, mul_dv[(nbi, tt - 2)])
                    for kc in range(8):
                        t.matmul(pg[pp][:, 0:nsz],
                                 lhsT=xqfm[:].rearrange("p (c t) -> p c t", t=TOK)[:, kc, tt * 128:(tt + 1) * 128],
                                 rhs=wqT[0][:, kc * nsz:(kc + 1) * nsz],
                                 start=(kc == 0), stop=(kc == 7))
                    for kc in range(8):
                        ins = t.matmul(pu[pp][:, 0:nsz],
                                       lhsT=xqfm[:].rearrange("p (c t) -> p c t", t=TOK)[:, kc, tt * 128:(tt + 1) * 128],
                                       rhs=wqT[1][:, kc * nsz:(kc + 1) * nsz],
                                       start=(kc == 0), stop=(kc == 7))
                    gu_mm[(nbi, tt)] = mm.inc(ins)
                    if tt == NT - 1:
                        gu_done_mm[nbi] = mm.n
                block.tensor(g_mm)

                def g_sc(sc, nbi=nbi, tt=tt, nsz=nsz):
                    sc.wait_ge(mm.h, gu_mm[(nbi, tt)])
                    if tt >= 1:
                        sc.wait_ge(dv.h, mul_dv[(nbi, tt - 1)])
                    sil_ac[(nbi, tt)] = ac.inc(
                        sc.activation(out=sil[:, 0:nsz], in_=pg[tt % 2][:, 0:nsz], func=AF.Silu))
                block.scalar(g_sc)

                def g_v(v, nbi=nbi, tt=tt, nsz=nsz):
                    v.wait_ge(ac.h, sil_ac[(nbi, tt)])
                    if tt >= 1:
                        v.wait_ge(mm.h, hq_tr[(nbi, tt - 1)])
                    dv.inc(v.tensor_tensor(out=hmt[:, 0:nsz], in0=sil[:, 0:nsz],
                                           in1=pu[tt % 2][:, 0:nsz], op=ALU.mult))
                    v.wait_ge(dv.h, dv.n)
                    mul_dv[(nbi, tt)] = quant(v, hmt[:, 0:nsz], hqt[:, 0:nsz], nsz)
                block.vector(g_v)

                def h_t(t, nbi=nbi, tt=tt, nsz=nsz):
                    t.wait_ge(dv.h, mul_dv[(nbi, tt)])
                    t.wait_ge(dv.h, ptq_free[0])
                    for c in range(nsz // 128):
                        ins = t.transpose(ptq[:, c * 128:(c + 1) * 128],
                                          in_=hqt[:, c * 128:(c + 1) * 128], identity=idb[:])
                    hq_tr[(nbi, tt)] = mm.inc(ins)
                block.tensor(h_t)

                def h_e(v, nbi=nbi, tt=tt, nsz=nsz):
                    v.wait_ge(mm.h, hq_tr[(nbi, tt)])
                    if nbi >= 1 and tt == 0:
                        v.wait_ge(mm.h, dn_done_mm[nbi - 1])
                    ins = v.tensor_copy(
                        out=hqn[:, 0:(nsz // 128) * TOK].rearrange("p (c t) -> p c t", t=TOK)[:, :, tt * 128:(tt + 1) * 128],
                        in_=ptq[:, 0:nsz].rearrange("p (c t) -> p c t", t=128))
                    hq_ev[(nbi, tt)] = dv.inc(ins)
                    ptq_free[0] = hq_ev[(nbi, tt)]
                block.vector(h_e)

            def d_pre(t, nbi=nbi):
                t.wait_ge(dv.h, hq_ev[(nbi, NT - 1)])
            block.tensor(d_pre)

            for tc in range(4):
                for dc in range(8):
                    def d_mm(t, nbi=nbi, nsz=nsz, nch=nch, tc=tc, dc=dc):
                        gk = nbi * 32 + tc * 8 + dc
                        if gk >= 2:
                            t.wait_ge(dv.h, dn_ev.get(gk - 2, 0))
                        for c in range(nch):
                            ins = t.matmul(
                                pdn[gk % 2][:],
                                lhsT=wdT[:, c * 1024 + dc * 128:c * 1024 + (dc + 1) * 128],
                                rhs=hqn[:, 0:nch * TOK].rearrange("p (cc t) -> p cc t", t=TOK)[:, c, tc * 512:(tc + 1) * 512],
                                start=(c == 0), stop=(c == nch - 1))
                        dn_mm[gk] = mm.inc(ins)
                        if tc == 3 and dc == 7:
                            dn_done_mm[nbi] = mm.n
                    block.tensor(d_mm)

                    def d_ev(v, nbi=nbi, tc=tc, dc=dc):
                        gk = nbi * 32 + tc * 8 + dc
                        v.wait_ge(mm.h, dn_mm[gk])
                        dst = facc[:].rearrange("p (c t) -> p c t", t=TOK)[:, dc, tc * 512:(tc + 1) * 512]
                        if nbi == 0:
                            ins = v.tensor_copy(out=dst, in_=pdn[gk % 2][:])
                        else:
                            ins = v.tensor_tensor(out=dst, in0=dst, in1=pdn[gk % 2][:], op=ALU.add)
                        dn_ev[gk] = dv.inc(ins)
                    block.vector(d_ev)
            kbase += njobs

        FFN_CAST = [0]

        def f_cast2(v):
            v.wait_ge(mm.h, dn_done_mm[len(NB_LIST) - 1])
            for c in range(4):
                ins = v.tensor_copy(out=hqn[:, c * TOK:(c + 1) * TOK],
                                    in_=facc[:, c * TOK:(c + 1) * TOK])
            FFN_CAST[0] = dv.inc(ins)
        block.vector(f_cast2)

        def f_st1(sync):
            sync.wait_ge(dv.h, FFN_CAST[0])
            sF.inc(sync.dma_start(out=ffn_sp[:, 0:4 * TOK], in_=hqn[:, 0:4 * TOK]), 16)
        block.sync(f_st1)
        FFN_ST1 = sF.fin()

        def f_cast3(v):
            W(v, FFN_ST1)
            for c in range(4):
                ins = v.tensor_copy(out=hqn[:, c * TOK:(c + 1) * TOK],
                                    in_=facc[:, (4 + c) * TOK:(5 + c) * TOK])
            FFN_CAST[0] = dv.inc(ins)
        block.vector(f_cast3)

        def f_st2(sync):
            sync.wait_ge(dv.h, FFN_CAST[0])
            sF.inc(sync.dma_start(out=ffn_sp[:, 4 * TOK:8 * TOK], in_=hqn[:, 0:4 * TOK]), 16)
        block.sync(f_st2)
        FFN_ST = sF.fin()

        P15_DV_END = dv.n
        P15_MM_END = mm.n

        p15.close()
        p1x.close()

        p2 = ExitStack()
        sb2 = lambda n_, s_, d_: p2.enter_context(nc.sbuf_tensor(n_, s_, d_))
        ps2 = lambda n_, s_, d_: p2.enter_context(nc.psum_tensor(n_, s_, d_))
        Qfm = [[sb2(f"Qfm{u}_{l}", [64, S], BF16) for l in range(2)] for u in range(2)]
        Kfm = [[sb2(f"Kfm{u}_{l}", [64, S], BF16) for l in range(2)] for u in range(2)]
        VT = [sb2(f"VT{u}", [128, 64 * 130], BF16) for u in range(2)]
        Et = [sb2(f"Et{i}", [128, 1024], BF16) for i in range(2)]
        rrow = sb2("rrow", [1, 1024], F32)
        rb = sb2("rb", [64, 1024], F32)
        cta = sb2("cta", [128, 512], BF16)

        unit_ld = [0, 0]

        p2p = ExitStack()
        qkst = [p2p.enter_context(nc.sbuf_tensor(f"qkst{i}", [128, 4096], BF16)) for i in range(2)]
        ptk = [p2p.enter_context(nc.psum_tensor(f"ptk{i}", [128, 1024], BF16)) for i in range(2)]
        qk_ld = {}
        qk_tr = {}
        qk_ev = {}
        gjob = 0
        for u in range(2):
            for ten in range(2):
                for hf in range(2):
                    def qk_sync(sync, u=u, ten=ten, hf=hf, gjob=gjob):
                        cbase = ten * 1024 + u * 128
                        if gjob >= 2:
                            sync.wait_ge(dv.h, qk_ev.get((gjob - 2, 3), 0))
                        if gjob == 0:
                            sync.wait_ge(cc.h, CC1)
                            sync.wait_ge(dv.h, P15_DV_END)
                            sync.wait_ge(mm.h, P15_MM_END)
                            W(sync, sF.fin())
                        qk_ld[gjob] = dQ[gjob % 2].rec(sync.dma_start(
                            out=qkst[gjob % 2][:].rearrange("p (a c) -> p a c", c=128),
                            in_=qkv_g.rearrange("(a p) c -> p a c", p=128)[:, hf * 32:(hf + 1) * 32][:, :, cbase:cbase + 896 + 128][:, :, ds(COFF[0], 128)]), 16)
                    block.sync(qk_sync)

                    for rd in range(4):
                        def qk_t(t, gjob=gjob, rd=rd):
                            W(t, qk_ld[gjob])
                            prev = (gjob, rd - 2) if rd >= 2 else ((gjob - 1, rd + 2) if gjob else None)
                            if prev is not None:
                                t.wait_ge(dv.h, qk_ev.get(prev, 0))
                            for c in range(8):
                                ins = t.transpose(ptk[rd % 2][:, c * 128:(c + 1) * 128],
                                                  in_=qkst[gjob % 2][:, rd * 1024 + c * 128:rd * 1024 + (c + 1) * 128],
                                                  identity=idb[:])
                            qk_tr[(gjob, rd)] = mm.inc(ins)
                        block.tensor(qk_t)

                        def qk_e(v, u=u, ten=ten, hf=hf, gjob=gjob, rd=rd):
                            v.wait_ge(mm.h, qk_tr[(gjob, rd)])
                            dst = (Qfm if ten == 0 else Kfm)[u]
                            a0 = hf * 4 + rd
                            v.tensor_copy(out=dst[0][:, a0 * 1024:(a0 + 1) * 1024],
                                          in_=ptk[rd % 2][0:64, :])
                            qk_ev[(gjob, rd)] = dv.inc(v.tensor_copy(
                                out=dst[1][:, a0 * 1024:(a0 + 1) * 1024], in_=ptk[rd % 2][64:128, :]))
                        block.vector(qk_e)
                    gjob += 1
        QK_DV = dv.n

        def p2_vld(g):
            g.wait_ge(dv.h, P15_DV_END)
            g.wait_ge(mm.h, P15_MM_END)
            for u in range(2):
                vbase = VCOLS + u * 130
                for hf in range(2):
                    dV[u].inc(g.dma_start(
                        out=VT[u][:].rearrange("p (kt c) -> p kt c", c=130)[:, hf * 32:(hf + 1) * 32, :],
                        in_=qkv_g.rearrange("(kt p) c -> p kt c", p=128)[:, hf * 32:(hf + 1) * 32][:, :, vbase:vbase + 780 + 130][:, :, ds(COFF[1], 130)]), 16)
                unit_ld[u] = dV[u].fin()
        block.sync(p2_vld)
        p2p.close()

        pS = [ps2(f"pS{i}", [128, 1024], F32) for i in range(2)]
        pc0 = ps2("pc0", [65, 512], F32)
        pc1 = ps2("pc1", [65, 512], F32)
        pbc = ps2("pbc", [64, 1024], F32)

        exp_rec = {}
        scm_rec = {}
        avm_rec = {}
        msk_rec = {}
        ctx_free = [dv.n]
        cta_st = [None]
        gi = 0
        for u in range(2):
            for qc in range(16):
                nkt = 4 * qc + 4
                gqc0 = gi
                for kt in range(nkt):
                    def a_mm(t, u=u, qc=qc, kt=kt, nkt=nkt, gi=gi):
                        if gi == 0:
                            W(t, unit_ld[0])
                            t.wait_ge(dv.h, QK_DV)
                        if u == 1 and qc == 0 and kt == 0:
                            W(t, unit_ld[1])
                        if gi >= 2:
                            t.wait_ge(ac.h, exp_rec[gi - 2])
                        t.matmul(pS[gi % 2][:, 0:512],
                                 lhsT=Kfm[u][0][:, kt * 128:(kt + 1) * 128],
                                 rhs=Qfm[u][0][:, qc * 512:(qc + 1) * 512], start=True, stop=True)
                        ins = t.matmul(pS[gi % 2][:, 512:1024],
                                       lhsT=Kfm[u][1][:, kt * 128:(kt + 1) * 128],
                                       rhs=Qfm[u][1][:, qc * 512:(qc + 1) * 512], start=True, stop=True)
                        scm_rec[gi] = mm.inc(ins)
                        if kt >= 1:
                            gip = gi - 1
                            t.wait_ge(ac.h, exp_rec[gip])
                            if kt - 1 == 0:
                                t.wait_ge(dv.h, ctx_free[0])
                            t.matmul(pc0[:], lhsT=VT[u][:, (kt - 1) * 130:(kt - 1) * 130 + 65],
                                     rhs=Et[gip % 2][:, 0:512], start=(kt - 1 == 0), stop=(kt - 1 == nkt - 1))
                            ins = t.matmul(pc1[:], lhsT=VT[u][:, (kt - 1) * 130 + 65:(kt - 1) * 130 + 130],
                                           rhs=Et[gip % 2][:, 512:1024], start=(kt - 1 == 0), stop=(kt - 1 == nkt - 1))
                            avm_rec[gip] = mm.inc(ins)
                    block.tensor(a_mm)

                    if kt >= 4 * qc:
                        def a_msk(v, qc=qc, kt=kt, gi=gi):
                            v.wait_ge(mm.h, scm_rec[gi])
                            mloc = kt - 4 * qc
                            mview = cm[:, mloc * 512:(mloc + 1) * 512].unsqueeze(1).broadcast_to([128, 2, 512])
                            msk_rec[gi] = dv.inc(v.tensor_tensor(
                                out=pS[gi % 2][:].rearrange("p (b q) -> p b q", q=512),
                                in0=pS[gi % 2][:].rearrange("p (b q) -> p b q", q=512),
                                in1=mview, op=ALU.add))
                        block.vector(a_msk)

                    def a_exp(sc, gi=gi, diag=(kt >= 4 * qc)):
                        sc.wait_ge(mm.h, scm_rec[gi])
                        if diag:
                            sc.wait_ge(dv.h, msk_rec[gi])
                        if gi >= 2:
                            sc.wait_ge(mm.h, avm_rec[gi - 2])
                        exp_rec[gi] = ac.inc(sc.activation(out=Et[gi % 2][:], in_=pS[gi % 2][:],
                                                           func=AF.Exp, scale=0.125))
                    block.scalar(a_exp)
                    gi += 1

                def a_avl(t, u=u, qc=qc, nkt=nkt, gi=gi):
                    gip = gi - 1
                    t.wait_ge(ac.h, exp_rec[gip])
                    if nkt == 1:
                        t.wait_ge(dv.h, ctx_free[0])
                    kt = nkt - 1
                    t.matmul(pc0[:], lhsT=VT[u][:, kt * 130:kt * 130 + 65],
                             rhs=Et[gip % 2][:, 0:512], start=(kt == 0), stop=True)
                    ins = t.matmul(pc1[:], lhsT=VT[u][:, kt * 130 + 65:kt * 130 + 130],
                                   rhs=Et[gip % 2][:, 512:1024], start=(kt == 0), stop=True)
                    avm_rec[gip] = mm.inc(ins)
                block.tensor(a_avl)

                rr_dv = [0]

                def a_rr(v, gi=gi):
                    v.wait_ge(mm.h, avm_rec[gi - 1])
                    dv.inc(v.reciprocal(out=rrow[0:1, 0:512], in_=pc0[64:65, :]))
                    dv.inc(v.reciprocal(out=rrow[0:1, 512:1024], in_=pc1[64:65, :]))
                    v.wait_ge(dv.h, dv.n)
                    rr_dv[0] = dv.n
                block.vector(a_rr)

                bc_mm = [0]

                def a_bc(t):
                    t.wait_ge(dv.h, rr_dv[0])
                    t.matmul(pbc[:, 0:512], lhsT=of32[0:1, 0:64], rhs=rrow[0:1, 0:512],
                             start=True, stop=True)
                    ins = t.matmul(pbc[:, 512:1024], lhsT=of32[0:1, 0:64], rhs=rrow[0:1, 512:1024],
                                   start=True, stop=True)
                    bc_mm[0] = mm.inc(ins)
                block.tensor(a_bc)

                def a_nrm(v, u=u, qc=qc):
                    v.wait_ge(mm.h, bc_mm[0])
                    if cta_st[0] is not None:
                        W(v, cta_st[0])
                    dv.inc(v.tensor_copy(out=rb[:], in_=pbc[:]))
                    v.wait_ge(dv.h, dv.n)
                    v.tensor_tensor(out=cta[0:64, :], in0=pc0[0:64, :], in1=rb[:, 0:512], op=ALU.mult)
                    ctx_free[0] = dv.inc(v.tensor_tensor(out=cta[64:128, :], in0=pc1[0:64, :],
                                                         in1=rb[:, 512:1024], op=ALU.mult))
                block.vector(a_nrm)

                def a_st(sync, u=u, qc=qc):
                    sync.wait_ge(dv.h, ctx_free[0])
                    cta_st[0] = sC.rec(sync.dma_start(out=ctx_loc[u * 128:(u + 1) * 128, qc * 512:(qc + 1) * 512],
                                          in_=cta[:]), 16)
                block.sync(a_st)

        def coll2(g):
            W(g, sC.fin())
            cc.inc(g.collective_compute("AllGather", ALU.bypass, replica_groups=rg,
                                        ins=[ctx_loc.opt()], outs=[ctx_g.opt()]))
        block.gpsimd(coll2)
        CC2 = cc.n

        p2.close()

        p3 = ExitStack()
        sb3 = lambda n_, s_, d_: p3.enter_context(nc.sbuf_tensor(n_, s_, d_))
        hbf = sb3("hbf", [128, 8 * TOK], BF16)
        wld3 = [sb3(f"wld3_{i}", [128, 1024], F32) for i in range(2)]
        a3r = sb3("a3r", [1, 2048], F32)
        c3r = sb3("c3r", [1, 2048], F32)
        pA = [p3.enter_context(nc.psum_tensor(f"pA{i}", [128, 512], F32)) for i in range(2)]
        pB = [p3.enter_context(nc.psum_tensor(f"pB{i}", [128, 512], F32)) for i in range(2)]

        p3a = ExitStack()
        sa = lambda n_, s_, d_: p3a.enter_context(nc.sbuf_tensor(n_, s_, d_))
        xnr = sa("xnr", [128, 8 * TOK], BF16)
        ctx_t = [sa(f"ctx_t{i}", [128, 8 * 512], BF16) for i in range(2)]
        oU = sa("oU", [128, 8 * 768], BF16)
        oV = sa("oV", [128, 6 * 1024], BF16)
        t1s = sa("t1s", [128, 6 * 512], BF16)
        ao = sa("ao", [128, 512], F32)

        XNR_LD = [0]

        def p3a_xn(sync):
            sync.wait_ge(cc.h, CC2)
            W(sync, sX.fin())
            XNR_LD[0] = dXN.rec(sync.dma_start(out=xnr[:], in_=xn_sp[:, :]), 16)
        block.sync(p3a_xn)

        ocast = {}

        def emit_wcast(i, src_ap, w, dst_ap):
            def ow_sync(sync, i=i):
                if i >= 2:
                    sync.wait_ge(dv.h, ocast[i - 2])
                ocast[(i, "ld")] = dW3.rec(sync.dma_start(out=wld3[i % 2][:, 0:w], in_=src_ap), 16)
            block.sync(ow_sync)

            def ow_vec(v, i=i):
                W(v, ocast[(i, "ld")])
                ocast[i] = dv.inc(v.tensor_copy(out=dst_ap, in_=wld3[i % 2][:, 0:w]))
            block.vector(ow_vec)

        def p3a_gate(sync):
            sync.wait_ge(cc.h, CC2)
        block.sync(p3a_gate)

        wi = 0
        for i in range(8):
            emit_wcast(wi, outU[i * 128:(i + 1) * 128, :], 768, oU[:, i * 768:(i + 1) * 768])
            wi += 1
        for i in range(6):
            emit_wcast(wi, outV[i * 128:(i + 1) * 128, :], 1024, oV[:, i * 1024:(i + 1) * 1024])
            wi += 1
        OW_DV = dv.n

        ctx_ld = [0] * 4
        t1_mm = {}
        t1_ev = {}
        t2_mm = {}
        h_dv = {}
        HSC = {}

        for tc in range(4):
            def c_sync(g, tc=tc):
                if tc == 0:
                    g.wait_ge(cc.h, CC2)
                if tc >= 2:
                    g.wait_ge(mm.h, t1_mm.get((tc - 2, 5), 0))
                ctx_ld[tc] = dT[tc % 2].rec(g.dma_start(
                    out=ctx_t[tc % 2][:].rearrange("p (dch t) -> p dch t", t=512),
                    in_=ctx_g.rearrange("(dch p) t -> p dch t", p=128)[:, :, tc * 512:tc * 512 + 6144 + 512][:, :, ds(COFF[2], 512)]), 16)
            block.sync(c_sync)

            def t1_pre(t, tc=tc):
                W(t, ctx_ld[tc])
                if tc == 0:
                    t.wait_ge(dv.h, OW_DV)
            block.tensor(t1_pre)

            for cc6 in range(6):
                def t1_f(t, tc=tc, cc6=cc6):
                    gk = tc * 6 + cc6
                    if gk >= 2:
                        t.wait_ge(dv.h, t1_ev.get(gk - 2, 0))
                    for dch in range(8):
                        ins = t.matmul(pA[gk % 2][:],
                                       lhsT=oU[:, dch * 768 + cc6 * 128:dch * 768 + (cc6 + 1) * 128],
                                       rhs=ctx_t[tc % 2][:, dch * 512:(dch + 1) * 512],
                                       start=(dch == 0), stop=(dch == 7))
                    t1_mm[(tc, cc6)] = mm.inc(ins)
                block.tensor(t1_f)

                def t1_e(v, tc=tc, cc6=cc6):
                    gk = tc * 6 + cc6
                    v.wait_ge(mm.h, t1_mm[(tc, cc6)])
                    t1_ev[gk] = dv.inc(v.tensor_copy(out=t1s[:, cc6 * 512:(cc6 + 1) * 512],
                                                     in_=pA[gk % 2][:]))
                block.vector(t1_e)
            T1E_DV = dv.n

            for dch in range(8):
                def t2_f2(t, tc=tc, dch=dch):
                    if dch == 0:
                        t.wait_ge(dv.h, T1E_DV)
                    gk = tc * 8 + dch
                    if gk >= 2:
                        t.wait_ge(dv.h, h_dv.get(gk - 2, 0))
                    for rc in range(6):
                        ins = t.matmul(pB[gk % 2][:],
                                       lhsT=oV[:, rc * 1024 + dch * 128:rc * 1024 + (dch + 1) * 128],
                                       rhs=t1s[:, rc * 512:(rc + 1) * 512],
                                       start=(rc == 0), stop=(rc == 5))
                    t2_mm[(tc, dch)] = mm.inc(ins)
                block.tensor(t2_f2)
            for dch in range(8):
                def h_dve(v, tc=tc, dch=dch):
                    v.wait_ge(mm.h, t2_mm[(tc, dch)])
                    if tc == 0 and dch == 0:
                        W(v, XNR_LD[0])
                    gk = tc * 8 + dch
                    v.tensor_scalar(out=ao[:], in0=pB[gk % 2][:],
                                    scalar1=ob_pp[:, dch:dch + 1], scalar2=None, op0=ALU.add)
                    h_dv[gk] = dv.inc(v.tensor_tensor(
                        out=hbf[:].rearrange("p (c t) -> p c t", t=TOK)[:, dch, tc * 512:(tc + 1) * 512],
                        in0=ao[:],
                        in1=xnr[:].rearrange("p (c t) -> p c t", t=TOK)[:, dch, tc * 512:(tc + 1) * 512],
                        op=ALU.add))
                block.vector(h_dve)
        H_DONE_DV = dv.n

        p3a.close()

        p3b = ExitStack()
        sbb = lambda n_, s_, d_: p3b.enter_context(nc.sbuf_tensor(n_, s_, d_))
        h2t = sbb("h2t", [128, 512], BF16)
        srow = sbb("srow", [1, 512], F32)
        trow = sbb("trow", [1, 512], F32)
        pR = [p3b.enter_context(nc.psum_tensor(f"pR{i}", [1, 512], F32)) for i in range(2)]

        sq_ac = {}
        st_mm = {}
        row_dv = {}
        row_ac = {}

        for tc in range(4):
            for dch in range(8):
                def b_sc(sc, tc=tc, dch=dch):
                    if tc == 0 and dch == 0:
                        sc.wait_ge(dv.h, H_DONE_DV)
                    if dch == 0 and tc >= 1:
                        sc.wait_ge(dv.h, row_dv.get(tc - 1, 0))
                    if dch >= 1:
                        sc.wait_ge(mm.h, st_mm.get((tc, dch - 1), 0))
                    sq_ac[(tc, dch)] = ac.inc(sc.activation(
                        out=h2t[:],
                        in_=hbf[:].rearrange("p (c t) -> p c t", t=TOK)[:, dch, tc * 512:(tc + 1) * 512],
                        func=AF.Square))
                block.scalar(b_sc)

                def b_mm(t, tc=tc, dch=dch):
                    t.wait_ge(ac.h, sq_ac[(tc, dch)])
                    t.matmul(pR[0][:],
                             lhsT=ones1[:],
                             rhs=hbf[:].rearrange("p (c t) -> p c t", t=TOK)[:, dch, tc * 512:(tc + 1) * 512],
                             start=(dch == 0), stop=(dch == 7))
                    ins = t.matmul(pR[1][:], lhsT=ones1[:], rhs=h2t[:],
                                   start=(dch == 0), stop=(dch == 7))
                    st_mm[(tc, dch)] = mm.inc(ins)
                block.tensor(b_mm)

            def b_row(v, tc=tc):
                v.wait_ge(mm.h, st_mm[(tc, 7)])
                v.tensor_scalar(out=srow[:], in0=pR[0][:], scalar1=1.0 / D, scalar2=None, op0=ALU.mult)
                dv.inc(v.tensor_scalar(out=trow[:], in0=pR[1][:], scalar1=1.0 / D, scalar2=None, op0=ALU.mult))
                v.wait_ge(dv.h, dv.n)
                dv.inc(v.scalar_tensor_tensor(out=c3r[0:1, tc * 512:(tc + 1) * 512], in0=srow[:], scalar=1.0,
                                              in1=srow[:], op0=ALU.mult, op1=ALU.mult))
                v.wait_ge(dv.h, dv.n)
                dv.inc(v.tensor_sub(out=trow[:], in0=trow[:], in1=c3r[0:1, tc * 512:(tc + 1) * 512]))
                v.wait_ge(dv.h, dv.n)
                row_dv[tc] = dv.inc(v.tensor_scalar(out=trow[:], in0=trow[:], scalar1=EPS,
                                                    scalar2=None, op0=ALU.add))
            block.vector(b_row)

            def b_rs(sc, tc=tc):
                sc.wait_ge(dv.h, row_dv[tc])
                sc.activation(out=a3r[0:1, tc * 512:(tc + 1) * 512], in_=trow[:], func=AF.Ln)
                row_ac[tc] = ac.inc(sc.activation(out=a3r[0:1, tc * 512:(tc + 1) * 512],
                                                  in_=a3r[0:1, tc * 512:(tc + 1) * 512],
                                                  func=AF.Exp, scale=-0.5))
            block.scalar(b_rs)

            def b_c3(v, tc=tc):
                v.wait_ge(ac.h, row_ac[tc])
                row_dv[tc] = dv.inc(v.scalar_tensor_tensor(
                    out=c3r[0:1, tc * 512:(tc + 1) * 512], in0=srow[:], scalar=-1.0,
                    in1=a3r[0:1, tc * 512:(tc + 1) * 512], op0=ALU.mult, op1=ALU.mult))
            block.vector(b_c3)
        STATS_DV = dv.n
        STATS_MM = mm.n
        p3b.close()

        p3c = ExitStack()
        scx = lambda n_, s_, d_: p3c.enter_context(nc.sbuf_tensor(n_, s_, d_))
        f1U = scx("f1U", [128, 8 * 512], BF16)
        f1V = scx("f1V", [128, 4 * 4096], BF16)
        f2U = scx("f2U", [128, 32 * 512], BF16)
        f2V = scx("f2V", [128, 4 * 1024], BF16)
        lnt = scx("lnt", [128, 8 * 512], BF16)
        t34 = scx("t34", [128, 4 * 512], BF16)
        hmid = scx("hmid", [128, 32 * 512], BF16)
        u1s = scx("u1s", [128, 512], F32)
        bb3 = scx("bb3", [128, 1024], F32)
        ao3 = scx("ao3", [128, 512], F32)
        pBC = p3c.enter_context(nc.psum_tensor("pBC", [128, 1024], F32))

        fcast = {}

        def emit_fw(i, src_ap, w, dst_ap):
            def fw_sync(sync, i=i):
                if i >= 2:
                    sync.wait_ge(dv.h, fcast[i - 2])
                else:
                    sync.wait_ge(dv.h, OW_DV)
                fcast[(i, "ld")] = dW3.rec(sync.dma_start(out=wld3[i % 2][:, 0:w], in_=src_ap), 16)
            block.sync(fw_sync)

            def fw_vec(v, i=i):
                W(v, fcast[(i, "ld")])
                fcast[i] = dv.inc(v.tensor_copy(out=dst_ap, in_=wld3[i % 2][:, 0:w]))
            block.vector(fw_vec)

        def p3c_gate(sync):
            sync.wait_ge(mm.h, STATS_MM)
            sync.wait_ge(dv.h, STATS_DV)
        block.sync(p3c_gate)

        fwi = 0
        for i in range(8):
            emit_fw(fwi, fc1U[i * 128:(i + 1) * 128, :], 512, f1U[:, i * 512:(i + 1) * 512])
            fwi += 1
        for i in range(4):
            for qr in range(4):
                emit_fw(fwi, fc1V[i * 128:(i + 1) * 128, qr * 1024:(qr + 1) * 1024], 1024,
                        f1V[:, i * 4096 + qr * 1024:i * 4096 + (qr + 1) * 1024])
                fwi += 1
        for i in range(32):
            emit_fw(fwi, fc2U[i * 128:(i + 1) * 128, :], 512, f2U[:, i * 512:(i + 1) * 512])
            fwi += 1
        for i in range(4):
            emit_fw(fwi, fc2V[i * 128:(i + 1) * 128, :], 1024, f2V[:, i * 1024:(i + 1) * 1024])
            fwi += 1
        FW_DV = dv.n

        bc3_mm = {}
        ln_dv = {}
        ln_ac = {}
        t3_mm = {}
        t3_ev = {}
        hm_mm = {}
        hm_ac = {}
        t4_mm = {}
        t4_ev = {}
        o5_mm = {}
        o5_ac = {}
        blk_dv = {}

        for tc in range(4):
            def m_bc(t, tc=tc):
                if tc == 0:
                    t.wait_ge(dv.h, STATS_DV)
                if tc >= 1:
                    t.wait_ge(dv.h, ln_dv.get((tc - 1, 7), 0))
                t.matmul(pBC[:, 0:512], lhsT=of32[0:1, 0:128], rhs=a3r[0:1, tc * 512:(tc + 1) * 512],
                         start=True, stop=True)
                ins = t.matmul(pBC[:, 512:1024], lhsT=of32[0:1, 0:128], rhs=c3r[0:1, tc * 512:(tc + 1) * 512],
                               start=True, stop=True)
                bc3_mm[tc] = mm.inc(ins)
            block.tensor(m_bc)

            def m_cp(v, tc=tc):
                v.wait_ge(mm.h, bc3_mm[tc])
                dv.inc(v.tensor_copy(out=bb3[:], in_=pBC[:]))
                v.wait_ge(dv.h, dv.n)
                bc3_mm[(tc, "cp")] = dv.n
            block.vector(m_cp)

            for dch in range(8):
                def m_ln(v, tc=tc, dch=dch):
                    v.wait_ge(dv.h, bc3_mm[(tc, "cp")])
                    prev = (tc, dch - 1) if dch else ((tc - 1, 7) if tc else None)
                    if prev is not None:
                        v.wait_ge(ac.h, ln_ac[prev])
                    v.tensor_mul(out=u1s[:],
                                 in0=hbf[:].rearrange("p (c t) -> p c t", t=TOK)[:, dch, tc * 512:(tc + 1) * 512],
                                 in1=bb3[:, 0:512])
                    ln_dv[(tc, dch)] = dv.inc(v.tensor_add(out=u1s[:], in0=u1s[:], in1=bb3[:, 512:1024]))
                block.vector(m_ln)

                def m_lna(sc, tc=tc, dch=dch):
                    sc.wait_ge(dv.h, ln_dv[(tc, dch)])
                    ln_ac[(tc, dch)] = ac.inc(sc.activation(
                        out=lnt[:, dch * 512:(dch + 1) * 512], in_=u1s[:], func=AF.Identity,
                        scale=l2g[:, dch:dch + 1], bias=l2b[:, dch:dch + 1]))
                block.scalar(m_lna)
            LN_AC = ac.n

            for rc in range(4):
                def m_t3(t, tc=tc, rc=rc):
                    if rc == 0:
                        t.wait_ge(ac.h, LN_AC)
                        if tc == 0:
                            t.wait_ge(dv.h, FW_DV)
                    gk = tc * 4 + rc
                    if gk >= 2:
                        t.wait_ge(dv.h, t3_ev.get(gk - 2, 0))
                    for dch in range(8):
                        ins = t.matmul(pA[gk % 2][:],
                                       lhsT=f1U[:, dch * 512 + rc * 128:dch * 512 + (rc + 1) * 128],
                                       rhs=lnt[:, dch * 512:(dch + 1) * 512],
                                       start=(dch == 0), stop=(dch == 7))
                    t3_mm[gk] = mm.inc(ins)
                block.tensor(m_t3)

                def m_t3e(v, tc=tc, rc=rc):
                    gk = tc * 4 + rc
                    v.wait_ge(mm.h, t3_mm[gk])
                    t3_ev[gk] = dv.inc(v.tensor_copy(out=t34[:, rc * 512:(rc + 1) * 512],
                                                     in_=pA[gk % 2][:]))
                block.vector(m_t3e)
            T3E_DV = dv.n

            for ic in range(32):
                def m_hm(t, tc=tc, ic=ic, tdv=T3E_DV):
                    if ic == 0:
                        t.wait_ge(dv.h, tdv)
                    gk = tc * 32 + ic
                    if gk >= 2:
                        t.wait_ge(ac.h, hm_ac.get(gk - 2, 0))
                    for rc in range(4):
                        ins = t.matmul(pB[gk % 2][:],
                                       lhsT=f1V[:, rc * 4096 + ic * 128:rc * 4096 + (ic + 1) * 128],
                                       rhs=t34[:, rc * 512:(rc + 1) * 512],
                                       start=(rc == 0), stop=(rc == 3))
                    hm_mm[gk] = mm.inc(ins)
                block.tensor(m_hm)

                def m_ge(sc, tc=tc, ic=ic):
                    sc.wait_ge(mm.h, hm_mm[tc * 32 + ic])
                    hm_ac[tc * 32 + ic] = ac.inc(sc.activation(
                        out=hmid[:, ic * 512:(ic + 1) * 512], in_=pB[(tc * 32 + ic) % 2][:],
                        func=AF.Gelu_apprx_tanh, bias=f1b_pp[:, ic:ic + 1]))
                block.scalar(m_ge)
            HM_AC = ac.n

            for cc4 in range(4):
                def m_t4(t, tc=tc, cc4=cc4):
                    if cc4 == 0:
                        t.wait_ge(ac.h, HM_AC)
                    gk = tc * 4 + cc4
                    if gk >= 2:
                        t.wait_ge(dv.h, t4_ev.get(gk - 2, 0))
                    for ic in range(32):
                        ins = t.matmul(pA[gk % 2][:],
                                       lhsT=f2U[:, ic * 512 + cc4 * 128:ic * 512 + (cc4 + 1) * 128],
                                       rhs=hmid[:, ic * 512:(ic + 1) * 512],
                                       start=(ic == 0), stop=(ic == 31))
                    t4_mm[gk] = mm.inc(ins)
                block.tensor(m_t4)

                def m_t4e(v, tc=tc, cc4=cc4):
                    gk = tc * 4 + cc4
                    v.wait_ge(mm.h, t4_mm[gk])
                    t4_ev[gk] = dv.inc(v.tensor_copy(out=t34[:, cc4 * 512:(cc4 + 1) * 512],
                                                     in_=pA[gk % 2][:]))
                block.vector(m_t4e)
            T4E_DV = dv.n

            for dch in range(8):
                def m_o5(t, tc=tc, dch=dch, tdv=T4E_DV):
                    if dch == 0:
                        t.wait_ge(dv.h, tdv)
                    gk = tc * 8 + dch
                    if gk >= 2:
                        t.wait_ge(dv.h, blk_dv.get(gk - 2, 0))
                    for cc4 in range(4):
                        ins = t.matmul(pB[gk % 2][:],
                                       lhsT=f2V[:, cc4 * 1024 + dch * 128:cc4 * 1024 + (dch + 1) * 128],
                                       rhs=t34[:, cc4 * 512:(cc4 + 1) * 512],
                                       start=(cc4 == 0), stop=(cc4 == 3))
                    o5_mm[gk] = mm.inc(ins)
                block.tensor(m_o5)

                def m_oa(sc, tc=tc, dch=dch):
                    sc.wait_ge(mm.h, o5_mm[tc * 8 + dch])
                    if tc * 8 + dch >= 1:
                        sc.wait_ge(dv.h, blk_dv.get(tc * 8 + dch - 1, 0))
                    o5_ac[tc * 8 + dch] = ac.inc(sc.activation(
                        out=ao3[:], in_=pB[(tc * 8 + dch) % 2][:], func=AF.Identity,
                        bias=f2b_pp[:, dch:dch + 1]))
                block.scalar(m_oa)

                def m_ov(v, tc=tc, dch=dch):
                    v.wait_ge(ac.h, o5_ac[tc * 8 + dch])
                    hsl = hbf[:].rearrange("p (c t) -> p c t", t=TOK)[:, dch, tc * 512:(tc + 1) * 512]
                    blk_dv[tc * 8 + dch] = dv.inc(v.tensor_tensor(out=hsl, in0=hsl, in1=ao3[:], op=ALU.add))
                block.vector(m_ov)
        BLK_DV = dv.n
        p3c.close()

        p3d = ExitStack()
        sdd = lambda n_, s_, d_: p3d.enter_context(nc.sbuf_tensor(n_, s_, d_))
        fft = sdd("fft", [128, 8 * 512], BF16)
        sfin = sdd("sfin", [128, 8 * 512], F32)
        xrt = [sdd(f"xrt{i}", [128, D], F32) for i in range(2)]
        ot = [sdd(f"ot{i}", [128, D], F32) for i in range(2)]
        ptf = [p3d.enter_context(nc.psum_tensor(f"ptf{i}", [128, 1024], F32)) for i in range(2)]

        fft_ld = [0] * 4
        sf_dv = [0] * 4
        ftr_mm = {}
        ot_dv = {}
        out_st = {}
        xr_ld = {}

        for tc in range(4):
            def d_ld(sync, tc=tc):
                if tc == 0:
                    W(sync, FFN_ST)
                    sync.wait_ge(dv.h, BLK_DV)
                if tc >= 1:
                    sync.wait_ge(dv.h, sf_dv[tc - 1])
                fft_ld[tc] = dF.rec(sync.dma_start(out=fft[:],
                                      in_=ffn_sp.rearrange("p (c t) -> p c t", t=TOK)[:, :, tc * 512:(tc + 1) * 512]), 16)
            block.sync(d_ld)

            def d_sum(v, tc=tc):
                W(v, fft_ld[tc])
                if tc == 0:
                    v.wait_ge(dv.h, BLK_DV)
                for dch in range(8):
                    ins = v.tensor_tensor(
                        out=sfin[:, dch * 512:(dch + 1) * 512],
                        in0=hbf[:].rearrange("p (c t) -> p c t", t=TOK)[:, dch, tc * 512:(tc + 1) * 512],
                        in1=fft[:, dch * 512:(dch + 1) * 512], op=ALU.add)
                sf_dv[tc] = dv.inc(ins)
            block.vector(d_sum)

            for tb in range(4):
                ti = tc * 4 + tb

                def d_x(sync, tc=tc, tb=tb, ti=ti):
                    if ti >= 2:
                        sync.wait_ge(dv.h, ot_dv.get(ti - 2, 0))
                    xr_ld[ti] = dX[ti % 2].rec(sync.dma_start(out=xrt[ti % 2][:],
                                          in_=x_own[ti * 128:(ti + 1) * 128, :]), 16)
                block.sync(d_x)

                def d_tr(t, tc=tc, tb=tb, ti=ti):
                    t.wait_ge(dv.h, sf_dv[tc])
                    if ti >= 2:
                        t.wait_ge(dv.h, ot_dv.get(ti - 2, 0))
                    for dch in range(8):
                        ins = t.transpose(ptf[ti % 2][:, dch * 128:(dch + 1) * 128],
                                          in_=sfin[:, dch * 512 + tb * 128:dch * 512 + (tb + 1) * 128],
                                          identity=idf[:])
                    ftr_mm[ti] = mm.inc(ins)
                block.tensor(d_tr)

                def d_add(v, ti=ti):
                    v.wait_ge(mm.h, ftr_mm[ti])
                    W(v, xr_ld[ti])
                    if ti >= 2 and out_st.get(ti - 2):
                        W(v, out_st.get(ti - 2))
                    v.tensor_tensor(out=ot[ti % 2][:, 0:512], in0=ptf[ti % 2][:, 0:512],
                                    in1=xrt[ti % 2][:, 0:512], op=ALU.add)
                    ot_dv[ti] = dv.inc(v.tensor_tensor(out=ot[ti % 2][:, 512:1024], in0=ptf[ti % 2][:, 512:1024],
                                                       in1=xrt[ti % 2][:, 512:1024], op=ALU.add))
                block.vector(d_add)

                def d_st(sync, ti=ti):
                    sync.wait_ge(dv.h, ot_dv[ti])
                    out_st[ti] = sO[ti % 2].rec(sync.dma_start(out=out_t[ti * 128:(ti + 1) * 128, :], in_=ot[ti % 2][:]), 16)
                block.sync(d_st)
        def fin_sync(sync):
            W(sync, sO[0].fin())
            W(sync, sO[1].fin())
        block.sync(fin_sync)

    p3d.close()
    p3.close()
    top.close()
    return nc


_CACHE = {}


def kernel(**inputs):
    g = lambda k: np.asarray(inputs[k])
    x = g("L0_attn_hidden_states").astype(np.float32)

    qU, kU, vU = g("L0_attn_q_U"), g("L0_attn_k_U"), g("L0_attn_v_U")
    qV, kV, vV = g("L0_attn_q_V"), g("L0_attn_k_V"), g("L0_attn_v_V")
    qb, kb, vb = g("L0_attn_q_b"), g("L0_attn_k_b"), g("L0_attn_v_b")

    qkvU = np.zeros((D, 3 * 1024), np.float32)
    qkvV = np.zeros((128, 3 * 1024), np.float32)
    qkvb = np.zeros((128, 3088), np.float32)
    for pi, (U, V, bvec) in enumerate([(qU, qV, qb), (kU, kV, kb), (vU, vV, vb)]):
        for h in range(H):
            qkvU[:, pi * 1024 + h * 64:pi * 1024 + h * 64 + 48] = U[:, h, :]
            qkvV[0:48, pi * 1024 + h * 64:pi * 1024 + (h + 1) * 64] = V[h]
            if pi < 2:
                qkvb[:, pi * 1024 + h * 64:pi * 1024 + (h + 1) * 64] = bvec[h][None, :]
            else:
                qkvb[:, 2048 + h * 65:2048 + h * 65 + 64] = bvec[h][None, :]
                qkvb[:, 2048 + h * 65 + 64] = 1.0

    pp = lambda vec, nch: np.ascontiguousarray(vec.astype(np.float32).reshape(nch, 128).T)

    kp = np.arange(128)[:, None]
    qq = np.arange(512)[None, :]
    cmask = np.zeros((128, 4 * 512), np.float32)
    for m in range(4):
        cmask[:, m * 512:(m + 1) * 512] = np.where(qq >= m * 128 + kp, 0.0, -1e9)

    feeds_common = {
        "qkvU": qkvU,
        "qkvV": qkvV,
        "ln1g": pp(g("L0_attn_ln1_g"), 8),
        "ln1b": pp(g("L0_attn_ln1_b"), 8),
        "ln2g": pp(g("L0_attn_ln2_g"), 8),
        "ln2b": pp(g("L0_attn_ln2_b"), 8),
        "outU": g("L0_attn_out_U").astype(np.float32),
        "outV": g("L0_attn_out_V").astype(np.float32),
        "outb": pp(g("L0_attn_out_b"), 8),
        "fc1U": g("L0_attn_fc1_U").astype(np.float32),
        "fc1V": g("L0_attn_fc1_V").astype(np.float32),
        "fc1b": pp(g("L0_attn_fc1_b"), 32),
        "fc2U": g("L0_attn_fc2_U").astype(np.float32),
        "fc2V": g("L0_attn_fc2_V").astype(np.float32),
        "fc2b": pp(g("L0_attn_fc2_b"), 8),
        "wgwu": np.ascontiguousarray(
            np.concatenate([g("L0_ffn_w_gate"), g("L0_ffn_w_up")], axis=0).astype(np.float32)),
        "wdn": g("L0_ffn_w_down").astype(np.float32),
        "cmask": cmask.astype(ml_dtypes.bfloat16),
        "qkvb": qkvb,
        "idenb": np.eye(128).astype(ml_dtypes.bfloat16),
        "idenf": np.eye(128, dtype=np.float32),
    }

    in_maps = []
    for c in range(8):
        b, j = c // 4, c % 4
        m = dict(feeds_common)
        m["x_own"] = np.ascontiguousarray(x[b, j * TOK:(j + 1) * TOK, :])
        m["coffs"] = np.tile(np.array([[j * 256, j * 260, j * TOK, 0]], np.int32), (128, 1))
        in_maps.append(m)

    if "nc" not in _CACHE:
        _CACHE["nc"] = build_nc()
    res = run_bass_kernel_spmd(_CACHE["nc"], in_maps, core_ids=list(range(8)))

    out = np.empty((B, S, D), np.float32)
    for c in range(8):
        b, j = c // 4, c % 4
        out[b, j * TOK:(j + 1) * TOK, :] = res.results[c]["out"]
    return out


# version 12
# speedup vs baseline: 1.0262x; 1.0013x over previous
import numpy as np
import ml_dtypes
from contextlib import ExitStack

import concourse.bass as bass
import concourse.mybir as mybir
from concourse.bass_utils import run_bass_kernel_spmd
from concourse.expressions import RuntimeValue

F32 = mybir.dt.float32
BF16 = mybir.dt.bfloat16
I32 = mybir.dt.int32
AF = mybir.ActivationFunctionType
ALU = mybir.AluOpType
AX = mybir.AxisListType
ds = bass.ds

B, S, D, H, DH = 2, 8192, 1024, 16, 64
TOK = 2048
NT = TOK // 128
EPS = 1e-5
NFFN = 2816
IM = 4096
VCOLS = 2048
GCOLS = 2048 + 16 * 65
MAGIC = 1.5 * (2 ** 22)
NB_LIST = [(0, 512), (512, 512), (1024, 512), (1536, 512), (2048, 512), (2560, 256)]


class Ctr:
    def __init__(self, h):
        self.h = h
        self.n = 0

    def inc(self, inst, k=1):
        inst.then_inc(self.h, k)
        self.n += k
        return self.n

    def rec(self, inst, k=1):
        self.inc(inst, k)
        return (self.h, self.n)

    def fin(self):
        return (self.h, self.n)


def W(eng, rec):
    if rec and rec[1] > 0:
        eng.wait_ge(rec[0], rec[1])


def build_nc():
    nc = bass.Bass("TRN2", target_bir_lowering=False)
    dt = nc.dram_tensor

    x_own = dt("x_own", [TOK, D], F32, kind="ExternalInput").ap()
    qkvU = dt("qkvU", [D, 3 * 1024], F32, kind="ExternalInput").ap()
    qkvV = dt("qkvV", [128, 3 * 1024], F32, kind="ExternalInput").ap()
    ln1g = dt("ln1g", [128, 8], F32, kind="ExternalInput").ap()
    ln1b = dt("ln1b", [128, 8], F32, kind="ExternalInput").ap()
    ln2g = dt("ln2g", [128, 8], F32, kind="ExternalInput").ap()
    ln2b = dt("ln2b", [128, 8], F32, kind="ExternalInput").ap()
    outU = dt("outU", [D, 768], F32, kind="ExternalInput").ap()
    outV = dt("outV", [768, D], F32, kind="ExternalInput").ap()
    outb = dt("outb", [128, 8], F32, kind="ExternalInput").ap()
    fc1U = dt("fc1U", [D, 512], F32, kind="ExternalInput").ap()
    fc1V = dt("fc1V", [512, IM], F32, kind="ExternalInput").ap()
    fc1b = dt("fc1b", [128, 32], F32, kind="ExternalInput").ap()
    fc2U = dt("fc2U", [IM, 512], F32, kind="ExternalInput").ap()
    fc2V = dt("fc2V", [512, D], F32, kind="ExternalInput").ap()
    fc2b = dt("fc2b", [128, 8], F32, kind="ExternalInput").ap()
    wgwu = dt("wgwu", [2 * NFFN, D], F32, kind="ExternalInput").ap()
    wdn = dt("wdn", [D, NFFN], F32, kind="ExternalInput").ap()
    cmask = dt("cmask", [128, 4 * 512], BF16, kind="ExternalInput").ap()
    idenb = dt("idenb", [128, 128], BF16, kind="ExternalInput").ap()
    idenf = dt("idenf", [128, 128], F32, kind="ExternalInput").ap()
    qkvb = dt("qkvb", [128, 3088], F32, kind="ExternalInput").ap()
    coffs = dt("coffs", [128, 4], I32, kind="ExternalInput").ap()

    out_t = dt("out", [TOK, D], F32, kind="ExternalOutput").ap()

    qkv_loc = dt("qkv_loc", [TOK, GCOLS], BF16, kind="Internal").ap()
    qkv_g = dt("qkv_g", [S, GCOLS], BF16, kind="Internal").ap()
    ctx_loc = dt("ctx_loc", [256, S], BF16, kind="Internal").ap()
    ctx_g = dt("ctx_g", [1024, S], BF16, kind="Internal").ap()
    xn_sp = dt("xn_sp", [128, 8 * TOK], BF16, kind="Internal").ap()
    ffn_sp = dt("ffn_sp", [128, 8 * TOK], BF16, kind="Internal").ap()

    rg = [[0, 1, 2, 3], [4, 5, 6, 7]]

    top = ExitStack()
    sbT = lambda n_, s_, d_: top.enter_context(nc.sbuf_tensor(n_, s_, d_))
    C = {k: Ctr(top.enter_context(nc.semaphore("s_" + k)))
         for k in ["ld", "st", "mm", "dv", "ac", "cc", "dK", "dCO", "dC", "dW3", "dXN", "dF",
                   "sX", "sF", "sC",
                   "dX0", "dX1", "dW0", "dW1", "dQ0", "dQ1", "dV0", "dV1", "dT0", "dT1",
                   "sA0", "sA1", "sO0", "sO1"]}
    ld, st, mm, dv, ac, cc = (C[k] for k in ["ld", "st", "mm", "dv", "ac", "cc"])
    dK, dCO, dC, dW3, dXN, dF = (C[k] for k in ["dK", "dCO", "dC", "dW3", "dXN", "dF"])
    sX, sF, sC = (C[k] for k in ["sX", "sF", "sC"])
    dX = [C["dX0"], C["dX1"]]
    dW = [C["dW0"], C["dW1"]]
    dQ = [C["dQ0"], C["dQ1"]]
    dV = [C["dV0"], C["dV1"]]
    dT = [C["dT0"], C["dT1"]]
    sA = [C["sA0"], C["sA1"]]
    sO = [C["sO0"], C["sO1"]]

    cm = sbT("cm", [128, 2048], BF16)
    idb = sbT("idb", [128, 128], BF16)
    idf = sbT("idf", [128, 128], F32)
    l1g = sbT("l1g", [128, 8], F32)
    l1b = sbT("l1b", [128, 8], F32)
    l2g = sbT("l2g", [128, 8], F32)
    l2b = sbT("l2b", [128, 8], F32)
    ob_pp = sbT("ob_pp", [128, 8], F32)
    f1b_pp = sbT("f1b_pp", [128, 32], F32)
    f2b_pp = sbT("f2b_pp", [128, 8], F32)
    qb_bf = sbT("qb_bf", [128, 3088], BF16)
    ones1 = sbT("ones1", [128, 1], BF16)
    of32 = sbT("of32", [1, 128], F32)
    co_sb = sbT("co_sb", [128, 4], I32)
    qs_box = {}

    class _QS:
        def __getitem__(self, k):
            return qs_box[k]
    

    def quant1(v, src, dst, n):
        qs_a = qs_box["a"]; qs_r = qs_box["r"]; qs_d = qs_box["d"]
        qs_y = qs_box["y"]; qs_m = qs_box["m"]; qs_t = qs_box["t"]
        nb = n // 16
        dv.inc(v.tensor_reduce(out=qs_a[:, 0:nb],
                               in_=src.rearrange("p (nb k) -> p nb k", k=16),
                               axis=AX.X, op=ALU.max, apply_absolute_value=True))
        v.wait_ge(dv.h, dv.n)
        v.tensor_scalar(out=qs_a[:, 0:nb], in0=qs_a[:, 0:nb], scalar1=6e-12, scalar2=None, op0=ALU.max)
        dv.inc(v.reciprocal(out=qs_r[:, 0:nb], in_=qs_a[:, 0:nb]))
        v.wait_ge(dv.h, dv.n)
        v.tensor_scalar(out=qs_r[:, 0:nb], in0=qs_r[:, 0:nb], scalar1=12.0, scalar2=None, op0=ALU.mult)
        v.tensor_scalar(out=qs_d[:, 0:nb], in0=qs_a[:, 0:nb], scalar1=1.0 / 12.0, scalar2=None, op0=ALU.mult)
        r12b = qs_r[:, 0:nb].unsqueeze(2).broadcast_to([128, nb, 16])
        dqb = qs_d[:, 0:nb].unsqueeze(2).broadcast_to([128, nb, 16])
        v.tensor_tensor(out=qs_y[:, 0:n].rearrange("p (nb k) -> p nb k", k=16),
                        in0=src.rearrange("p (nb k) -> p nb k", k=16), in1=r12b, op=ALU.mult)
        v.tensor_scalar(out=qs_m[:, 0:n].bitcast(I32), in0=qs_y[:, 0:n].bitcast(I32),
                        scalar1=0x7F800000, scalar2=None, op0=ALU.bitwise_and)
        v.tensor_scalar(out=qs_m[:, 0:n], in0=qs_m[:, 0:n], scalar1=2.0, scalar2=MAGIC,
                        op0=ALU.max, op1=ALU.mult)
        v.tensor_tensor(out=qs_t[:, 0:n], in0=qs_y[:, 0:n], in1=qs_m[:, 0:n], op=ALU.add)
        v.tensor_tensor(out=qs_t[:, 0:n], in0=qs_t[:, 0:n], in1=qs_m[:, 0:n], op=ALU.subtract)
        return dv.inc(v.tensor_tensor(out=dst.rearrange("p (nb k) -> p nb k", k=16),
                                      in0=qs_t[:, 0:n].rearrange("p (nb k) -> p nb k", k=16),
                                      in1=dqb, op=ALU.mult))

    def quant(v, src, dst, n):
        last = 0
        for o in range(0, n, 1024):
            w = min(1024, n - o)
            last = quant1(v, src[:, o:o + w], dst[:, o:o + w], w)
        return last

    def rsqrt_row(sc, dst, src):
        sc.activation(out=dst, in_=src, func=AF.Ln)
        return ac.inc(sc.activation(out=dst, in_=dst, func=AF.Exp, scale=-0.5))

    with nc.Block() as block:
        COFF = [None, None, None]

        def p0_sync(sync):
            dCO.inc(sync.dma_start(out=co_sb[:], in_=coffs), 16)
            sync.wait_ge(dCO.h, 16)
            r0 = sync.alloc_register(name="coff0")
            sync.load(r0, co_sb[0:1, 0:1])
            COFF[0] = RuntimeValue(r0, min_val=0, max_val=768)
            r1s = sync.alloc_register(name="coffs1")
            sync.load(r1s, co_sb[0:1, 1:2])
            COFF[1] = RuntimeValue(r1s, min_val=0, max_val=780)
            r2s = sync.alloc_register(name="coffs2")
            sync.load(r2s, co_sb[0:1, 2:3])
            COFF[2] = RuntimeValue(r2s, min_val=0, max_val=6144)
            for d_, s_ in [(cm, cmask), (idb, idenb), (idf, idenf), (l1g, ln1g),
                           (l1b, ln1b), (l2g, ln2g), (l2b, ln2b), (ob_pp, outb),
                           (f1b_pp, fc1b), (f2b_pp, fc2b)]:
                dK.inc(sync.dma_start(out=d_[:], in_=s_), 16)
        block.sync(p0_sync)
        P0_LD = dK.fin()

        def p0_vec(v):
            v.memset(ones1[:], 1.0)
            dv.inc(v.memset(of32[:], 1.0))
        block.vector(p0_vec)


        p1x = ExitStack()
        xqfm = p1x.enter_context(nc.sbuf_tensor("xqfm", [128, 8 * TOK], BF16))
        for _k, _sh in [("y", [128, 1024]), ("m", [128, 1024]), ("t", [128, 1024]),
                        ("a", [128, 64]), ("r", [128, 64]), ("d", [128, 64])]:
            qs_box[_k] = p1x.enter_context(nc.sbuf_tensor("qs_" + _k, _sh, F32))
        qs_y = qs_box["y"]

        qbcast = {}
        for qbi in range(4):
            w = 772
            def qb_sync(sync, qbi=qbi, w=w):
                if qbi >= 1:
                    sync.wait_ge(dv.h, qbcast[qbi - 1])
                qbcast[(qbi, "ld")] = dC.rec(sync.dma_start(out=qs_y[:, 0:w].bitcast(F32),
                                      in_=qkvb[:, qbi * w:(qbi + 1) * w]), 16)
            block.sync(qb_sync)

            def qb_vec(v, qbi=qbi, w=w):
                W(v, qbcast[(qbi, "ld")])
                qbcast[qbi] = dv.inc(v.tensor_copy(out=qb_bf[:, qbi * w:(qbi + 1) * w],
                                                   in_=qs_y[:, 0:w].bitcast(F32)))
            block.vector(qb_vec)
        QB_DV = dv.n


        p1 = ExitStack()
        sb1 = lambda n_, s_, d_: p1.enter_context(nc.sbuf_tensor(n_, s_, d_))
        ps1 = lambda n_, s_, d_: p1.enter_context(nc.psum_tensor(n_, s_, d_))
        wU = sb1("wU", [128, 8 * 3072], BF16)
        wVa = sb1("wVa", [128, 3072], BF16)
        wld = [sb1(f"wld{i}", [128, 1536], F32) for i in range(2)]
        xt = [sb1(f"xt{i}", [128, D], F32) for i in range(2)]
        xnb = [sb1(f"xnb{i}", [128, D], F32) for i in range(2)]
        xnbf = [sb1(f"xnbf{i}", [128, D], BF16) for i in range(2)]
        l1c = [sb1(f"l1c{i}", [128, D], BF16) for i in range(2)]
        xqt = [sb1(f"xqt{i}", [128, D], BF16) for i in range(2)]
        ring = [sb1(f"ring{i}", [128, 8 * 512], BF16) for i in range(2)]
        xst = [sb1(f"xst{i}", [128, D], BF16) for i in range(2)]
        psb = [sb1(f"psb{i}", [128, 8 * 512], BF16) for i in range(3)]
        asm = [sb1(f"asm{i}", [128, 1040], BF16) for i in range(2)]
        rw = {k: [sb1(f"rw_{k}{i}", [128, 1], F32) for i in range(2)]
              for k in ["s1", "s2", "mu", "var", "u1", "u2", "r1", "a2", "nm1", "nm2"]}
        ptb = [ps1(f"ptb{i}", [128, 1024], BF16) for i in range(2)]
        pst1 = [ps1(f"pst1{i}", [128, 512], F32) for i in range(2)]
        pst2 = [ps1(f"pst2{i}", [128, 512], F32) for i in range(2)]

        wu_ld = [0] * 18
        wu_dv = [0] * 18
        for j in range(16):
            i, hh = j // 2, j % 2
            def wl_sync(sync, j=j, i=i, hh=hh):
                if j >= 2:
                    sync.wait_ge(dv.h, wu_dv[j - 2])
                else:
                    sync.wait_ge(dv.h, QB_DV)
                wu_ld[j] = dC.rec(sync.dma_start(
                    out=wld[j % 2][:],
                    in_=qkvU[i * 128:(i + 1) * 128, hh * 1536:(hh + 1) * 1536]), 16)
            block.sync(wl_sync)

            def wl_vec(v, j=j, i=i, hh=hh):
                W(v, wu_ld[j])
                wu_dv[j] = dv.inc(v.tensor_copy(
                    out=wU[:, i * 3072 + hh * 1536:i * 3072 + (hh + 1) * 1536],
                    in_=wld[j % 2][:]))
            block.vector(wl_vec)

        for hh in range(2):
            def wv_sync(sync, hh=hh):
                sync.wait_ge(dv.h, wu_dv[14 + hh])
                wu_ld[16 + hh] = dC.rec(sync.dma_start(
                    out=wld[hh][:, :], in_=qkvV[:, hh * 1536:(hh + 1) * 1536]), 16)
            block.sync(wv_sync)

            def wv_vec(v, hh=hh):
                W(v, wu_ld[16 + hh])
                dv.inc(v.tensor_copy(out=wVa[0:48, hh * 1536:(hh + 1) * 1536], in_=wld[hh][0:48, :]))
                dv.inc(v.tensor_copy(out=wVa[64:112, hh * 1536:(hh + 1) * 1536], in_=wld[hh][0:48, :]))
            block.vector(wv_vec)
        WV_DV = dv.n

        x_ld = [0] * NT
        stat_dv = [0] * NT
        nm_dv = [0] * NT
        rs_ac = [0] * NT
        xn_ac = [0] * NT
        q_dv = [0] * NT
        tr1_mm = [0] * NT
        tr2_mm = [0] * NT
        ev1_ac = [0] * NT
        ev1_dv = [0] * NT
        ev2_dv = [0] * NT
        s1_done_mm = [0] * 4
        s2_done_mm = [0] * 4
        pst1_ev = {}
        pst2_ev = {}
        asm_st = {}
        xsp_st = {}
        qkv_st = [0] * 4

        for i in range(NT):
            p = i % 2
            s = i // 4

            def t_sync(sync, i=i, p=p):
                if i >= 2:
                    sync.wait_ge(ac.h, xn_ac[i - 2])
                x_ld[i] = dX[p].rec(sync.dma_start(out=xt[p][:], in_=x_own[i * 128:(i + 1) * 128, :]), 16)
            block.sync(t_sync)

            def t_vec1(v, i=i, p=p):
                W(v, x_ld[i])
                if i == 0:
                    W(v, P0_LD)
                dv.inc(v.reduce_sum(out=rw["s1"][p][:], in_=xt[p][:], axis=AX.X))
                dv.inc(v.scalar_tensor_tensor(out=qs_y[:, 0:D], in0=xt[p][:], scalar=1.0,
                                              in1=xt[p][:], op0=ALU.mult, op1=ALU.mult,
                                              accum_out=rw["s2"][p][:]))
                v.wait_ge(dv.h, dv.n)
                v.tensor_scalar(out=rw["mu"][p][:], in0=rw["s1"][p][:], scalar1=1.0 / D, scalar2=None, op0=ALU.mult)
                dv.inc(v.tensor_scalar(out=rw["var"][p][:], in0=rw["s2"][p][:], scalar1=1.0 / D, scalar2=None, op0=ALU.mult))
                v.wait_ge(dv.h, dv.n)
                dv.inc(v.tensor_tensor(out=rw["u1"][p][:], in0=rw["mu"][p][:], in1=rw["mu"][p][:], op=ALU.mult))
                v.wait_ge(dv.h, dv.n)
                dv.inc(v.tensor_tensor(out=rw["var"][p][:], in0=rw["var"][p][:], in1=rw["u1"][p][:], op=ALU.subtract))
                v.wait_ge(dv.h, dv.n)
                v.tensor_scalar(out=rw["u1"][p][:], in0=rw["var"][p][:], scalar1=EPS, scalar2=None, op0=ALU.add)
                stat_dv[i] = dv.inc(v.tensor_scalar(out=rw["u2"][p][:], in0=rw["var"][p][:],
                                                    scalar1=1.0 + EPS, scalar2=EPS * EPS,
                                                    op0=ALU.mult, op1=ALU.add))
            block.vector(t_vec1)

            def t_sc1(sc, i=i, p=p):
                sc.wait_ge(dv.h, stat_dv[i])
                rsqrt_row(sc, rw["r1"][p][:], rw["u1"][p][:])
                rs_ac[i] = rsqrt_row(sc, rw["a2"][p][:], rw["u2"][p][:])
            block.scalar(t_sc1)

            def t_vec2(v, i=i, p=p):
                v.wait_ge(ac.h, rs_ac[i])
                v.scalar_tensor_tensor(out=rw["nm1"][p][:], in0=rw["mu"][p][:], scalar=-1.0,
                                       in1=rw["r1"][p][:], op0=ALU.mult, op1=ALU.mult)
                nm_dv[i] = dv.inc(v.scalar_tensor_tensor(out=rw["nm2"][p][:], in0=rw["mu"][p][:],
                                                         scalar=-1.0, in1=rw["a2"][p][:],
                                                         op0=ALU.mult, op1=ALU.mult))
            block.vector(t_vec2)

            def t_sc2(sc, i=i, p=p):
                sc.wait_ge(dv.h, nm_dv[i])
                if i >= 2:
                    sc.wait_ge(dv.h, q_dv[i - 2])
                    sc.wait_ge(mm.h, tr1_mm[i - 2])
                sc.activation(out=xnb[p][:], in_=xt[p][:], func=AF.Identity,
                              scale=rw["r1"][p][:], bias=rw["nm1"][p][:])
                sc.activation(out=xnbf[p][:], in_=xt[p][:], func=AF.Identity,
                              scale=rw["r1"][p][:], bias=rw["nm1"][p][:])
                xn_ac[i] = ac.inc(sc.activation(out=l1c[p][:], in_=xt[p][:], func=AF.Identity,
                                                scale=rw["a2"][p][:], bias=rw["nm2"][p][:]))
            block.scalar(t_sc2)

            def t_vec3(v, i=i, p=p):
                v.wait_ge(ac.h, xn_ac[i])
                if i >= 2:
                    v.wait_ge(mm.h, tr2_mm[i - 2])
                q_dv[i] = quant(v, xnb[p][:], xqt[p][:], D)
            block.vector(t_vec3)

            def t_mm1(t, i=i, p=p):
                t.wait_ge(ac.h, xn_ac[i])
                if i >= 1:
                    t.wait_ge(dv.h, ev2_dv[i - 1])
                    t.wait_ge(dv.h, ev1_dv[i - 1])
                for c in range(8):
                    t.transpose(ptb[0][:, c * 128:(c + 1) * 128],
                                in_=l1c[p][:, c * 128:(c + 1) * 128], identity=idb[:])
                for c in range(8):
                    ins = t.transpose(ptb[1][:, c * 128:(c + 1) * 128],
                                      in_=xnbf[p][:, c * 128:(c + 1) * 128], identity=idb[:])
                tr1_mm[i] = mm.inc(ins)
            block.tensor(t_mm1)

            def t_sc3(sc, i=i, s=s):
                sc.wait_ge(mm.h, tr1_mm[i])
                if s >= 2 and i % 4 == 0:
                    sc.wait_ge(mm.h, s1_done_mm[s - 2])
                slot = i % 4
                for c in range(8):
                    ins = sc.activation(
                        out=ring[s % 2][:, c * 512 + slot * 128: c * 512 + (slot + 1) * 128],
                        in_=ptb[0][:, c * 128:(c + 1) * 128], func=AF.Identity,
                        scale=l1g[:, c:c + 1], bias=l1b[:, c:c + 1])
                ev1_ac[i] = ac.inc(ins)
            block.scalar(t_sc3)

            def t_vec4(v, i=i, p=p):
                v.wait_ge(mm.h, tr1_mm[i])
                if i >= 2:
                    W(v, xsp_st[i - 2])
                ev1_dv[i] = dv.inc(v.tensor_copy(out=xst[p][:], in_=ptb[1][:]))
            block.vector(t_vec4)

            def t_mm2(t, i=i, p=p):
                t.wait_ge(dv.h, q_dv[i])
                t.wait_ge(ac.h, ev1_ac[i])
                for c in range(8):
                    ins = t.transpose(ptb[0][:, c * 128:(c + 1) * 128],
                                      in_=xqt[p][:, c * 128:(c + 1) * 128], identity=idb[:])
                tr2_mm[i] = mm.inc(ins)
            block.tensor(t_mm2)

            def t_vec5(v, i=i):
                v.wait_ge(mm.h, tr2_mm[i])
                ev2_dv[i] = dv.inc(v.tensor_copy(
                    out=xqfm[:].rearrange("p (c t) -> p c t", t=TOK)[:, :, i * 128:(i + 1) * 128],
                    in_=ptb[0][:].rearrange("p (c t) -> p c t", t=128)))
            block.vector(t_vec5)

            def t_sync2(sync, i=i, p=p):
                sync.wait_ge(dv.h, ev1_dv[i])
                xsp_st[i] = sX.rec(sync.dma_start(
                    out=xn_sp.rearrange("p (c t) -> p c t", t=TOK)[:, :, i * 128:(i + 1) * 128],
                    in_=xst[p][:].rearrange("p (c t) -> p c t", t=128)), 16)
            block.sync(t_sync2)

            if i % 4 == 3:
                s1mm = {}

                def sl_pre(t, s=s, i=i):
                    t.wait_ge(ac.h, ev1_ac[i])
                    if s == 0:
                        t.wait_ge(dv.h, WV_DV)
                        t.wait_ge(dv.h, QB_DV)
                    if s >= 1:
                        t.wait_ge(mm.h, s2_done_mm[s - 1])
                block.tensor(sl_pre)

                for prj in range(3):
                    for ccx in range(8):
                        def sl_mm(t, s=s, prj=prj, ccx=ccx):
                            gk = s * 24 + prj * 8 + ccx
                            if gk >= 2:
                                t.wait_ge(dv.h, pst1_ev.get(gk - 2, 0))
                            for dch in range(8):
                                base = dch * 3072 + prj * 1024 + ccx * 128
                                ins = t.matmul(pst1[gk % 2][:],
                                               lhsT=wU[:, base:base + 128],
                                               rhs=ring[s % 2][:, dch * 512:(dch + 1) * 512],
                                               start=(dch == 0), stop=(dch == 7))
                            s1mm[(prj, ccx)] = mm.inc(ins)
                            if prj == 2 and ccx == 7:
                                s1_done_mm[s] = mm.n
                        block.tensor(sl_mm)

                        def sl_ev(v, s=s, prj=prj, ccx=ccx):
                            gk = s * 24 + prj * 8 + ccx
                            v.wait_ge(mm.h, s1mm[(prj, ccx)])
                            v.tensor_copy(out=psb[prj][0:48, ccx * 512:(ccx + 1) * 512],
                                          in_=pst1[gk % 2][0:48, :])
                            pst1_ev[gk] = dv.inc(
                                v.tensor_copy(out=psb[prj][64:112, ccx * 512:(ccx + 1) * 512],
                                              in_=pst1[gk % 2][64:112, :]))
                        block.vector(sl_ev)

                S1EV = dv.n
                s2mm = {}

                def s2_pre(t, s=s):
                    t.wait_ge(dv.h, S1EV)
                block.tensor(s2_pre)

                for tt in range(4):
                    a = tt % 2
                    for prj in range(3):
                        def s2_mm(t, s=s, tt=tt, prj=prj):
                            gk = s * 12 + tt * 3 + prj
                            if gk >= 1:
                                t.wait_ge(dv.h, pst2_ev.get(gk - 1, 0))
                            for h in [0, 2, 4, 6, 8, 10, 12, 14, -1, 1, 3, 5, 7, 9, 11, 13, 15]:
                                if h < 0:
                                    t.drain()
                                    continue
                                base = (h // 2) * 512 + tt * 128
                                ins = t.matmul(
                                    pst2[h // 8][:, (h % 8) * 64:(h % 8 + 1) * 64],
                                    lhsT=psb[prj][(h % 2) * 64:(h % 2) * 64 + 48, base:base + 128],
                                    rhs=wVa[(h % 2) * 64:(h % 2) * 64 + 48,
                                            prj * 1024 + h * 64:prj * 1024 + (h + 1) * 64],
                                    start=True, stop=True)
                            t.drain()
                            s2mm[(tt, prj)] = mm.inc(ins)
                            if tt == 3 and prj == 2:
                                s2_done_mm[s] = mm.n
                        block.tensor(s2_mm)

                        def s2_ev(v, s=s, tt=tt, prj=prj, a=a):
                            gk = s * 12 + tt * 3 + prj
                            v.wait_ge(mm.h, s2mm[(tt, prj)])
                            gabs = s * 4 + tt
                            prev_user = (gabs, prj - 1) if prj >= 1 else (gabs - 2, 2)
                            if prev_user in asm_st:
                                W(v, asm_st[prev_user])
                            if prj < 2:
                                v.tensor_tensor(out=asm[a][:, 0:512], in0=pst2[0][:],
                                                in1=qb_bf[:, prj * 1024:prj * 1024 + 512], op=ALU.add)
                                ins = v.tensor_tensor(out=asm[a][:, 512:1024], in0=pst2[1][:],
                                                      in1=qb_bf[:, prj * 1024 + 512:prj * 1024 + 1024], op=ALU.add)
                            else:
                                v.tensor_copy(out=asm[a][:, 0:1040], in_=qb_bf[:, 2048:3088])
                                av = asm[a][:].rearrange("p (h c) -> p h c", c=65)
                                v.tensor_tensor(out=av[:, 0:8, 0:64], in0=pst2[0][:].rearrange("p (h d) -> p h d", d=64),
                                                in1=av[:, 0:8, 0:64], op=ALU.add)
                                ins = v.tensor_tensor(out=av[:, 8:16, 0:64], in0=pst2[1][:].rearrange("p (h d) -> p h d", d=64),
                                                      in1=av[:, 8:16, 0:64], op=ALU.add)
                            pst2_ev[gk] = dv.inc(ins)
                        block.vector(s2_ev)

                        def s2_st1(sync, s=s, tt=tt, prj=prj, a=a):
                            gk = s * 12 + tt * 3 + prj
                            sync.wait_ge(dv.h, pst2_ev[gk])
                            row0 = s * 512 + tt * 128
                            width = 1024 if prj < 2 else 1040
                            asm_st[(s * 4 + tt, prj)] = sA[a].rec(sync.dma_start(
                                out=qkv_loc[row0:row0 + 128, prj * 1024:prj * 1024 + width],
                                in_=asm[a][:, 0:width]), 16)
                        block.sync(s2_st1)

        COFFG = [None, None, None]

        def coll1(g):
            W(g, sA[0].fin())
            W(g, sA[1].fin())
            g.wait_ge(dCO.h, 16)
            r1 = g.alloc_register(name="coffg1")
            g.load(r1, co_sb[0:1, 1:2])
            COFFG[1] = RuntimeValue(r1, min_val=0, max_val=780)
            r2 = g.alloc_register(name="coffg2")
            g.load(r2, co_sb[0:1, 2:3])
            COFFG[2] = RuntimeValue(r2, min_val=0, max_val=6144)
            cc.inc(g.collective_compute("AllGather", ALU.bypass, replica_groups=rg,
                                        ins=[qkv_loc.opt()], outs=[qkv_g.opt()]))
        block.gpsimd(coll1)
        CC1 = cc.n

        p1.close()

        p15 = ExitStack()
        sb5 = lambda n_, s_, d_: p15.enter_context(nc.sbuf_tensor(n_, s_, d_))
        ps5 = lambda n_, s_, d_: p15.enter_context(nc.psum_tensor(n_, s_, d_))
        wfl = [sb5(f"wfl{i}", [128, NFFN], F32) for i in range(2)]
        wqs = sb5("wqs", [128, D], BF16)
        wds = sb5("wds", [128, 512], BF16)
        wqT = [sb5(f"wqT{i}", [128, 8 * 512], BF16) for i in range(2)]
        wdT = sb5("wdT", [128, 4 * 1024], BF16)
        hqn = sb5("hqn", [128, 4 * TOK], BF16)
        sil = sb5("sil", [128, 512], F32)
        hmt = sb5("hmt", [128, 512], F32)
        hqt = sb5("hqt", [128, 512], BF16)
        facc = sb5("facc", [128, 8 * TOK], F32)
        ptq = ps5("ptq", [128, 1024], BF16)
        pth = ps5("pth", [128, 1024], BF16)
        pg = [ps5(f"pg{i}", [128, 512], F32) for i in range(2)]
        pu = [ps5(f"pu{i}", [128, 512], F32) for i in range(2)]
        pdn = [ps5(f"pdn{i}", [128, 512], F32) for i in range(2)]

        wjobs = []
        for nbi, (noff, nsz) in enumerate(NB_LIST):
            nch = nsz // 128
            for r_ in range(nch):
                wjobs.append(("g", nbi, r_, noff, nsz))
            for r_ in range(nch):
                wjobs.append(("u", nbi, r_, noff, nsz))
            for r_ in range(8):
                wjobs.append(("d", nbi, r_, noff, nsz))
        wj_ld = {}
        wj_q = {}
        wj_tr = {}
        wj_ev = {}
        ptq_free = [0]
        gu_done_mm = {}
        dn_done_mm = {}
        gu_mm = {}
        sil_ac = {}
        mul_dv = {}
        hq_tr = {}
        hq_ev = {}
        dn_mm = {}
        dn_ev = {}

        def emit_wjob(k):
            kind, nbi, r_, noff, nsz = wjobs[k]
            nch = nsz // 128

            def wj_sync(sync, k=k, kind=kind, nbi=nbi, r_=r_, noff=noff, nsz=nsz):
                if k < 2:
                    sync.wait_ge(mm.h, s2_done_mm[3])
                    W(sync, sA[0].fin())
                    W(sync, sA[1].fin())
                    W(sync, sX.fin())
                if k >= 2:
                    sync.wait_ge(dv.h, wj_q[k - 2])
                if kind == "g":
                    src = wgwu[noff + r_ * 128:noff + (r_ + 1) * 128, :]
                    w = D
                elif kind == "u":
                    src = wgwu[NFFN + noff + r_ * 128:NFFN + noff + (r_ + 1) * 128, :]
                    w = D
                else:
                    src = wdn[r_ * 128:(r_ + 1) * 128, noff:noff + nsz]
                    w = nsz
                wj_ld[k] = dW[k % 2].rec(sync.dma_start(out=wfl[k % 2][:, 0:w], in_=src), 16)
            block.sync(wj_sync)

            def wj_v(v, k=k, kind=kind, nsz=nsz):
                W(v, wj_ld[k])
                if k >= 1:
                    v.wait_ge(mm.h, wj_tr[k - 1])
                if kind == "d":
                    wj_q[k] = quant(v, wfl[k % 2][:, 0:nsz], wds[:, 0:nsz], nsz)
                else:
                    wj_q[k] = quant(v, wfl[k % 2][:, 0:D], wqs[:], D)
            block.vector(wj_v)

            def wj_t(t, k=k, kind=kind, nsz=nsz):
                t.wait_ge(dv.h, wj_q[k])
                t.wait_ge(dv.h, ptq_free[0])
                nblk = (nsz // 128) if kind == "d" else 8
                srcb = wds if kind == "d" else wqs
                for b_ in range(nblk):
                    ins = t.transpose(ptq[:, b_ * 128:(b_ + 1) * 128],
                                      in_=srcb[:, b_ * 128:(b_ + 1) * 128], identity=idb[:])
                wj_tr[k] = mm.inc(ins)
            block.tensor(wj_t)

            def wj_e(v, k=k, kind=kind, nbi=nbi, r_=r_, nsz=nsz):
                v.wait_ge(mm.h, wj_tr[k])
                if nbi >= 1:
                    if kind in ("g", "u"):
                        v.wait_ge(mm.h, gu_done_mm[nbi - 1])
                    else:
                        v.wait_ge(mm.h, dn_done_mm[nbi - 1])
                nblk = (nsz // 128) if kind == "d" else 8
                for b_ in range(nblk):
                    if kind == "d":
                        dst = wdT[:, b_ * 1024 + r_ * 128:b_ * 1024 + (r_ + 1) * 128]
                    else:
                        wi = 0 if kind == "g" else 1
                        dst = wqT[wi][:, b_ * nsz + r_ * 128:b_ * nsz + (r_ + 1) * 128]
                    ins = v.tensor_copy(out=dst, in_=ptq[:, b_ * 128:(b_ + 1) * 128])
                wj_ev[k] = dv.inc(ins)
                ptq_free[0] = wj_ev[k]
            block.vector(wj_e)

        kbase = 0
        for nbi, (noff, nsz) in enumerate(NB_LIST):
            nch = nsz // 128
            njobs = 2 * nch + 8
            for k in range(kbase, kbase + njobs):
                emit_wjob(k)
            WJ_EV_NB = dv.n

            for tt in range(NT):
                def g_mm(t, nbi=nbi, nsz=nsz, tt=tt):
                    if tt == 0:
                        t.wait_ge(dv.h, WJ_EV_NB)
                        if nbi == 0:
                            t.wait_ge(dv.h, ev2_dv[NT - 1])
                    pp = tt % 2
                    if tt >= 2:
                        t.wait_ge(ac.h, sil_ac[(nbi, tt - 2)])
                        t.wait_ge(dv.h, mul_dv[(nbi, tt - 2)])
                    for kc in range(8):
                        t.matmul(pg[pp][:, 0:nsz],
                                 lhsT=xqfm[:].rearrange("p (c t) -> p c t", t=TOK)[:, kc, tt * 128:(tt + 1) * 128],
                                 rhs=wqT[0][:, kc * nsz:(kc + 1) * nsz],
                                 start=(kc == 0), stop=(kc == 7))
                    for kc in range(8):
                        ins = t.matmul(pu[pp][:, 0:nsz],
                                       lhsT=xqfm[:].rearrange("p (c t) -> p c t", t=TOK)[:, kc, tt * 128:(tt + 1) * 128],
                                       rhs=wqT[1][:, kc * nsz:(kc + 1) * nsz],
                                       start=(kc == 0), stop=(kc == 7))
                    gu_mm[(nbi, tt)] = mm.inc(ins)
                    if tt == NT - 1:
                        gu_done_mm[nbi] = mm.n
                block.tensor(g_mm)

                def g_sc(sc, nbi=nbi, tt=tt, nsz=nsz):
                    sc.wait_ge(mm.h, gu_mm[(nbi, tt)])
                    if tt >= 1:
                        sc.wait_ge(dv.h, mul_dv[(nbi, tt - 1)])
                    sil_ac[(nbi, tt)] = ac.inc(
                        sc.activation(out=sil[:, 0:nsz], in_=pg[tt % 2][:, 0:nsz], func=AF.Silu))
                block.scalar(g_sc)

                def g_v(v, nbi=nbi, tt=tt, nsz=nsz):
                    v.wait_ge(ac.h, sil_ac[(nbi, tt)])
                    if tt >= 1:
                        v.wait_ge(mm.h, hq_tr[(nbi, tt - 1)])
                    dv.inc(v.tensor_tensor(out=hmt[:, 0:nsz], in0=sil[:, 0:nsz],
                                           in1=pu[tt % 2][:, 0:nsz], op=ALU.mult))
                    v.wait_ge(dv.h, dv.n)
                    mul_dv[(nbi, tt)] = quant(v, hmt[:, 0:nsz], hqt[:, 0:nsz], nsz)
                block.vector(g_v)

                def h_t(t, nbi=nbi, tt=tt, nsz=nsz):
                    t.wait_ge(dv.h, mul_dv[(nbi, tt)])
                    prev = (nbi, tt - 1) if tt >= 1 else ((nbi - 1, NT - 1) if nbi else None)
                    if prev is not None:
                        t.wait_ge(dv.h, hq_ev.get(prev, 0))
                    for c in range(nsz // 128):
                        ins = t.transpose(pth[:, c * 128:(c + 1) * 128],
                                          in_=hqt[:, c * 128:(c + 1) * 128], identity=idb[:])
                    hq_tr[(nbi, tt)] = mm.inc(ins)
                block.tensor(h_t)

                def h_e(v, nbi=nbi, tt=tt, nsz=nsz):
                    v.wait_ge(mm.h, hq_tr[(nbi, tt)])
                    if nbi >= 1 and tt == 0:
                        v.wait_ge(mm.h, dn_done_mm[nbi - 1])
                    ins = v.tensor_copy(
                        out=hqn[:, 0:(nsz // 128) * TOK].rearrange("p (c t) -> p c t", t=TOK)[:, :, tt * 128:(tt + 1) * 128],
                        in_=pth[:, 0:nsz].rearrange("p (c t) -> p c t", t=128))
                    hq_ev[(nbi, tt)] = dv.inc(ins)
                block.vector(h_e)

            def d_pre(t, nbi=nbi):
                t.wait_ge(dv.h, hq_ev[(nbi, NT - 1)])
            block.tensor(d_pre)

            for tc in range(4):
                for dc in range(8):
                    def d_mm(t, nbi=nbi, nsz=nsz, nch=nch, tc=tc, dc=dc):
                        gk = nbi * 32 + tc * 8 + dc
                        if gk >= 2:
                            t.wait_ge(dv.h, dn_ev.get(gk - 2, 0))
                        for c in range(nch):
                            ins = t.matmul(
                                pdn[gk % 2][:],
                                lhsT=wdT[:, c * 1024 + dc * 128:c * 1024 + (dc + 1) * 128],
                                rhs=hqn[:, 0:nch * TOK].rearrange("p (cc t) -> p cc t", t=TOK)[:, c, tc * 512:(tc + 1) * 512],
                                start=(c == 0), stop=(c == nch - 1))
                        dn_mm[gk] = mm.inc(ins)
                        if tc == 3 and dc == 7:
                            dn_done_mm[nbi] = mm.n
                    block.tensor(d_mm)

                    def d_ev(v, nbi=nbi, tc=tc, dc=dc):
                        gk = nbi * 32 + tc * 8 + dc
                        v.wait_ge(mm.h, dn_mm[gk])
                        dst = facc[:].rearrange("p (c t) -> p c t", t=TOK)[:, dc, tc * 512:(tc + 1) * 512]
                        if nbi == 0:
                            ins = v.tensor_copy(out=dst, in_=pdn[gk % 2][:])
                        else:
                            ins = v.tensor_tensor(out=dst, in0=dst, in1=pdn[gk % 2][:], op=ALU.add)
                        dn_ev[gk] = dv.inc(ins)
                    block.vector(d_ev)
            kbase += njobs

        FFN_CAST = [0]

        def f_cast2(v):
            v.wait_ge(mm.h, dn_done_mm[len(NB_LIST) - 1])
            for c in range(4):
                ins = v.tensor_copy(out=hqn[:, c * TOK:(c + 1) * TOK],
                                    in_=facc[:, c * TOK:(c + 1) * TOK])
            FFN_CAST[0] = dv.inc(ins)
        block.vector(f_cast2)

        def f_st1(sync):
            sync.wait_ge(dv.h, FFN_CAST[0])
            sF.inc(sync.dma_start(out=ffn_sp[:, 0:4 * TOK], in_=hqn[:, 0:4 * TOK]), 16)
        block.sync(f_st1)
        FFN_ST1 = sF.fin()

        def f_cast3(v):
            W(v, FFN_ST1)
            for c in range(4):
                ins = v.tensor_copy(out=hqn[:, c * TOK:(c + 1) * TOK],
                                    in_=facc[:, (4 + c) * TOK:(5 + c) * TOK])
            FFN_CAST[0] = dv.inc(ins)
        block.vector(f_cast3)

        def f_st2(sync):
            sync.wait_ge(dv.h, FFN_CAST[0])
            sF.inc(sync.dma_start(out=ffn_sp[:, 4 * TOK:8 * TOK], in_=hqn[:, 0:4 * TOK]), 16)
        block.sync(f_st2)
        FFN_ST = sF.fin()

        P15_DV_END = dv.n
        P15_MM_END = mm.n

        p15.close()
        p1x.close()

        p2 = ExitStack()
        sb2 = lambda n_, s_, d_: p2.enter_context(nc.sbuf_tensor(n_, s_, d_))
        ps2 = lambda n_, s_, d_: p2.enter_context(nc.psum_tensor(n_, s_, d_))
        Qfm = [[sb2(f"Qfm{u}_{l}", [64, S], BF16) for l in range(2)] for u in range(2)]
        Kfm = [[sb2(f"Kfm{u}_{l}", [64, S], BF16) for l in range(2)] for u in range(2)]
        VT = [sb2(f"VT{u}", [128, 64 * 130], BF16) for u in range(2)]
        Et = [sb2(f"Et{i}", [128, 1024], BF16) for i in range(2)]
        rrow = sb2("rrow", [1, 1024], F32)
        rb = sb2("rb", [64, 1024], F32)
        cta = sb2("cta", [128, 512], BF16)

        unit_ld = [0, 0]

        p2p = ExitStack()
        qkst = [p2p.enter_context(nc.sbuf_tensor(f"qkst{i}", [128, 4096], BF16)) for i in range(2)]
        ptk = [p2p.enter_context(nc.psum_tensor(f"ptk{i}", [128, 1024], BF16)) for i in range(2)]
        qk_ld = {}
        qk_tr = {}
        qk_ev = {}
        gjob = 0
        for u in range(2):
            for ten in range(2):
                for hf in range(2):
                    def qk_sync(sync, u=u, ten=ten, hf=hf, gjob=gjob):
                        cbase = ten * 1024 + u * 128
                        if gjob >= 2:
                            sync.wait_ge(dv.h, qk_ev.get((gjob - 2, 3), 0))
                        if gjob == 0:
                            sync.wait_ge(cc.h, CC1)
                            sync.wait_ge(dv.h, P15_DV_END)
                            sync.wait_ge(mm.h, P15_MM_END)
                            W(sync, sF.fin())
                        qk_ld[gjob] = dQ[gjob % 2].rec(sync.dma_start(
                            out=qkst[gjob % 2][:].rearrange("p (a c) -> p a c", c=128),
                            in_=qkv_g.rearrange("(a p) c -> p a c", p=128)[:, hf * 32:(hf + 1) * 32][:, :, cbase:cbase + 896 + 128][:, :, ds(COFF[0], 128)]), 16)
                    block.sync(qk_sync)

                    for rd in range(4):
                        def qk_t(t, gjob=gjob, rd=rd):
                            W(t, qk_ld[gjob])
                            prev = (gjob, rd - 2) if rd >= 2 else ((gjob - 1, rd + 2) if gjob else None)
                            if prev is not None:
                                t.wait_ge(dv.h, qk_ev.get(prev, 0))
                            for c in range(8):
                                ins = t.transpose(ptk[rd % 2][:, c * 128:(c + 1) * 128],
                                                  in_=qkst[gjob % 2][:, rd * 1024 + c * 128:rd * 1024 + (c + 1) * 128],
                                                  identity=idb[:])
                            qk_tr[(gjob, rd)] = mm.inc(ins)
                        block.tensor(qk_t)

                        def qk_e(v, u=u, ten=ten, hf=hf, gjob=gjob, rd=rd):
                            v.wait_ge(mm.h, qk_tr[(gjob, rd)])
                            dst = (Qfm if ten == 0 else Kfm)[u]
                            a0 = hf * 4 + rd
                            v.tensor_copy(out=dst[0][:, a0 * 1024:(a0 + 1) * 1024],
                                          in_=ptk[rd % 2][0:64, :])
                            qk_ev[(gjob, rd)] = dv.inc(v.tensor_copy(
                                out=dst[1][:, a0 * 1024:(a0 + 1) * 1024], in_=ptk[rd % 2][64:128, :]))
                        block.vector(qk_e)
                    gjob += 1
        QK_DV = dv.n

        def p2_vld(g):
            g.wait_ge(dv.h, P15_DV_END)
            g.wait_ge(mm.h, P15_MM_END)
            for u in range(2):
                vbase = VCOLS + u * 130
                for hf in range(2):
                    dV[u].inc(g.dma_start(
                        out=VT[u][:].rearrange("p (kt c) -> p kt c", c=130)[:, hf * 32:(hf + 1) * 32, :],
                        in_=qkv_g.rearrange("(kt p) c -> p kt c", p=128)[:, hf * 32:(hf + 1) * 32][:, :, vbase:vbase + 780 + 130][:, :, ds(COFF[1], 130)]), 16)
                unit_ld[u] = dV[u].fin()
        block.sync(p2_vld)
        p2p.close()

        pS = [ps2(f"pS{i}", [128, 1024], F32) for i in range(2)]
        pc0 = ps2("pc0", [65, 512], F32)
        pc1 = ps2("pc1", [65, 512], F32)
        pbc = ps2("pbc", [64, 1024], F32)

        exp_rec = {}
        scm_rec = {}
        avm_rec = {}
        msk_rec = {}
        ctx_free = [dv.n]
        cta_st = [None]
        gi = 0
        for u in range(2):
            for qc in range(16):
                nkt = 4 * qc + 4
                gqc0 = gi
                for kt in range(nkt):
                    def a_mm(t, u=u, qc=qc, kt=kt, nkt=nkt, gi=gi):
                        if gi == 0:
                            W(t, unit_ld[0])
                            t.wait_ge(dv.h, QK_DV)
                        if u == 1 and qc == 0 and kt == 0:
                            W(t, unit_ld[1])
                        if gi >= 2:
                            t.wait_ge(ac.h, exp_rec[gi - 2])
                        t.matmul(pS[gi % 2][:, 0:512],
                                 lhsT=Kfm[u][0][:, kt * 128:(kt + 1) * 128],
                                 rhs=Qfm[u][0][:, qc * 512:(qc + 1) * 512], start=True, stop=True)
                        ins = t.matmul(pS[gi % 2][:, 512:1024],
                                       lhsT=Kfm[u][1][:, kt * 128:(kt + 1) * 128],
                                       rhs=Qfm[u][1][:, qc * 512:(qc + 1) * 512], start=True, stop=True)
                        scm_rec[gi] = mm.inc(ins)
                        if kt >= 1:
                            gip = gi - 1
                            t.wait_ge(ac.h, exp_rec[gip])
                            if kt - 1 == 0:
                                t.wait_ge(dv.h, ctx_free[0])
                            t.matmul(pc0[:], lhsT=VT[u][:, (kt - 1) * 130:(kt - 1) * 130 + 65],
                                     rhs=Et[gip % 2][:, 0:512], start=(kt - 1 == 0), stop=(kt - 1 == nkt - 1))
                            ins = t.matmul(pc1[:], lhsT=VT[u][:, (kt - 1) * 130 + 65:(kt - 1) * 130 + 130],
                                           rhs=Et[gip % 2][:, 512:1024], start=(kt - 1 == 0), stop=(kt - 1 == nkt - 1))
                            avm_rec[gip] = mm.inc(ins)
                    block.tensor(a_mm)

                    if kt >= 4 * qc:
                        def a_msk(v, qc=qc, kt=kt, gi=gi):
                            v.wait_ge(mm.h, scm_rec[gi])
                            mloc = kt - 4 * qc
                            mview = cm[:, mloc * 512:(mloc + 1) * 512].unsqueeze(1).broadcast_to([128, 2, 512])
                            msk_rec[gi] = dv.inc(v.tensor_tensor(
                                out=pS[gi % 2][:].rearrange("p (b q) -> p b q", q=512),
                                in0=pS[gi % 2][:].rearrange("p (b q) -> p b q", q=512),
                                in1=mview, op=ALU.add))
                        block.vector(a_msk)

                    def a_exp(sc, gi=gi, diag=(kt >= 4 * qc)):
                        sc.wait_ge(mm.h, scm_rec[gi])
                        if diag:
                            sc.wait_ge(dv.h, msk_rec[gi])
                        if gi >= 2:
                            sc.wait_ge(mm.h, avm_rec[gi - 2])
                        exp_rec[gi] = ac.inc(sc.activation(out=Et[gi % 2][:], in_=pS[gi % 2][:],
                                                           func=AF.Exp, scale=0.125))
                    block.scalar(a_exp)
                    gi += 1

                def a_avl(t, u=u, qc=qc, nkt=nkt, gi=gi):
                    gip = gi - 1
                    t.wait_ge(ac.h, exp_rec[gip])
                    if nkt == 1:
                        t.wait_ge(dv.h, ctx_free[0])
                    kt = nkt - 1
                    t.matmul(pc0[:], lhsT=VT[u][:, kt * 130:kt * 130 + 65],
                             rhs=Et[gip % 2][:, 0:512], start=(kt == 0), stop=True)
                    ins = t.matmul(pc1[:], lhsT=VT[u][:, kt * 130 + 65:kt * 130 + 130],
                                   rhs=Et[gip % 2][:, 512:1024], start=(kt == 0), stop=True)
                    avm_rec[gip] = mm.inc(ins)
                block.tensor(a_avl)

                rr_dv = [0]

                def a_rr(v, gi=gi):
                    v.wait_ge(mm.h, avm_rec[gi - 1])
                    dv.inc(v.reciprocal(out=rrow[0:1, 0:512], in_=pc0[64:65, :]))
                    dv.inc(v.reciprocal(out=rrow[0:1, 512:1024], in_=pc1[64:65, :]))
                    v.wait_ge(dv.h, dv.n)
                    rr_dv[0] = dv.n
                block.vector(a_rr)

                bc_mm = [0]

                def a_bc(t):
                    t.wait_ge(dv.h, rr_dv[0])
                    t.matmul(pbc[:, 0:512], lhsT=of32[0:1, 0:64], rhs=rrow[0:1, 0:512],
                             start=True, stop=True)
                    ins = t.matmul(pbc[:, 512:1024], lhsT=of32[0:1, 0:64], rhs=rrow[0:1, 512:1024],
                                   start=True, stop=True)
                    bc_mm[0] = mm.inc(ins)
                block.tensor(a_bc)

                def a_nrm(v, u=u, qc=qc):
                    v.wait_ge(mm.h, bc_mm[0])
                    if cta_st[0] is not None:
                        W(v, cta_st[0])
                    dv.inc(v.tensor_copy(out=rb[:], in_=pbc[:]))
                    v.wait_ge(dv.h, dv.n)
                    v.tensor_tensor(out=cta[0:64, :], in0=pc0[0:64, :], in1=rb[:, 0:512], op=ALU.mult)
                    ctx_free[0] = dv.inc(v.tensor_tensor(out=cta[64:128, :], in0=pc1[0:64, :],
                                                         in1=rb[:, 512:1024], op=ALU.mult))
                block.vector(a_nrm)

                def a_st(sync, u=u, qc=qc):
                    sync.wait_ge(dv.h, ctx_free[0])
                    cta_st[0] = sC.rec(sync.dma_start(out=ctx_loc[u * 128:(u + 1) * 128, qc * 512:(qc + 1) * 512],
                                          in_=cta[:]), 16)
                block.sync(a_st)

        def coll2(g):
            W(g, sC.fin())
            cc.inc(g.collective_compute("AllGather", ALU.bypass, replica_groups=rg,
                                        ins=[ctx_loc.opt()], outs=[ctx_g.opt()]))
        block.gpsimd(coll2)
        CC2 = cc.n

        p2.close()

        p3 = ExitStack()
        sb3 = lambda n_, s_, d_: p3.enter_context(nc.sbuf_tensor(n_, s_, d_))
        hbf = sb3("hbf", [128, 8 * TOK], BF16)
        wld3 = [sb3(f"wld3_{i}", [128, 1024], F32) for i in range(2)]
        a3r = sb3("a3r", [1, 2048], F32)
        c3r = sb3("c3r", [1, 2048], F32)
        pA = [p3.enter_context(nc.psum_tensor(f"pA{i}", [128, 512], F32)) for i in range(2)]
        pB = [p3.enter_context(nc.psum_tensor(f"pB{i}", [128, 512], F32)) for i in range(2)]

        p3a = ExitStack()
        sa = lambda n_, s_, d_: p3a.enter_context(nc.sbuf_tensor(n_, s_, d_))
        xnr = sa("xnr", [128, 8 * TOK], BF16)
        ctx_t = [sa(f"ctx_t{i}", [128, 8 * 512], BF16) for i in range(2)]
        oU = sa("oU", [128, 8 * 768], BF16)
        oV = sa("oV", [128, 6 * 1024], BF16)
        t1s = sa("t1s", [128, 6 * 512], BF16)
        ao = sa("ao", [128, 512], F32)

        XNR_LD = [0]

        def p3a_xn(sync):
            sync.wait_ge(cc.h, CC2)
            W(sync, sX.fin())
            XNR_LD[0] = dXN.rec(sync.dma_start(out=xnr[:], in_=xn_sp[:, :]), 16)
        block.sync(p3a_xn)

        ocast = {}

        def emit_wcast(i, src_ap, w, dst_ap):
            def ow_sync(sync, i=i):
                if i >= 2:
                    sync.wait_ge(dv.h, ocast[i - 2])
                ocast[(i, "ld")] = dW3.rec(sync.dma_start(out=wld3[i % 2][:, 0:w], in_=src_ap), 16)
            block.sync(ow_sync)

            def ow_vec(v, i=i):
                W(v, ocast[(i, "ld")])
                ocast[i] = dv.inc(v.tensor_copy(out=dst_ap, in_=wld3[i % 2][:, 0:w]))
            block.vector(ow_vec)

        def p3a_gate(sync):
            sync.wait_ge(cc.h, CC2)
        block.sync(p3a_gate)

        wi = 0
        for i in range(8):
            emit_wcast(wi, outU[i * 128:(i + 1) * 128, :], 768, oU[:, i * 768:(i + 1) * 768])
            wi += 1
        for i in range(6):
            emit_wcast(wi, outV[i * 128:(i + 1) * 128, :], 1024, oV[:, i * 1024:(i + 1) * 1024])
            wi += 1
        OW_DV = dv.n

        ctx_ld = [0] * 4
        t1_mm = {}
        t1_ev = {}
        t2_mm = {}
        h_dv = {}
        HSC = {}

        for tc in range(4):
            def c_sync(g, tc=tc):
                if tc == 0:
                    g.wait_ge(cc.h, CC2)
                if tc >= 2:
                    g.wait_ge(mm.h, t1_mm.get((tc - 2, 5), 0))
                ctx_ld[tc] = dT[tc % 2].rec(g.dma_start(
                    out=ctx_t[tc % 2][:].rearrange("p (dch t) -> p dch t", t=512),
                    in_=ctx_g.rearrange("(dch p) t -> p dch t", p=128)[:, :, tc * 512:tc * 512 + 6144 + 512][:, :, ds(COFF[2], 512)]), 16)
            block.sync(c_sync)

            def t1_pre(t, tc=tc):
                W(t, ctx_ld[tc])
                if tc == 0:
                    t.wait_ge(dv.h, OW_DV)
            block.tensor(t1_pre)

            for cc6 in range(6):
                def t1_f(t, tc=tc, cc6=cc6):
                    gk = tc * 6 + cc6
                    if gk >= 2:
                        t.wait_ge(dv.h, t1_ev.get(gk - 2, 0))
                    for dch in range(8):
                        ins = t.matmul(pA[gk % 2][:],
                                       lhsT=oU[:, dch * 768 + cc6 * 128:dch * 768 + (cc6 + 1) * 128],
                                       rhs=ctx_t[tc % 2][:, dch * 512:(dch + 1) * 512],
                                       start=(dch == 0), stop=(dch == 7))
                    t1_mm[(tc, cc6)] = mm.inc(ins)
                block.tensor(t1_f)

                def t1_e(v, tc=tc, cc6=cc6):
                    gk = tc * 6 + cc6
                    v.wait_ge(mm.h, t1_mm[(tc, cc6)])
                    t1_ev[gk] = dv.inc(v.tensor_copy(out=t1s[:, cc6 * 512:(cc6 + 1) * 512],
                                                     in_=pA[gk % 2][:]))
                block.vector(t1_e)
            T1E_DV = dv.n

            for dch in range(8):
                def t2_f2(t, tc=tc, dch=dch):
                    if dch == 0:
                        t.wait_ge(dv.h, T1E_DV)
                    gk = tc * 8 + dch
                    if gk >= 2:
                        t.wait_ge(dv.h, h_dv.get(gk - 2, 0))
                    for rc in range(6):
                        ins = t.matmul(pB[gk % 2][:],
                                       lhsT=oV[:, rc * 1024 + dch * 128:rc * 1024 + (dch + 1) * 128],
                                       rhs=t1s[:, rc * 512:(rc + 1) * 512],
                                       start=(rc == 0), stop=(rc == 5))
                    t2_mm[(tc, dch)] = mm.inc(ins)
                block.tensor(t2_f2)
            for dch in range(8):
                def h_dve(v, tc=tc, dch=dch):
                    v.wait_ge(mm.h, t2_mm[(tc, dch)])
                    if tc == 0 and dch == 0:
                        W(v, XNR_LD[0])
                    gk = tc * 8 + dch
                    v.tensor_scalar(out=ao[:], in0=pB[gk % 2][:],
                                    scalar1=ob_pp[:, dch:dch + 1], scalar2=None, op0=ALU.add)
                    h_dv[gk] = dv.inc(v.tensor_tensor(
                        out=hbf[:].rearrange("p (c t) -> p c t", t=TOK)[:, dch, tc * 512:(tc + 1) * 512],
                        in0=ao[:],
                        in1=xnr[:].rearrange("p (c t) -> p c t", t=TOK)[:, dch, tc * 512:(tc + 1) * 512],
                        op=ALU.add))
                block.vector(h_dve)
        H_DONE_DV = dv.n

        p3a.close()

        p3b = ExitStack()
        sbb = lambda n_, s_, d_: p3b.enter_context(nc.sbuf_tensor(n_, s_, d_))
        h2t = sbb("h2t", [128, 512], BF16)
        srow = sbb("srow", [1, 512], F32)
        trow = sbb("trow", [1, 512], F32)
        pR = [p3b.enter_context(nc.psum_tensor(f"pR{i}", [1, 512], F32)) for i in range(2)]

        sq_ac = {}
        st_mm = {}
        row_dv = {}
        row_ac = {}

        for tc in range(4):
            for dch in range(8):
                def b_sc(sc, tc=tc, dch=dch):
                    if tc == 0 and dch == 0:
                        sc.wait_ge(dv.h, H_DONE_DV)
                    if dch == 0 and tc >= 1:
                        sc.wait_ge(dv.h, row_dv.get(tc - 1, 0))
                    if dch >= 1:
                        sc.wait_ge(mm.h, st_mm.get((tc, dch - 1), 0))
                    sq_ac[(tc, dch)] = ac.inc(sc.activation(
                        out=h2t[:],
                        in_=hbf[:].rearrange("p (c t) -> p c t", t=TOK)[:, dch, tc * 512:(tc + 1) * 512],
                        func=AF.Square))
                block.scalar(b_sc)

                def b_mm(t, tc=tc, dch=dch):
                    t.wait_ge(ac.h, sq_ac[(tc, dch)])
                    t.matmul(pR[0][:],
                             lhsT=ones1[:],
                             rhs=hbf[:].rearrange("p (c t) -> p c t", t=TOK)[:, dch, tc * 512:(tc + 1) * 512],
                             start=(dch == 0), stop=(dch == 7))
                    ins = t.matmul(pR[1][:], lhsT=ones1[:], rhs=h2t[:],
                                   start=(dch == 0), stop=(dch == 7))
                    st_mm[(tc, dch)] = mm.inc(ins)
                block.tensor(b_mm)

            def b_row(v, tc=tc):
                v.wait_ge(mm.h, st_mm[(tc, 7)])
                v.tensor_scalar(out=srow[:], in0=pR[0][:], scalar1=1.0 / D, scalar2=None, op0=ALU.mult)
                dv.inc(v.tensor_scalar(out=trow[:], in0=pR[1][:], scalar1=1.0 / D, scalar2=None, op0=ALU.mult))
                v.wait_ge(dv.h, dv.n)
                dv.inc(v.scalar_tensor_tensor(out=c3r[0:1, tc * 512:(tc + 1) * 512], in0=srow[:], scalar=1.0,
                                              in1=srow[:], op0=ALU.mult, op1=ALU.mult))
                v.wait_ge(dv.h, dv.n)
                dv.inc(v.tensor_sub(out=trow[:], in0=trow[:], in1=c3r[0:1, tc * 512:(tc + 1) * 512]))
                v.wait_ge(dv.h, dv.n)
                row_dv[tc] = dv.inc(v.tensor_scalar(out=trow[:], in0=trow[:], scalar1=EPS,
                                                    scalar2=None, op0=ALU.add))
            block.vector(b_row)

            def b_rs(sc, tc=tc):
                sc.wait_ge(dv.h, row_dv[tc])
                sc.activation(out=a3r[0:1, tc * 512:(tc + 1) * 512], in_=trow[:], func=AF.Ln)
                row_ac[tc] = ac.inc(sc.activation(out=a3r[0:1, tc * 512:(tc + 1) * 512],
                                                  in_=a3r[0:1, tc * 512:(tc + 1) * 512],
                                                  func=AF.Exp, scale=-0.5))
            block.scalar(b_rs)

            def b_c3(v, tc=tc):
                v.wait_ge(ac.h, row_ac[tc])
                row_dv[tc] = dv.inc(v.scalar_tensor_tensor(
                    out=c3r[0:1, tc * 512:(tc + 1) * 512], in0=srow[:], scalar=-1.0,
                    in1=a3r[0:1, tc * 512:(tc + 1) * 512], op0=ALU.mult, op1=ALU.mult))
            block.vector(b_c3)
        STATS_DV = dv.n
        STATS_MM = mm.n
        p3b.close()

        p3c = ExitStack()
        scx = lambda n_, s_, d_: p3c.enter_context(nc.sbuf_tensor(n_, s_, d_))
        f1U = scx("f1U", [128, 8 * 512], BF16)
        f1V = scx("f1V", [128, 4 * 4096], BF16)
        f2U = scx("f2U", [128, 32 * 512], BF16)
        f2V = scx("f2V", [128, 4 * 1024], BF16)
        lnt = scx("lnt", [128, 8 * 512], BF16)
        t34 = scx("t34", [128, 4 * 512], BF16)
        hmid = scx("hmid", [128, 32 * 512], BF16)
        u1s = scx("u1s", [128, 512], F32)
        bb3 = scx("bb3", [128, 1024], F32)
        ao3 = scx("ao3", [128, 512], F32)
        pBC = p3c.enter_context(nc.psum_tensor("pBC", [128, 1024], F32))

        fcast = {}

        def emit_fw(i, src_ap, w, dst_ap):
            def fw_sync(sync, i=i):
                if i >= 2:
                    sync.wait_ge(dv.h, fcast[i - 2])
                else:
                    sync.wait_ge(dv.h, OW_DV)
                fcast[(i, "ld")] = dW3.rec(sync.dma_start(out=wld3[i % 2][:, 0:w], in_=src_ap), 16)
            block.sync(fw_sync)

            def fw_vec(v, i=i):
                W(v, fcast[(i, "ld")])
                fcast[i] = dv.inc(v.tensor_copy(out=dst_ap, in_=wld3[i % 2][:, 0:w]))
            block.vector(fw_vec)

        def p3c_gate(sync):
            sync.wait_ge(mm.h, STATS_MM)
            sync.wait_ge(dv.h, STATS_DV)
        block.sync(p3c_gate)

        fwi = 0
        for i in range(8):
            emit_fw(fwi, fc1U[i * 128:(i + 1) * 128, :], 512, f1U[:, i * 512:(i + 1) * 512])
            fwi += 1
        for i in range(4):
            for qr in range(4):
                emit_fw(fwi, fc1V[i * 128:(i + 1) * 128, qr * 1024:(qr + 1) * 1024], 1024,
                        f1V[:, i * 4096 + qr * 1024:i * 4096 + (qr + 1) * 1024])
                fwi += 1
        for i in range(32):
            emit_fw(fwi, fc2U[i * 128:(i + 1) * 128, :], 512, f2U[:, i * 512:(i + 1) * 512])
            fwi += 1
        for i in range(4):
            emit_fw(fwi, fc2V[i * 128:(i + 1) * 128, :], 1024, f2V[:, i * 1024:(i + 1) * 1024])
            fwi += 1
        FW_DV = dv.n

        bc3_mm = {}
        ln_dv = {}
        ln_ac = {}
        t3_mm = {}
        t3_ev = {}
        hm_mm = {}
        hm_ac = {}
        t4_mm = {}
        t4_ev = {}
        o5_mm = {}
        o5_ac = {}
        blk_dv = {}

        for tc in range(4):
            def m_bc(t, tc=tc):
                if tc == 0:
                    t.wait_ge(dv.h, STATS_DV)
                if tc >= 1:
                    t.wait_ge(dv.h, ln_dv.get((tc - 1, 7), 0))
                t.matmul(pBC[:, 0:512], lhsT=of32[0:1, 0:128], rhs=a3r[0:1, tc * 512:(tc + 1) * 512],
                         start=True, stop=True)
                ins = t.matmul(pBC[:, 512:1024], lhsT=of32[0:1, 0:128], rhs=c3r[0:1, tc * 512:(tc + 1) * 512],
                               start=True, stop=True)
                bc3_mm[tc] = mm.inc(ins)
            block.tensor(m_bc)

            def m_cp(v, tc=tc):
                v.wait_ge(mm.h, bc3_mm[tc])
                dv.inc(v.tensor_copy(out=bb3[:], in_=pBC[:]))
                v.wait_ge(dv.h, dv.n)
                bc3_mm[(tc, "cp")] = dv.n
            block.vector(m_cp)

            for dch in range(8):
                def m_ln(v, tc=tc, dch=dch):
                    v.wait_ge(dv.h, bc3_mm[(tc, "cp")])
                    prev = (tc, dch - 1) if dch else ((tc - 1, 7) if tc else None)
                    if prev is not None:
                        v.wait_ge(ac.h, ln_ac[prev])
                    v.tensor_mul(out=u1s[:],
                                 in0=hbf[:].rearrange("p (c t) -> p c t", t=TOK)[:, dch, tc * 512:(tc + 1) * 512],
                                 in1=bb3[:, 0:512])
                    ln_dv[(tc, dch)] = dv.inc(v.tensor_add(out=u1s[:], in0=u1s[:], in1=bb3[:, 512:1024]))
                block.vector(m_ln)

                def m_lna(sc, tc=tc, dch=dch):
                    sc.wait_ge(dv.h, ln_dv[(tc, dch)])
                    ln_ac[(tc, dch)] = ac.inc(sc.activation(
                        out=lnt[:, dch * 512:(dch + 1) * 512], in_=u1s[:], func=AF.Identity,
                        scale=l2g[:, dch:dch + 1], bias=l2b[:, dch:dch + 1]))
                block.scalar(m_lna)
            LN_AC = ac.n

            for rc in range(4):
                def m_t3(t, tc=tc, rc=rc):
                    if rc == 0:
                        t.wait_ge(ac.h, LN_AC)
                        if tc == 0:
                            t.wait_ge(dv.h, FW_DV)
                    gk = tc * 4 + rc
                    if gk >= 2:
                        t.wait_ge(dv.h, t3_ev.get(gk - 2, 0))
                    for dch in range(8):
                        ins = t.matmul(pA[gk % 2][:],
                                       lhsT=f1U[:, dch * 512 + rc * 128:dch * 512 + (rc + 1) * 128],
                                       rhs=lnt[:, dch * 512:(dch + 1) * 512],
                                       start=(dch == 0), stop=(dch == 7))
                    t3_mm[gk] = mm.inc(ins)
                block.tensor(m_t3)

                def m_t3e(v, tc=tc, rc=rc):
                    gk = tc * 4 + rc
                    v.wait_ge(mm.h, t3_mm[gk])
                    t3_ev[gk] = dv.inc(v.tensor_copy(out=t34[:, rc * 512:(rc + 1) * 512],
                                                     in_=pA[gk % 2][:]))
                block.vector(m_t3e)
            T3E_DV = dv.n

            for ic in range(32):
                def m_hm(t, tc=tc, ic=ic, tdv=T3E_DV):
                    if ic == 0:
                        t.wait_ge(dv.h, tdv)
                    gk = tc * 32 + ic
                    if gk >= 2:
                        t.wait_ge(ac.h, hm_ac.get(gk - 2, 0))
                    for rc in range(4):
                        ins = t.matmul(pB[gk % 2][:],
                                       lhsT=f1V[:, rc * 4096 + ic * 128:rc * 4096 + (ic + 1) * 128],
                                       rhs=t34[:, rc * 512:(rc + 1) * 512],
                                       start=(rc == 0), stop=(rc == 3))
                    hm_mm[gk] = mm.inc(ins)
                block.tensor(m_hm)

                def m_ge(sc, tc=tc, ic=ic):
                    sc.wait_ge(mm.h, hm_mm[tc * 32 + ic])
                    hm_ac[tc * 32 + ic] = ac.inc(sc.activation(
                        out=hmid[:, ic * 512:(ic + 1) * 512], in_=pB[(tc * 32 + ic) % 2][:],
                        func=AF.Gelu_apprx_tanh, bias=f1b_pp[:, ic:ic + 1]))
                block.scalar(m_ge)
            HM_AC = ac.n

            for cc4 in range(4):
                def m_t4(t, tc=tc, cc4=cc4):
                    if cc4 == 0:
                        t.wait_ge(ac.h, HM_AC)
                    gk = tc * 4 + cc4
                    if gk >= 2:
                        t.wait_ge(dv.h, t4_ev.get(gk - 2, 0))
                    for ic in range(32):
                        ins = t.matmul(pA[gk % 2][:],
                                       lhsT=f2U[:, ic * 512 + cc4 * 128:ic * 512 + (cc4 + 1) * 128],
                                       rhs=hmid[:, ic * 512:(ic + 1) * 512],
                                       start=(ic == 0), stop=(ic == 31))
                    t4_mm[gk] = mm.inc(ins)
                block.tensor(m_t4)

                def m_t4e(v, tc=tc, cc4=cc4):
                    gk = tc * 4 + cc4
                    v.wait_ge(mm.h, t4_mm[gk])
                    t4_ev[gk] = dv.inc(v.tensor_copy(out=t34[:, cc4 * 512:(cc4 + 1) * 512],
                                                     in_=pA[gk % 2][:]))
                block.vector(m_t4e)
            T4E_DV = dv.n

            for dch in range(8):
                def m_o5(t, tc=tc, dch=dch, tdv=T4E_DV):
                    if dch == 0:
                        t.wait_ge(dv.h, tdv)
                    gk = tc * 8 + dch
                    if gk >= 2:
                        t.wait_ge(dv.h, blk_dv.get(gk - 2, 0))
                    for cc4 in range(4):
                        ins = t.matmul(pB[gk % 2][:],
                                       lhsT=f2V[:, cc4 * 1024 + dch * 128:cc4 * 1024 + (dch + 1) * 128],
                                       rhs=t34[:, cc4 * 512:(cc4 + 1) * 512],
                                       start=(cc4 == 0), stop=(cc4 == 3))
                    o5_mm[gk] = mm.inc(ins)
                block.tensor(m_o5)

                def m_oa(sc, tc=tc, dch=dch):
                    sc.wait_ge(mm.h, o5_mm[tc * 8 + dch])
                    if tc * 8 + dch >= 1:
                        sc.wait_ge(dv.h, blk_dv.get(tc * 8 + dch - 1, 0))
                    o5_ac[tc * 8 + dch] = ac.inc(sc.activation(
                        out=ao3[:], in_=pB[(tc * 8 + dch) % 2][:], func=AF.Identity,
                        bias=f2b_pp[:, dch:dch + 1]))
                block.scalar(m_oa)

                def m_ov(v, tc=tc, dch=dch):
                    v.wait_ge(ac.h, o5_ac[tc * 8 + dch])
                    hsl = hbf[:].rearrange("p (c t) -> p c t", t=TOK)[:, dch, tc * 512:(tc + 1) * 512]
                    blk_dv[tc * 8 + dch] = dv.inc(v.tensor_tensor(out=hsl, in0=hsl, in1=ao3[:], op=ALU.add))
                block.vector(m_ov)
        BLK_DV = dv.n
        p3c.close()

        p3d = ExitStack()
        sdd = lambda n_, s_, d_: p3d.enter_context(nc.sbuf_tensor(n_, s_, d_))
        fft = sdd("fft", [128, 8 * 512], BF16)
        sfin = sdd("sfin", [128, 8 * 512], F32)
        xrt = [sdd(f"xrt{i}", [128, D], F32) for i in range(2)]
        ot = [sdd(f"ot{i}", [128, D], F32) for i in range(2)]
        ptf = [p3d.enter_context(nc.psum_tensor(f"ptf{i}", [128, 1024], F32)) for i in range(2)]

        fft_ld = [0] * 4
        sf_dv = [0] * 4
        ftr_mm = {}
        ot_dv = {}
        out_st = {}
        xr_ld = {}

        for tc in range(4):
            def d_ld(sync, tc=tc):
                if tc == 0:
                    W(sync, FFN_ST)
                    sync.wait_ge(dv.h, BLK_DV)
                if tc >= 1:
                    sync.wait_ge(dv.h, sf_dv[tc - 1])
                fft_ld[tc] = dF.rec(sync.dma_start(out=fft[:],
                                      in_=ffn_sp.rearrange("p (c t) -> p c t", t=TOK)[:, :, tc * 512:(tc + 1) * 512]), 16)
            block.sync(d_ld)

            def d_sum(v, tc=tc):
                W(v, fft_ld[tc])
                if tc == 0:
                    v.wait_ge(dv.h, BLK_DV)
                for dch in range(8):
                    ins = v.tensor_tensor(
                        out=sfin[:, dch * 512:(dch + 1) * 512],
                        in0=hbf[:].rearrange("p (c t) -> p c t", t=TOK)[:, dch, tc * 512:(tc + 1) * 512],
                        in1=fft[:, dch * 512:(dch + 1) * 512], op=ALU.add)
                sf_dv[tc] = dv.inc(ins)
            block.vector(d_sum)

            for tb in range(4):
                ti = tc * 4 + tb

                def d_x(sync, tc=tc, tb=tb, ti=ti):
                    if ti >= 2:
                        sync.wait_ge(dv.h, ot_dv.get(ti - 2, 0))
                    xr_ld[ti] = dX[ti % 2].rec(sync.dma_start(out=xrt[ti % 2][:],
                                          in_=x_own[ti * 128:(ti + 1) * 128, :]), 16)
                block.sync(d_x)

                def d_tr(t, tc=tc, tb=tb, ti=ti):
                    t.wait_ge(dv.h, sf_dv[tc])
                    if ti >= 2:
                        t.wait_ge(dv.h, ot_dv.get(ti - 2, 0))
                    for dch in range(8):
                        ins = t.transpose(ptf[ti % 2][:, dch * 128:(dch + 1) * 128],
                                          in_=sfin[:, dch * 512 + tb * 128:dch * 512 + (tb + 1) * 128],
                                          identity=idf[:])
                    ftr_mm[ti] = mm.inc(ins)
                block.tensor(d_tr)

                def d_add(v, ti=ti):
                    v.wait_ge(mm.h, ftr_mm[ti])
                    W(v, xr_ld[ti])
                    if ti >= 2 and out_st.get(ti - 2):
                        W(v, out_st.get(ti - 2))
                    v.tensor_tensor(out=ot[ti % 2][:, 0:512], in0=ptf[ti % 2][:, 0:512],
                                    in1=xrt[ti % 2][:, 0:512], op=ALU.add)
                    ot_dv[ti] = dv.inc(v.tensor_tensor(out=ot[ti % 2][:, 512:1024], in0=ptf[ti % 2][:, 512:1024],
                                                       in1=xrt[ti % 2][:, 512:1024], op=ALU.add))
                block.vector(d_add)

                def d_st(sync, ti=ti):
                    sync.wait_ge(dv.h, ot_dv[ti])
                    out_st[ti] = sO[ti % 2].rec(sync.dma_start(out=out_t[ti * 128:(ti + 1) * 128, :], in_=ot[ti % 2][:]), 16)
                block.sync(d_st)
        def fin_sync(sync):
            W(sync, sO[0].fin())
            W(sync, sO[1].fin())
        block.sync(fin_sync)

    p3d.close()
    p3.close()
    top.close()
    return nc


_CACHE = {}


def kernel(**inputs):
    g = lambda k: np.asarray(inputs[k])
    x = g("L0_attn_hidden_states").astype(np.float32)

    qU, kU, vU = g("L0_attn_q_U"), g("L0_attn_k_U"), g("L0_attn_v_U")
    qV, kV, vV = g("L0_attn_q_V"), g("L0_attn_k_V"), g("L0_attn_v_V")
    qb, kb, vb = g("L0_attn_q_b"), g("L0_attn_k_b"), g("L0_attn_v_b")

    qkvU = np.zeros((D, 3 * 1024), np.float32)
    qkvV = np.zeros((128, 3 * 1024), np.float32)
    qkvb = np.zeros((128, 3088), np.float32)
    for pi, (U, V, bvec) in enumerate([(qU, qV, qb), (kU, kV, kb), (vU, vV, vb)]):
        for h in range(H):
            qkvU[:, pi * 1024 + h * 64:pi * 1024 + h * 64 + 48] = U[:, h, :]
            qkvV[0:48, pi * 1024 + h * 64:pi * 1024 + (h + 1) * 64] = V[h]
            if pi < 2:
                qkvb[:, pi * 1024 + h * 64:pi * 1024 + (h + 1) * 64] = bvec[h][None, :]
            else:
                qkvb[:, 2048 + h * 65:2048 + h * 65 + 64] = bvec[h][None, :]
                qkvb[:, 2048 + h * 65 + 64] = 1.0

    pp = lambda vec, nch: np.ascontiguousarray(vec.astype(np.float32).reshape(nch, 128).T)

    kp = np.arange(128)[:, None]
    qq = np.arange(512)[None, :]
    cmask = np.zeros((128, 4 * 512), np.float32)
    for m in range(4):
        cmask[:, m * 512:(m + 1) * 512] = np.where(qq >= m * 128 + kp, 0.0, -1e9)

    feeds_common = {
        "qkvU": qkvU,
        "qkvV": qkvV,
        "ln1g": pp(g("L0_attn_ln1_g"), 8),
        "ln1b": pp(g("L0_attn_ln1_b"), 8),
        "ln2g": pp(g("L0_attn_ln2_g"), 8),
        "ln2b": pp(g("L0_attn_ln2_b"), 8),
        "outU": g("L0_attn_out_U").astype(np.float32),
        "outV": g("L0_attn_out_V").astype(np.float32),
        "outb": pp(g("L0_attn_out_b"), 8),
        "fc1U": g("L0_attn_fc1_U").astype(np.float32),
        "fc1V": g("L0_attn_fc1_V").astype(np.float32),
        "fc1b": pp(g("L0_attn_fc1_b"), 32),
        "fc2U": g("L0_attn_fc2_U").astype(np.float32),
        "fc2V": g("L0_attn_fc2_V").astype(np.float32),
        "fc2b": pp(g("L0_attn_fc2_b"), 8),
        "wgwu": np.ascontiguousarray(
            np.concatenate([g("L0_ffn_w_gate"), g("L0_ffn_w_up")], axis=0).astype(np.float32)),
        "wdn": g("L0_ffn_w_down").astype(np.float32),
        "cmask": cmask.astype(ml_dtypes.bfloat16),
        "qkvb": qkvb,
        "idenb": np.eye(128).astype(ml_dtypes.bfloat16),
        "idenf": np.eye(128, dtype=np.float32),
    }

    in_maps = []
    for c in range(8):
        b, j = c // 4, c % 4
        m = dict(feeds_common)
        m["x_own"] = np.ascontiguousarray(x[b, j * TOK:(j + 1) * TOK, :])
        m["coffs"] = np.tile(np.array([[j * 256, j * 260, j * TOK, 0]], np.int32), (128, 1))
        in_maps.append(m)

    if "nc" not in _CACHE:
        _CACHE["nc"] = build_nc()
    res = run_bass_kernel_spmd(_CACHE["nc"], in_maps, core_ids=list(range(8)))

    out = np.empty((B, S, D), np.float32)
    for c in range(8):
        b, j = c // 4, c % 4
        out[b, j * TOK:(j + 1) * TOK, :] = res.results[c]["out"]
    return out
